# version 10
# speedup vs baseline: 1.2977x; 1.1004x over previous
import numpy as np
import concourse.bass as bass
import concourse.mybir as mybir
from concourse.bass_utils import run_bass_kernel_spmd

F32 = mybir.dt.float32
BF16 = mybir.dt.bfloat16
U8 = mybir.dt.uint8
AF = mybir.ActivationFunctionType
ALU = mybir.AluOpType
AX = mybir.AxisListType

T = 2048
D = 2048
L = 1024
NT = T // 128
ND = D // 128
NL = L // 128
HID = [1024, 2048, 3072, 4096, 1024, 2048, 3072, 4096]
NJ = [h // 128 for h in HID]
COST_LAMBDA = 1e-07
EPS = 1e-5
TB = 1024
NTB = T // TB
NTI = TB // 128
JG = 8
NEG_BIG = -1e30
SQD = 1.0 / D

SEM_NAMES = [
    "sCONST", "sX0", "sX1", "sSPQ", "sKL", "sQS0", "sQS1", "sASP",
    "sAL0", "sAL1", "sXD", "sX2S0", "sX2S1", "sBR", "sG2S", "sAGS",
    "sBC", "sABC", "sBUD", "sGL0", "sGL1", "sXG0", "sXG1", "sXB0", "sXB1", "sOUT",
    "sWC", "sWS0", "sWS1", "sWO", "sDW", "sW10", "sW11", "sW20", "sW21", "sGW",
    "sOU0", "sOU1",
    "sPE", "sACT", "sDVE",
]


class MK:
    def __init__(self, phase_limit=99, dbg=None):
        self.phase_limit = phase_limit
        self.dbg = dbg or []
        self.dbg_done = []
        nc = bass.Bass("TRN2", target_bir_lowering=False)
        self.nc = nc
        self._dram()
        self._sbuf()
        self._build()

    def _dram(self):
        nc = self.nc

        def din(name, shape):
            return nc.dram_tensor(name, shape, F32, kind="ExternalInput").ap()

        self.x = din("x", [T, D])
        self.Wq = din("Wq", [D, D]); self.Wk = din("Wk", [D, D])
        self.Wv = din("Wv", [D, D]); self.Wo = din("Wo", [D, D])
        self.down_w = din("down_w", [D, L]); self.down_b = din("down_b", [L])
        self.up_w = din("up_w", [L, D]); self.up_b = din("up_b", [D])
        self.router_w = din("router_w", [D, 8]); self.router_b = din("router_b", [8])
        self.shared_w1 = din("shared_w1", [L // 128, 128, NL, 128])
        self.shared_b1 = din("shared_b1", [L])
        self.shared_w2 = din("shared_w2", [L, L]); self.shared_b2 = din("shared_b2", [L])
        self.core_w = din("core_w", [D, D]); self.core_b = din("core_b", [D])
        self.ew1 = [din(f"e{i}_w1", [NJ[i], 128, NL, 128]) for i in range(8)]
        self.eb1 = [din(f"e{i}_b1", [HID[i]]) for i in range(8)]
        self.ew2 = [din(f"e{i}_w2", [HID[i], L]) for i in range(8)]
        self.eb2 = [din(f"e{i}_b2", [L]) for i in range(8)]
        self.ident = din("ident", [128, 128])
        self.negcost = din("negcost", [1, 8])
        self.out = nc.dram_tensor("out", [T, D], F32, kind="ExternalOutput").ap()

        def scr(name, shape, dt):
            return nc.dram_tensor(name, shape, dt, kind="Internal").ap()

        self.scrQ = scr("scrQ", [4, NT, 128, 512], BF16)
        self.scrK = scr("scrK", [4, NT, 128, 512], BF16)
        self.scrATT = scr("scrATT", [NT, 128, ND, 128], BF16)
        self.scrX2 = scr("scrX2", [NT, 128, D], F32)
        self.scrG2 = scr("scrG2", [NT, 128, ND, 128], BF16)
        self.scrAG = scr("scrAG", [9, T], BF16)

    def _sbuf(self):
        nc = self.nc

        def arena(name, nbytes):
            return nc.alloc_sbuf_tensor(name, [128, nbytes], U8)

        def view(ar, off, shape, dt):
            esz = {F32: 4, BF16: 2}[dt]
            n = 1
            for s in shape[1:]:
                n *= s
            assert off + n * esz <= ar.shape[1], (off, n * esz)
            ap = ar[:, off:off + n * esz].bitcast(dt)
            if len(shape) == 3:
                ap = ap.rearrange("p (a c) -> p a c", a=shape[1])
            return ap

        arA = arena("arA", 65536)
        arB = arena("arB", 65536)
        arC = arena("arC", 53504)
        self.view = view

        self.ln1T = view(arA, 0, [128, ND, T], BF16)
        self.Kar = view(arA, 0, [128, NT, D], BF16)
        self.ln2T = view(arA, 0, [128, ND, T], BF16)
        self.wtil = view(arA, 0, [128, 2 * JG, TB], BF16)
        self.w2grp = view(arA, 32768, [128, 2 * JG, L], BF16)
        self.corew = view(arA, 0, [128, ND, D], BF16)
        self.VT = view(arB, 0, [128, ND, T], BF16)
        self.Wosb = view(arB, 0, [128, ND, D], BF16)
        self.z = view(arB, 0, [128, NT, L], F32)
        self.upw = view(arB, 0, [128, NL, D], BF16)
        self.y2t = view(arB, 32768, [128, D], F32)
        self.x2tG = [view(arB, 40960, [128, D], F32), view(arB, 49152, [128, D], F32)]
        self.outt = view(arB, 57344, [128, D], F32)
        self.xtA = [view(arC, 32768, [128, D], F32), view(arC, 32768, [128, D], F32)]
        self.lnA = [view(arC, 40960, [128, D], BF16), view(arC, 45056, [128, D], BF16)]
        self.wslab = [view(arC, 0, [128, ND, 512], BF16), view(arC, 16384, [128, ND, 512], BF16)]
        self.stage = view(arB, 49152, [128, NT, 512], BF16)
        self.stageK = view(arB, 0, [128, NT, 512], BF16)
        self.qslab = [view(arC, 0, [128, NT, 512], BF16), view(arC, 16384, [128, NT, 512], BF16)]
        self.Srow = view(arC, 32768, [128, D], F32)
        self.Ptile = view(arC, 40960, [128, D], BF16)
        self.ATcol = view(arC, 45056, [128, ND, 128], BF16)
        self.attst = view(arC, 49152, [128, D], BF16)
        self.attsl = [view(arC, 0, [128, ND, 128], BF16), view(arC, 4096, [128, ND, 128], BF16)]
        self.xtD = view(arC, 8192, [128, D], F32)
        self.ytD = view(arC, 16384, [128, D], F32)
        self.x2tD = [view(arC, 24576, [128, D], F32), view(arC, 32768, [128, D], F32)]
        self.lnD = [view(arC, 40960, [128, D], BF16), view(arC, 45056, [128, D], BF16)]
        self.GT = view(arC, 0, [128, NL, T], BF16)
        self.dwsl = view(arC, 32768, [128, ND, 512], BF16)
        self.g2st = view(arC, 49152, [128, D], BF16)
        self.agT9 = arC[0:9, 32768:32768 + T * 4].bitcast(F32)
        self.stl = [view(arC, 32768, [128, TB], BF16), view(arC, 34816, [128, TB], BF16)]
        self.xtraB = view(arC, 36864, [128, TB], BF16)
        self.w1sl = [view(arC, 40960, [128, NL, 128], BF16), view(arC, 43008, [128, NL, 128], BF16)]
        self.xtraF = view(arC, 45056, [128, TB], F32)
        self.B2bc = view(arC, 49152, [128, L], F32)
        self.z16T = view(arC, 0, [128, NL, T], BF16)
        self.g2sl = [view(arC, 32768, [128, ND, 128], BF16), view(arC, 36864, [128, ND, 128], BF16)]
        self.zst = view(arC, 40960, [128, L], BF16)
        self.sqbG = view(arC, 40960, [128, D], BF16)
        self.B_ud = view(arC, 45056, [128, D], F32)

        self.identf = nc.alloc_sbuf_tensor("identf", [128, 128], F32)
        self.identb = nc.alloc_sbuf_tensor("identb", [128, 128], BF16)
        self.st = nc.alloc_sbuf_tensor("stats", [128, 32], F32)
        self.RW = nc.alloc_sbuf_tensor("rw3", [128, 16, 14, 8], F32)
        self.rst = nc.alloc_sbuf_tensor("rst", [128, 16, 12], F32)
        self.gsbc = nc.alloc_sbuf_tensor("gsbc", [128, TB], BF16)
        self.abc = nc.alloc_sbuf_tensor("abc", [128, TB], BF16)
        self.bbc = nc.alloc_sbuf_tensor("bbc", [128, TB], BF16)
        self.dwnbT = nc.alloc_sbuf_tensor("dwnbT", [128, 8], F32)
        self.sb1T = nc.alloc_sbuf_tensor("sb1T", [128, 8], F32)
        self.b1T = nc.alloc_sbuf_tensor("b1T", [128, 160], F32)
        self.rT = nc.alloc_sbuf_tensor("rT", [128, 160], F32)
        self.brows = nc.alloc_sbuf_tensor("brows", [128, 128], F32)
        self.rbbc = nc.alloc_sbuf_tensor("rbbc", [128, 8], F32)
        self.rbtmp = nc.alloc_sbuf_tensor("rbtmp", [128, 8], F32)
        self.rwsb = nc.alloc_sbuf_tensor("rwsb", [128, ND, 8], BF16)
        self.rzc = nc.alloc_sbuf_tensor("rzc", [128, 4], F32)
        self.constv = nc.alloc_sbuf_tensor("constv", [128, 8], F32)
        self.CONSTS = [EPS, -0.5, 0.125, 0.1, 1.0, 0.0]
        for i, cval in enumerate(self.CONSTS):
            nc.const_aps.register(F32, cval, self.constv[:, i:i + 1])
        assert nc.sbuf_bytes_remaining >= 0, nc.sbuf_bytes_remaining

    def _build(self):
        from contextlib import ExitStack
        nc = self.nc
        with ExitStack() as ctx:
            self.pq = [ctx.enter_context(nc.psum_tensor(f"pq{i}", [128, 1024], F32))
                       for i in range(4)]
            self.pb = [self.pq[i // 2][:, (i % 2) * 512:(i % 2 + 1) * 512]
                       for i in range(8)]
            self.pbbf = [self.pb[i].bitcast(BF16) for i in range(8)]
            self.sem = {}
            for nm in SEM_NAMES:
                self.sem[nm] = ctx.enter_context(nc.semaphore(nm))
            self.cnt = {nm: 0 for nm in SEM_NAMES}
            with nc.Block() as block:
                self.block = block
                self._program()

    def inc(self, inst, nm, n=1):
        self.cnt[nm] += n
        inst.then_inc(self.sem[nm], n)
        return (nm, self.cnt[nm])

    def W(self, eng, *toks):
        best = {}
        for tk in toks:
            if tk is not None and tk[1] > 0:
                if best.get(tk[0], 0) < tk[1]:
                    best[tk[0]] = tk[1]
        for nm, v in best.items():
            eng.wait_ge(self.sem[nm], v)

    def tokd(self, nm):
        return (nm, self.cnt[nm])

    def vf(self, v, inst):
        tk = self.inc(inst, "sDVE")
        v.wait_ge(self.sem["sDVE"], tk[1])

    def af(self, s, inst):
        tk = self.inc(inst, "sACT")
        s.wait_ge(self.sem["sACT"], tk[1])

    def _program(self):
        self._setup()
        pl = self.phase_limit
        if pl >= 1:
            self._phaseA()
        if pl >= 2:
            self._phaseB()
        if pl >= 3:
            self._phaseC()
        if pl >= 4:
            self._phaseD()
        if pl >= 5:
            self._phaseE()
        if pl >= 6:
            self._phaseF()
        if pl >= 7:
            self._phaseG()
        self._dbg_flush()
        self._final_wait()

    def _setup(self):
        block = self.block

        @block.sync
        def _(sy):
            self.inc(sy.dma_start(self.identf[:, :], self.ident), "sCONST", 16)
            self.inc(sy.dma_start(self.rbbc[:, :],
                                  self.router_b[None, :].to_broadcast([128, 8])), "sCONST", 16)
            self.t_const = self.inc(sy.dma_start(
                self.rbtmp[:, :], self.negcost.to_broadcast([128, 8])), "sCONST", 16)

        @block.gpsimd
        def _(g):
            self.inc(g.dma_start(self.identb[:, :], self.ident), "sWC", 16)
            self.t_wc = self.inc(g.dma_start(
                self.rwsb[:, :, :],
                self.router_w.rearrange("(c p) e -> p c e", p=128)), "sWC", 16)

        @block.vector
        def _(v):
            for i, cval in enumerate(self.CONSTS):
                v.memset(self.constv[:, i:i + 1], cval)
            self.W(v, self.t_const)
            self.t_rbbc = self.inc(
                v.tensor_add(self.rbbc[:, :], self.rbbc[:, :], self.rbtmp[:, :]), "sDVE")

    def _ln_stats(self, v, xsrc, sqdump):
        st = self.st
        v.reduce_sum(st[:, 0:1], xsrc, axis=AX.X)
        self.vf(v, v.scalar_tensor_tensor(sqdump, xsrc, 1.0, xsrc,
                                          op0=ALU.mult, op1=ALU.mult, accum_out=st[:, 1:2]))
        self.vf(v, v.tensor_scalar(st[:, 2:3], st[:, 0:1], -SQD, None, op0=ALU.mult))
        self.vf(v, v.tensor_mul(st[:, 3:4], st[:, 2:3], st[:, 2:3]))
        self.vf(v, v.scalar_tensor_tensor(st[:, 4:5], st[:, 1:2], SQD, st[:, 3:4],
                                          op0=ALU.mult, op1=ALU.subtract))
        return self.tokd("sDVE")

    def _ln_rstd(self, s):
        st = self.st
        self.af(s, s.activation(st[:, 5:6], st[:, 4:5], AF.Ln, bias=EPS))
        self.af(s, s.activation(st[:, 6:7], st[:, 5:6], AF.Exp, scale=-0.5))

    def _transp16(self, src_bf, dst3, tt, gate, state):
        block = self.block
        pbbf = self.pbbf
        nch = dst3.shape[1]
        per = nch // 2

        for half in range(2):
            @block.tensor
            def _(t, half=half):
                self.W(t, gate, state["trev"][half])
                if not getattr(self, "_identb_ok", False):
                    self.W(t, self.t_wc)
                    self._identb_ok = True
                bank = pbbf[6 + half]
                last = None
                for i in range(per):
                    fc = half * per + i
                    last = t.transpose(bank[:, i * 128:(i + 1) * 128],
                                       src_bf[:, fc * 128:(fc + 1) * 128], self.identb[:, :])
                state["ttok"][half] = self.inc(last, "sPE")

            @block.vector
            def _(v, half=half):
                self.W(v, state["ttok"][half])
                bank = pbbf[6 + half]
                dst = dst3[:, half * per:(half + 1) * per, tt * 128:(tt + 1) * 128]
                state["trev"][half] = self.inc(
                    v.tensor_copy(dst, bank[:, 0:per * 128].rearrange(
                        "p (c n) -> p c n", c=per)), "sDVE")
        return state["ttok"][1]

    def _phaseA(self):
        block = self.block
        self.lnT_tok = {}
        stA = {"ttok": {0: None, 1: None}, "trev": {0: None, 1: None}}
        apply_tok = {}
        tt_pe = {}
        xload = {}
        wtokA = {}

        @block.gpsimd
        def _(g):
            for k in range(2):
                src = [self.Wq, self.Wk][k][:, 0:512]
                for h in range(2):
                    wtokA[k] = self.inc(g.dma_start(
                        self.wslab[k][:, h * 8:(h + 1) * 8, :],
                        src.rearrange("(c p) n -> p c n", p=128)[:, h * 8:(h + 1) * 8, :]),
                        "sWS%d" % k, 16)

        self.Bst = {"slab_pe": {}, "ev_bank": {i: None for i in range(4)}, "pbt": [None]}
        V0 = self.cnt["sDVE"]
        A0 = self.cnt["sACT"]

        @block.scalar
        def _(s, V0=V0):
            for tt in range(NT):
                s.wait_ge(self.sem["sDVE"], V0 + 9 * tt + 4)
                self._ln_rstd(s)
        assert self.cnt["sACT"] == A0 + 2 * NT

        for tt in range(NT):
            @block.sync
            def _(sy, tt=tt):
                if tt >= 1:
                    self.W(sy, apply_tok[tt - 1])
                xload[tt] = self.inc(
                    sy.dma_start(self.xtA[0][:, :], self.x[tt * 128:(tt + 1) * 128, :]),
                    "sX0", 16)

            @block.vector
            def _(v, tt=tt):
                self.W(v, xload[tt])
                if tt >= 2:
                    self.W(v, tt_pe[tt - 2])
                self._ln_stats(v, self.xtA[0][:, :], self.lnA[tt % 2][:, :])
            tv = self.tokd("sDVE")
            assert tv[1] == V0 + 9 * tt + 4
            t_rs = ("sACT", A0 + 2 * (tt + 1))

            @block.vector
            def _(v, tt=tt, t_rs=t_rs):
                self.W(v, t_rs)
                apply_tok[tt] = self.inc(
                    v.tensor_scalar(self.lnA[tt % 2][:, :], self.xtA[0][:, :],
                                    self.st[:, 2:3], self.st[:, 6:7],
                                    op0=ALU.add, op1=ALU.mult), "sDVE")

            tt_pe[tt] = self._transp16(self.lnA[tt % 2], self.ln1T, tt, apply_tok[tt], stA)
            self.lnT_tok[tt] = self.tokd("sDVE")

            for k in range(2):
                bk = tt % 2 if k == 0 else 2 + tt % 2

                @block.tensor
                def _(t, tt=tt, k=k, bk=bk):
                    if tt == 0:
                        self.W(t, wtokA[k])
                    self.W(t, self.lnT_tok[tt], self.Bst["ev_bank"][bk])
                    last = None
                    for fc in range(ND):
                        last = t.matmul(self.pb[bk][:, :],
                                        self.ln1T[:, fc, tt * 128:(tt + 1) * 128],
                                        self.wslab[k][:, fc, :],
                                        start=(fc == 0), stop=(fc == ND - 1))
                    self.Bst["pbt"][0] = self.inc(last, "sPE")
                    if tt == NT - 1:
                        self.Bst["slab_pe"][k] = self.Bst["pbt"][0]

                @block.vector
                def _(v, tt=tt, k=k, bk=bk):
                    self.W(v, self.Bst["pbt"][0])
                    dst = (self.stage if k == 0 else self.stageK)[:, tt, :]
                    self.Bst["ev_bank"][bk] = self.inc(
                        v.tensor_copy(dst, self.pb[bk][:, :]), "sDVE")

        @block.sync
        def _(sy):
            self.W(sy, self.tokd("sDVE"))
            self.inc(sy.dma_start(self.scrQ[0].rearrange("t p n -> p t n"),
                                  self.stage[:, :, :]), "sSPQ", 16)
            self.inc(sy.dma_start(self.scrK[0].rearrange("t p n -> p t n"),
                                  self.stageK[:, :, :]), "sSPQ", 16)
        self.Bst["stage_spill"] = self.tokd("sSPQ")
        self.t_ln1T = self.tokd("sDVE")
        self.t_A_pe = self.tokd("sPE")
        self._dbg_mark("A")

    def _phaseB(self):
        block = self.block
        pb = self.pb
        WQKV = [self.Wq, self.Wk, self.Wv]
        slab_pe = self.Bst["slab_pe"]
        ev_bank = self.Bst["ev_bank"]
        stage_spill = self.Bst["stage_spill"]
        pbt = self.Bst["pbt"]

        for fo in range(4):
            for wi in range(3):
                if fo == 0 and wi < 2:
                    continue
                k = fo * 3 + wi
                wtok = [None]

                @block.gpsimd
                def _(g, fo=fo, wi=wi, k=k):
                    if k >= 2:
                        self.W(g, slab_pe[k - 2])
                    src = WQKV[wi][:, fo * 512:(fo + 1) * 512]
                    snm = "sWS%d" % (k % 2)
                    for h in range(2):
                        wtok[0] = self.inc(g.dma_start(
                            self.wslab[k % 2][:, h * 8:(h + 1) * 8, :],
                            src.rearrange("(c p) n -> p c n", p=128)[:, h * 8:(h + 1) * 8, :]),
                            snm, 16)

                if wi < 2:
                    for tt in range(NT):
                        @block.tensor
                        def _(t, k=k, tt=tt):
                            if tt == 0:
                                self.W(t, wtok[0])
                            if k < 2:
                                self.W(t, self.lnT_tok[tt])
                            bk = tt % 2
                            self.W(t, ev_bank[bk])
                            last = None
                            for fc in range(ND):
                                last = t.matmul(pb[bk][:, :],
                                                self.ln1T[:, fc, tt * 128:(tt + 1) * 128],
                                                self.wslab[k % 2][:, fc, :],
                                                start=(fc == 0), stop=(fc == ND - 1))
                            pbt[0] = self.inc(last, "sPE")
                            if tt == NT - 1:
                                slab_pe[k] = pbt[0]

                        @block.vector
                        def _(v, tt=tt):
                            self.W(v, pbt[0])
                            if tt == 0:
                                self.W(v, stage_spill)
                            ev_bank[tt % 2] = self.inc(
                                v.tensor_copy(self.stage[:, tt, :], pb[tt % 2][:, :]), "sDVE")

                    @block.sync
                    def _(sy, fo=fo, wi=wi):
                        self.W(sy, self.tokd("sDVE"))
                        dst = (self.scrQ if wi == 0 else self.scrK)[fo]
                        self.inc(sy.dma_start(dst.rearrange("t p n -> p t n"),
                                              self.stage[:, :, :]), "sSPQ", 16)
                    stage_spill = self.tokd("sSPQ")
                else:
                    for i in range(16):
                        fc4 = i // 4

                        @block.tensor
                        def _(t, k=k, fo=fo, fc4=fc4, i=i):
                            tb4 = i % 4
                            if i == 0:
                                self.W(t, wtok[0])
                            if k < 3:
                                self.W(t, self.lnT_tok[4 * tb4 + 3])
                            bk = 2 + i % 2
                            self.W(t, ev_bank[bk])
                            last = None
                            for fc in range(ND):
                                last = t.matmul(pb[bk][:, :],
                                                self.wslab[k % 2][:, fc, fc4 * 128:(fc4 + 1) * 128],
                                                self.ln1T[:, fc, tb4 * 512:(tb4 + 1) * 512],
                                                start=(fc == 0), stop=(fc == ND - 1))
                            pbt[0] = self.inc(last, "sPE")
                            if i == 15:
                                slab_pe[k] = pbt[0]

                        @block.vector
                        def _(v, fo=fo, fc4=fc4, i=i):
                            tb4 = i % 4
                            self.W(v, pbt[0])
                            if i == 0 and fo in (0, 3):
                                self.W(v, stage_spill)
                            dst = self.VT[:, fo * 4 + fc4, tb4 * 512:(tb4 + 1) * 512]
                            ev_bank[2 + i % 2] = self.inc(
                                v.tensor_copy(dst, pb[2 + i % 2][:, :]), "sDVE")
        self.t_B_pe = self.tokd("sPE")
        self.t_B_spill = self.tokd("sSPQ")
        self.t_B_vt = self.tokd("sDVE")
        self._dbg_mark("B")

    def _phaseC(self):
        block = self.block
        pb = self.pb
        st = self.st
        t_Kload = [None]

        @block.sync
        def _(sy):
            self.W(sy, self.t_B_pe, self.t_B_spill)
            for fo in range(4):
                for hh in range(2):
                    t_Kload[0] = self.inc(sy.dma_start(
                        self.Kar[:, hh * 8:(hh + 1) * 8, fo * 512:(fo + 1) * 512],
                        self.scrK[fo][hh * 8:(hh + 1) * 8].rearrange("t p n -> p t n")),
                        "sKL", 16)

        qload, qcons = {}, {}
        s_evac = {}
        t_exp = {}
        t_rz = {}
        av_ev = {0: None, 1: None}
        att_sp = None
        stC = {"ttok": {0: None, 1: None}, "trev": {0: None, 1: None}}
        Stok = {}
        avtok = [None]

        def emit_S(ec):
            fo = ec // 4
            if ec % 4 == 0:
                @block.sync
                def _(sy, fo=fo):
                    if fo >= 2:
                        self.W(sy, qcons[fo - 2])
                    for hh in range(2):
                        qload[fo] = self.inc(sy.dma_start(
                            self.qslab[fo % 2][:, hh * 8:(hh + 1) * 8, :],
                            self.scrQ[fo][hh * 8:(hh + 1) * 8].rearrange("t p n -> p t n")),
                            "sQS%d" % (fo % 2), 16)

            @block.tensor
            def _(t, ec=ec, fo=fo):
                self.W(t, qload[fo], t_Kload[0], s_evac.get(ec - 1))
                if ec == 0:
                    self.W(t, self.t_B_vt)
                esub = ec % 4
                last = None
                for fb in range(4):
                    for tt in range(NT):
                        last = t.matmul(pb[fb][:, :],
                                        self.qslab[fo % 2][:, tt, esub * 128:(esub + 1) * 128],
                                        self.Kar[:, tt, fb * 512:(fb + 1) * 512],
                                        start=(tt == 0), stop=(tt == NT - 1))
                Stok[ec] = self.inc(last, "sPE")
                if esub == 3:
                    qcons[fo] = Stok[ec]

        def emit_TAV_mm(ec):
            self._transp16(self.Ptile, self.ATcol, 0, t_exp[ec], stC)
            t_at = self.tokd("sDVE")
            A0 = self.cnt["sACT"]
            avt = []
            for tb4 in range(4):
                @block.tensor
                def _(t, ec=ec, tb4=tb4, t_at=t_at):
                    if tb4 == 0:
                        self.W(t, t_at)
                    bk = 4 + tb4 % 2
                    self.W(t, av_ev[tb4 % 2])
                    last = None
                    for fc in range(ND):
                        last = t.matmul(pb[bk][:, :], self.ATcol[:, fc, :],
                                        self.VT[:, fc, tb4 * 512:(tb4 + 1) * 512],
                                        start=(fc == 0), stop=(fc == ND - 1))
                    avt.append(self.inc(last, "sPE"))
                av_ev[tb4 % 2] = ("sACT", A0 + tb4 + 2)
            return avt, A0

        def emit_scalar(ec_tav, avt, ec_sm):
            nonlocal att_sp
            V0 = self.cnt["sDVE"]

            @block.scalar
            def _(sc, ec_tav=ec_tav, avt=avt, ec_sm=ec_sm, V0=V0):
                if ec_sm is not None:
                    self.W(sc, Stok[ec_sm])
                    if ec_sm >= 1:
                        self.W(sc, t_exp[ec_sm - 1])
                    last = None
                    for fb in range(4):
                        last = sc.activation(self.Srow[:, fb * 512:(fb + 1) * 512],
                                             pb[fb][:, :], AF.Copy, scale=0.125)
                    self.inc(last, "sACT")
                    s_evac[ec_sm] = self.tokd("sACT")
                if avt is not None:
                    for tb4 in range(4):
                        self.W(sc, avt[tb4], t_rz[ec_tav])
                        if tb4 == 0:
                            self.W(sc, att_sp)
                        self.inc(sc.activation(
                            self.attst[:, tb4 * 512:(tb4 + 1) * 512],
                            pb[4 + tb4 % 2][:, :], AF.Copy,
                            scale=self.rzc[:, ec_tav % 2:ec_tav % 2 + 1]), "sACT")
                if ec_sm is not None:
                    sc.wait_ge(self.sem["sDVE"], V0 + 1)
                    if ec_sm >= 1:
                        self.W(sc, stC["ttok"][1])
                    self.inc(sc.activation(self.Ptile[:, :], self.Srow[:, :], AF.Exp,
                                           bias=st[:, 16 + ec_sm % 2:17 + ec_sm % 2],
                                           accum_out=st[:, 18 + ec_sm % 2:19 + ec_sm % 2]),
                             "sACT")
                    t_exp[ec_sm] = self.tokd("sACT")

            if ec_sm is not None:
                @block.vector
                def _(v, ec=ec_sm, V0=V0):
                    self.W(v, s_evac[ec])
                    self.vf(v, v.reduce_max(st[:, 16 + ec % 2:17 + ec % 2], self.Srow[:, :],
                                            axis=AX.X, negate=True))
                    assert self.cnt["sDVE"] == V0 + 1
                    self.W(v, t_exp[ec])
                    self.vf(v, v.reciprocal(self.rzc[:, ec % 2:ec % 2 + 1],
                                            st[:, 18 + ec % 2:19 + ec % 2]))
                    t_rz[ec] = self.tokd("sDVE")

            if avt is not None:
                @block.sync
                def _(sy, ec=ec_tav):
                    self.W(sy, self.tokd("sACT") if ec_sm is None
                           else av_ev[1])
                    self.inc(sy.dma_start(
                        self.scrATT[:, :, ec, :].rearrange("t p n -> p t n"),
                        self.attst.rearrange("p (t n) -> p t n", t=NT)), "sASP", 16)
                att_sp = self.tokd("sASP")

        emit_S(0)
        emit_scalar(None, None, 0)
        for ec in range(1, 16):
            emit_S(ec)
            avt, _ = emit_TAV_mm(ec - 1)
            emit_scalar(ec - 1, avt, ec)
        avt, A0f = emit_TAV_mm(15)
        for b in (0, 1):
            av_ev[b] = ("sACT", av_ev[b][1] - 1)
        emit_scalar(15, avt, None)
        self.t_C_sp = self.tokd("sASP")
        self.t_C_pe = self.tokd("sPE")
        self.t_C_sev = s_evac[15]
        self.t_C_avev = max(av_ev[0], av_ev[1], key=lambda x: x[1])
        self._dbg_mark("C")

    def _phaseD(self):
        block = self.block
        pb = self.pb
        st = self.st
        t_wo = [None]

        @block.gpsimd
        def _(g):
            self.W(g, self.t_C_pe)
            for fo in range(4):
                for hh in range(2):
                    t_wo[0] = self.inc(g.dma_start(
                        self.Wosb[:, hh * 8:(hh + 1) * 8, fo * 512:(fo + 1) * 512],
                        self.Wo[:, fo * 512:(fo + 1) * 512]
                        .rearrange("(c p) n -> p c n", p=128)[:, hh * 8:(hh + 1) * 8, :]),
                        "sWO", 16)

        att_ld, x_ld, x2_sp = {}, {}, {}
        ytok, x2tok, apply2, ttokD = {}, {}, {}, {}
        stD = {"ttok": {0: None, 1: None}, "trev": {0: None, 1: None}}
        V0D = self.cnt["sDVE"]
        A0D = self.cnt["sACT"]

        def emit_loads(tt):
            @block.sync
            def _(sy, tt=tt):
                if tt == 0:
                    self.W(sy, self.t_C_sp)
                if tt >= 2:
                    self.W(sy, ytok[tt - 2])
                att_ld[tt] = self.inc(sy.dma_start(
                    self.attsl[tt % 2][:, :, :], self.scrATT[tt]),
                    "sAL%d" % (tt % 2), 16)

            @block.gpsimd
            def _(g, tt=tt):
                if tt >= 1:
                    self.W(g, ("sDVE", V0D + 15 * (tt - 1) + 8))
                x_ld[tt] = self.inc(g.dma_start(
                    self.xtD[:, :], self.x[tt * 128:(tt + 1) * 128, :]), "sXD", 16)

        def emit_Y(tt):
            @block.tensor
            def _(t, tt=tt):
                self.W(t, att_ld[tt])
                if tt == 0:
                    self.W(t, t_wo[0], self.t_C_sev)
                else:
                    self.W(t, ("sACT", A0D + 7 * (tt - 1) + 1))
                last = None
                for db in range(4):
                    for ec in range(ND):
                        last = t.matmul(pb[db][:, :], self.attsl[tt % 2][:, ec, :],
                                        self.Wosb[:, ec, db * 512:(db + 1) * 512],
                                        start=(ec == 0), stop=(ec == ND - 1))
                ytok[tt] = self.inc(last, "sPE")

        def emit_chain(tt):
            V0 = self.cnt["sDVE"]
            A0 = self.cnt["sACT"]
            assert V0 == V0D + 15 * tt and A0 == A0D + 7 * tt

            @block.scalar
            def _(s, tt=tt, V0=V0):
                self.W(s, ytok[tt])
                if tt >= 1:
                    self.W(s, x2tok[tt - 1])
                if tt >= 2:
                    self.W(s, x2_sp[tt - 2], apply2[tt - 2])
                last = None
                for db in range(4):
                    last = s.activation(self.ytD[:, db * 512:(db + 1) * 512],
                                        pb[db][:, :], AF.Copy,
                                        accum_out=st[:, 8 + db:9 + db])
                self.af(s, last)
                self.inc(s.activation(self.x2tD[tt % 2][:, :], self.ytD[:, :], AF.Square,
                                      accum_out=st[:, 1:2]), "sACT")
                s.wait_ge(self.sem["sDVE"], V0 + 6)
                self.af(s, s.activation(st[:, 5:6], st[:, 4:5], AF.Ln, bias=EPS))
                self.af(s, s.activation(st[:, 6:7], st[:, 5:6], AF.Exp, scale=-0.5))
                s.wait_ge(self.sem["sDVE"], V0 + 8)
                self.inc(s.activation(self.ytD[:, :], self.x2tD[tt % 2][:, :], AF.Square,
                                      accum_out=st[:, 17:18]), "sACT")
                s.wait_ge(self.sem["sDVE"], V0 + 12)
                self.af(s, s.activation(st[:, 21:22], st[:, 20:21], AF.Ln, bias=EPS))
                self.af(s, s.activation(st[:, 22:23], st[:, 21:22], AF.Exp, scale=-0.5))
            t_evsq = ("sACT", A0 + 2)
            t_r = ("sACT", A0 + 4)
            t_sq2 = ("sACT", A0 + 5)
            t_rs2 = ("sACT", A0 + 7)
            assert self.cnt["sACT"] == A0 + 7

            @block.vector
            def _(v, tt=tt, t_evsq=t_evsq):
                self.W(v, t_evsq)
                self.vf(v, v.tensor_add(st[:, 12:13], st[:, 8:9], st[:, 9:10]))
                self.vf(v, v.tensor_add(st[:, 13:14], st[:, 10:11], st[:, 11:12]))
                self.vf(v, v.tensor_add(st[:, 0:1], st[:, 12:13], st[:, 13:14]))
                self.vf(v, v.tensor_scalar(st[:, 2:3], st[:, 0:1], -SQD, None, op0=ALU.mult))
                self.vf(v, v.tensor_mul(st[:, 3:4], st[:, 2:3], st[:, 2:3]))
                self.vf(v, v.scalar_tensor_tensor(st[:, 4:5], st[:, 1:2], SQD, st[:, 3:4],
                                                  op0=ALU.mult, op1=ALU.subtract))

            @block.vector
            def _(v, tt=tt, t_r=t_r):
                self.W(v, t_r, x_ld[tt])
                self.vf(v, v.tensor_scalar(self.x2tD[tt % 2][:, :], self.ytD[:, :],
                                           st[:, 2:3], st[:, 6:7],
                                           op0=ALU.add, op1=ALU.mult))
                self.vf(v, v.tensor_add(self.x2tD[tt % 2][:, :], self.x2tD[tt % 2][:, :],
                                        self.xtD[:, :]))
                x2tok[tt] = self.tokd("sDVE")
            assert x2tok[tt][1] == V0 + 8

            @block.sync
            def _(sy, tt=tt):
                self.W(sy, x2tok[tt])
                x2_sp[tt] = self.inc(
                    sy.dma_start(self.scrX2[tt], self.x2tD[tt % 2][:, :]),
                    "sX2S%d" % (tt % 2), 16)

            @block.vector
            def _(v, tt=tt, t_sq2=t_sq2):
                self.vf(v, v.reduce_sum(st[:, 16:17], self.x2tD[tt % 2][:, :], axis=AX.X))
                self.vf(v, v.tensor_scalar(st[:, 18:19], st[:, 16:17], -SQD, None,
                                           op0=ALU.mult))
                self.vf(v, v.tensor_mul(st[:, 19:20], st[:, 18:19], st[:, 18:19]))
                self.W(v, t_sq2)
                self.vf(v, v.scalar_tensor_tensor(st[:, 20:21], st[:, 17:18], SQD,
                                                  st[:, 19:20],
                                                  op0=ALU.mult, op1=ALU.subtract))
            assert self.cnt["sDVE"] == V0 + 12

            @block.vector
            def _(v, tt=tt, t_rs2=t_rs2):
                self.W(v, t_rs2)
                if tt >= 2:
                    self.W(v, ttokD[tt - 2])
                apply2[tt] = self.inc(
                    v.tensor_scalar(self.lnD[tt % 2][:, :], self.x2tD[tt % 2][:, :],
                                    st[:, 18:19], st[:, 22:23],
                                    op0=ALU.add, op1=ALU.mult), "sDVE")

            ttokD[tt] = self._transp16(self.lnD[tt % 2], self.ln2T, tt, apply2[tt], stD)

        for tt in range(NT):
            emit_loads(tt)
            emit_Y(tt)
            if tt >= 1:
                emit_chain(tt - 1)
        emit_chain(NT - 1)
        self.t_ln2T = self.tokd("sDVE")
        self.t_D_x2sp = [x2_sp[NT - 2], x2_sp[NT - 1]]
        self.t_D_yev = None
        self._dbg_mark("D")

    def _phaseE(self):
        block = self.block
        pb = self.pb
        t_bias1 = [None]

        @block.sync
        def _(sy):
            self.inc(sy.dma_start(self.brows[0:8, :],
                                  self.down_b.rearrange("(c n) -> c n", c=8)), "sBR", 16)
            t_bias1[0] = self.inc(sy.dma_start(
                self.brows[8:16, :],
                self.shared_b1.rearrange("(c n) -> c n", c=8)), "sBR", 16)

        bt = [None]

        @block.tensor
        def _(t):
            self.W(t, t_bias1[0], self.t_C_avev)
            bt[0] = self.inc(t.transpose(pb[5][:, 0:128], self.brows[:, :],
                                         self.identf[:, :]), "sPE")

        @block.vector
        def _(v):
            self.W(v, bt[0])
            v.tensor_copy(self.dwnbT[:, 0:8], pb[5][:, 0:8])
            self.inc(v.tensor_copy(self.sb1T[:, 0:8], pb[5][:, 8:16]), "sDVE")
        t_btv = self.tokd("sDVE")

        self.b1off = {}
        boff = 0
        b1t = {}
        for e in range(8):
            self.b1off[e] = boff

            @block.sync
            def _(sy, e=e):
                if e >= 1:
                    self.W(sy, b1t[e - 1])
                else:
                    self.W(sy, bt[0])
                self.inc(sy.dma_start(self.brows[0:NJ[e], :],
                                      self.eb1[e].rearrange("(c n) -> c n", c=NJ[e])),
                         "sBR", 16)

            @block.tensor
            def _(t, e=e):
                self.W(t, self.tokd("sBR"), self.tokd("sDVE"))
                b1t[e] = self.inc(t.transpose(pb[5][:, 0:128], self.brows[:, :],
                                              self.identf[:, :]), "sPE")

            @block.vector
            def _(v, e=e, boff=boff):
                self.W(v, b1t[e])
                self.inc(v.tensor_copy(self.b1T[:, boff:boff + NJ[e]],
                                       pb[5][:, 0:NJ[e]]), "sDVE")
            boff += NJ[e]
        t_b1 = self.tokd("sDVE")

        @block.scalar
        def _(s):
            self.W(s, t_b1)
            self.inc(s.activation(self.rT[:, 0:160], self.b1T[:, 0:160], AF.Gelu), "sACT")
        self.t_rT = self.tokd("sACT")

        gcons = {}
        gt = [None]
        gev = {0: self.t_D_yev, 1: self.t_D_yev}
        dwl = {}
        S0 = self.cnt["sG2S"]
        g2src = self.g2st.rearrange("p (t n) -> p t n", t=NT)
        for lh in range(2):
            @block.gpsimd
            def _(g, lh=lh):
                if lh == 0:
                    self.W(g, self.t_ln2T, self.t_D_x2sp[0], self.t_D_x2sp[1])
                else:
                    self.W(g, gcons[0])
                for hh in range(2):
                    dwl[lh] = self.inc(g.dma_start(
                        self.dwsl[:, hh * 8:(hh + 1) * 8, :],
                        self.down_w[:, lh * 512:(lh + 1) * 512]
                        .rearrange("(c p) n -> p c n", p=128)[:, hh * 8:(hh + 1) * 8, :]),
                        "sDW", 16)

            A0 = self.cnt["sACT"]
            order = []
            for i in range(16):
                order.append(("ev", i))
                if i % 2 == 1:
                    order.append(("g2", lh * 8 + i // 2))
            tok_of = {it: ("sACT", A0 + pos + 1) for pos, it in enumerate(order)}

            gtl = []
            for i in range(16):
                lc4, tb4 = i // 4, i % 4

                @block.tensor
                def _(t, lh=lh, lc4=lc4, tb4=tb4, i=i):
                    if i == 0:
                        self.W(t, dwl[lh])
                    bk = i % 2
                    self.W(t, gev[bk])
                    last = None
                    for fc in range(ND):
                        last = t.matmul(pb[bk][:, :],
                                        self.dwsl[:, fc, lc4 * 128:(lc4 + 1) * 128],
                                        self.ln2T[:, fc, tb4 * 512:(tb4 + 1) * 512],
                                        start=(fc == 0), stop=(fc == ND - 1))
                    gtl.append(self.inc(last, "sPE"))
                    if i == 15:
                        gcons[lh] = gtl[-1]
                gev[i % 2] = tok_of[("ev", i)]

            @block.scalar
            def _(sc, lh=lh, gtl=gtl, order=order, S0=S0):
                for it in order:
                    if it[0] == "ev":
                        i = it[1]
                        lc4, tb4 = i // 4, i % 4
                        Lc = lh * 4 + lc4
                        self.W(sc, gtl[i])
                        if lh == 0 and i == 0:
                            self.W(sc, t_btv, self.t_D_x2sp[0], self.t_D_x2sp[1],
                                   self.t_ln2T)
                        self.inc(sc.activation(
                            self.GT[:, Lc, tb4 * 512:(tb4 + 1) * 512], pb[i % 2][:, :],
                            AF.Gelu, bias=self.dwnbT[:, Lc:Lc + 1]), "sACT")
                    else:
                        fc = it[1]
                        if fc >= 1:
                            sc.wait_ge(self.sem["sG2S"], S0 + 16 * fc)
                        self.inc(sc.activation(self.g2st[:, :], self.ln2T[:, fc, :],
                                               AF.Gelu), "sACT")
            assert self.cnt["sACT"] == A0 + 24

            @block.sync
            def _(sy, order=order, tok_of=tok_of):
                for it in order:
                    if it[0] != "g2":
                        continue
                    fc = it[1]
                    self.W(sy, tok_of[it])
                    self.inc(sy.dma_start(
                        self.scrG2[:, :, fc, :].rearrange("t p n -> p t n"),
                        g2src), "sG2S", 16)
        self.t_g2 = self.tokd("sG2S")
        assert self.t_g2[1] == S0 + 16 * ND
        self.t_GT = self.tokd("sACT")


        lgev = {}
        lgt = [None]
        for tt in range(NT):
            @block.tensor
            def _(t, tt=tt):
                if tt == 0:
                    self.W(t, self.t_wc, self.t_C_avev)
                bk = 4 + tt % 2
                if tt >= 2:
                    self.W(t, lgev[tt - 2])
                last = None
                for fc in range(ND):
                    last = t.matmul(pb[bk][:, 0:8],
                                    self.ln2T[:, fc, tt * 128:(tt + 1) * 128],
                                    self.rwsb[:, fc, :], start=(fc == 0), stop=(fc == ND - 1))
                lgt[0] = self.inc(last, "sPE")

            @block.vector
            def _(v, tt=tt):
                self.W(v, lgt[0])
                if tt == 0:
                    self.W(v, self.t_rbbc)
                lgev[tt] = self.inc(
                    v.tensor_add(self.RW[:, tt, 0, :], pb[4 + tt % 2][:, 0:8],
                                 self.rbbc[:, :]), "sDVE")
        t_logit = self.tokd("sDVE")

        RW, rst = self.RW, self.rst

        def bc(col):
            base = rst[:, :, col:col + 1]
            return bass.AP(base.tensor, base.offset,
                           [list(base.ap[0]), list(base.ap[1]), [0, 8]])

        names = ["LG", "PR", "GE", "C1", "C2", "C3", "LE", "M1", "M2", "P2", "AL", "EX", "EX2"]
        S = {nm: RW[:, :, i, :] for i, nm in enumerate(names)}
        t_mid = [None]

        @block.vector
        def _(v):
            f = lambda inst: self.vf(v, inst)
            self.W(v, t_logit)
            f(v.reduce_max(rst[:, :, 0:1], S["LG"], axis=AX.X, negate=True))
            f(v.tensor_tensor(S["EX"], S["LG"], bc(0), op=ALU.add))
            t_mid[0] = self.tokd("sDVE")

        @block.scalar
        def _(s):
            self.W(s, t_mid[0])
            self.inc(s.activation(S["EX2"], S["EX"], AF.Exp), "sACT")
        t_e = self.tokd("sACT")

        @block.vector
        def _(v):
            f = lambda inst: self.vf(v, inst)
            self.W(v, t_e)
            f(v.reduce_sum(rst[:, :, 1:2], S["EX2"], axis=AX.X))
            f(v.reciprocal(rst[:, :, 2:3], rst[:, :, 1:2]))
            f(v.tensor_tensor(S["PR"], S["EX2"], bc(2), op=ALU.mult))

            def firstmax(dst, src, vcol):
                f(v.reduce_max(rst[:, :, vcol:vcol + 1], src, axis=AX.X))
                f(v.tensor_tensor(S["GE"], src, bc(vcol), op=ALU.is_ge))
                f(v.tensor_copy(S["C1"][:, :, 0:1], S["GE"][:, :, 0:1]))
                f(v.tensor_add(S["C1"][:, :, 1:8], S["GE"][:, :, 1:8], S["GE"][:, :, 0:7]))
                f(v.tensor_copy(S["C2"][:, :, 0:2], S["C1"][:, :, 0:2]))
                f(v.tensor_add(S["C2"][:, :, 2:8], S["C1"][:, :, 2:8], S["C1"][:, :, 0:6]))
                f(v.tensor_copy(S["C3"][:, :, 0:4], S["C2"][:, :, 0:4]))
                f(v.tensor_add(S["C3"][:, :, 4:8], S["C2"][:, :, 4:8], S["C2"][:, :, 0:4]))
                f(v.tensor_single_scalar(S["LE"], S["C3"], 1.0, op=ALU.is_le))
                f(v.tensor_mul(dst, S["GE"], S["LE"]))

            firstmax(S["M1"], S["PR"], 3)
            f(v.scalar_tensor_tensor(S["P2"], S["M1"], NEG_BIG, S["PR"],
                                     op0=ALU.mult, op1=ALU.add))
            firstmax(S["M2"], S["P2"], 4)
            f(v.tensor_sub(rst[:, :, 5:6], rst[:, :, 4:5], rst[:, :, 3:4]))
            t_mid[0] = self.tokd("sDVE")

        @block.scalar
        def _(s):
            self.W(s, t_mid[0])
            self.inc(s.activation(rst[:, :, 6:7], rst[:, :, 5:6], AF.Exp), "sACT")
        t_e2 = self.tokd("sACT")

        @block.vector
        def _(v):
            f = lambda inst: self.vf(v, inst)
            self.W(v, t_e2)
            f(v.tensor_single_scalar(rst[:, :, 7:8], rst[:, :, 6:7], 1.0, op=ALU.add))
            f(v.reciprocal(rst[:, :, 8:9], rst[:, :, 7:8]))
            f(v.tensor_mul(rst[:, :, 9:10], rst[:, :, 6:7], rst[:, :, 8:9]))
            f(v.tensor_add(rst[:, :, 10:11], rst[:, :, 8:9], rst[:, :, 9:10]))
            f(v.tensor_tensor(S["GE"], S["M1"], bc(8), op=ALU.mult))
            f(v.tensor_tensor(S["M2"], S["M2"], bc(9), op=ALU.mult))
            f(v.tensor_add(S["AL"], S["GE"], S["M2"]))
            f(v.tensor_copy(S["EX"][:, :, 0:1], rst[:, :, 10:11]))
        t_alpha = self.tokd("sDVE")

        agev = {}
        agt = [None]
        for tt in range(NT):
            @block.tensor
            def _(t, tt=tt):
                if tt == 0:
                    self.W(t, t_alpha, gcons[1])
                bk = 4 + tt % 2
                if tt < 2:
                    self.W(t, lgev[14 + tt])
                else:
                    self.W(t, agev[tt - 2])
                base = self.RW[:, tt, 10, 0:1]
                ag9 = bass.AP(base.tensor, base.offset, [list(base.ap[0]), [1, 9]])
                agt[0] = self.inc(
                    t.transpose(pb[bk][0:9, 0:128], ag9, self.identf[:, :]), "sPE")

            @block.vector
            def _(v, tt=tt):
                self.W(v, agt[0])
                if tt == 0:
                    self.W(v, self.t_GT)
                agev[tt] = self.inc(
                    v.tensor_copy(self.agT9[0:9, tt * 128:(tt + 1) * 128],
                                  pb[4 + tt % 2][0:9, 0:128]), "sDVE")
        self.t_agT = self.tokd("sDVE")

        @block.gpsimd
        def _(g):
            self.W(g, self.t_agT)
            self.t_agsp = self.inc(g.dma_start(self.scrAG, self.agT9[:, :]), "sAGS", 16)
        self.t_E_pe = self.tokd("sPE")
        self.t_E_gcons = gcons[1]
        self._dbg_mark("E")

    def _phaseF(self):
        block = self.block
        pb = self.pb
        zb_ev = {0: self.t_agT, 1: self.t_agT}
        p_act = {0: self.t_GT, 1: self.t_GT}
        wdve = {0: None, 1: None}
        w1gate = {0: None, 1: None}
        w2free = {0: self.t_E_pe, 1: self.t_E_pe}
        gp = [0]
        zbt = [None]
        wcons = {}
        pend = [None]
        self._shev = {0: None, 1: None}

        t_bc = [None]

        @block.sync
        def _(sy):
            self.W(sy, self.t_g2)
            t_bc[0] = self.inc(sy.dma_start(
                self.B2bc[:, :], self.eb2[0][None, :].to_broadcast([128, L])), "sBC", 16)
        for e in range(1, 8):
            @block.sync
            def _(sy, e=e):
                self.W(sy, t_bc[0], self.tokd("sDVE"))
                if e == 1:
                    self.W(sy, self.t_agsp)
                t_bc[0] = self.inc(sy.dma_start(
                    self.xtraF[:, 0:L], self.eb2[e][None, :].to_broadcast([128, L])),
                    "sBC", 16)

            @block.vector
            def _(v, e=e):
                self.W(v, t_bc[0])
                self.inc(v.tensor_add(self.B2bc[:, :], self.B2bc[:, :],
                                      self.xtraF[:, 0:L]), "sDVE")
        t_b2bc = self.tokd("sDVE")

        gs_ld = {}
        abc_ld = {}
        t_sb2 = [None]
        for tb in range(NTB):
            @block.sync
            def _(sy, tb=tb):
                self.W(sy, self.t_agsp, self.tokd("sDVE"))
                gs_ld[tb] = self.inc(sy.dma_start(
                    self.gsbc[:, :],
                    self.scrAG[8:9, tb * TB:(tb + 1) * TB].to_broadcast([128, TB])),
                    "sABC", 16)

            for e in range(9):
                njc = NJ[e] if e < 8 else 8
                groups = [(g0, min(g0 + JG, njc)) for g0 in range(0, njc, JG)]
                if e < 8:
                    @block.sync
                    def _(sy, tb=tb, e=e):
                        self.W(sy, self.tokd("sDVE"))
                        abc_ld[(tb, e)] = self.inc(sy.dma_start(
                            self.abc[:, :],
                            self.scrAG[e:e + 1, tb * TB:(tb + 1) * TB]
                            .to_broadcast([128, TB])), "sABC", 16)

                    @block.vector
                    def _(v, tb=tb, e=e):
                        self.W(v, abc_ld[(tb, e)], gs_ld[tb])
                        self.inc(v.scalar_tensor_tensor(self.bbc[:, :], self.abc[:, :], -1.0,
                                                        self.gsbc[:, :], op0=ALU.mult,
                                                        op1=ALU.add), "sDVE")
                    t_ab = self.tokd("sDVE")
                else:
                    @block.sync
                    def _(sy, tb=tb):
                        self.W(sy, self.tokd("sDVE"))
                        t_sb2[0] = self.inc(sy.dma_start(
                            self.xtraF[:, 0:L],
                            self.shared_b2[None, :].to_broadcast([128, L])), "sBC", 16)
                    t_ab = None

                for (g0, g1) in groups:
                    nj = g1 - g0
                    assert nj == JG
                    w1s = self.ew1[e] if e < 8 else self.shared_w1
                    w2s = self.ew2[e] if e < 8 else self.shared_w2
                    t_w = {}
                    T_base = self.cnt["sPE"]
                    nb = NTI if pend[0] is not None else 0

                    def pred_pe(j, g0=g0, T_base=T_base, nb=nb):
                        d = j - g0 + 1
                        return ("sPE", T_base + d + (nb if d > 2 else 0))

                    def w1load(j, tb=tb, e=e, g0=g0, pred_pe=pred_pe, w1s=w1s,
                               t_w=t_w):
                        @block.gpsimd
                        def _(g, j=j):
                            if j - 2 >= g0:
                                self.W(g, pred_pe(j - 2))
                            elif w1gate[j % 2]:
                                self.W(g, w1gate[j % 2])
                            if tb == 0 and e == 0 and j == 0:
                                self.W(g, self.t_E_gcons)
                            t_w[j] = self.inc(g.dma_start(
                                self.w1sl[j % 2][:, :, :], w1s[j]),
                                "sW1%d" % (j % 2), 16)

                    def pmm(j, pred_act, tb=tb, e=e, g0=g0, pred_pe=pred_pe,
                            t_w=t_w):
                        @block.tensor
                        def _(t, j=j):
                            self.W(t, t_w[j])
                            if j - 2 >= g0:
                                self.W(t, pred_act(j - 2))
                            else:
                                self.W(t, p_act[j % 2])
                            bk = (j % 2) * 2
                            last = None
                            for h in range(2):
                                for Lc in range(NL):
                                    last = t.matmul(
                                        pb[bk + h][:, :], self.w1sl[j % 2][:, Lc, :],
                                        self.GT[:, Lc,
                                                tb * TB + h * 512:tb * TB + (h + 1) * 512],
                                        start=(Lc == 0), stop=(Lc == NL - 1))
                            wcons[(tb, e, j)] = self.inc(last, "sPE")
                            assert wcons[(tb, e, j)] == pred_pe(j), \
                                (wcons[(tb, e, j)], pred_pe(j))
                            w1gate[j % 2] = wcons[(tb, e, j)]

                    w1load(g0)
                    w1load(g0 + 1)
                    pmm(g0, None)
                    pmm(g0 + 1, None)
                    if pend[0] is not None:
                        pend[0]()
                        pend[0] = None

                    half = gp[0] % 2
                    woff = half * JG
                    A_base = self.cnt["sACT"]
                    V_base = self.cnt["sDVE"]

                    def pred_act(j, g0=g0, A_base=A_base):
                        return ("sACT", A_base + (j - g0 + 1))

                    def pred_dve(j, g0=g0, V_base=V_base):
                        return ("sDVE", V_base + 2 * (j - g0 + 1))

                    def w2load(j, jj, tb=tb, e=e, half=half, woff=woff, w2s=w2s):
                        @block.gpsimd
                        def _(g, j=j, jj=jj):
                            if jj == 0:
                                self.W(g, w2free[half])
                            if tb == 0 and e == 0 and j == 0:
                                self.W(g, self.t_g2)
                            self.inc(g.dma_start(
                                self.w2grp[:, woff + jj, :],
                                w2s[j * 128:(j + 1) * 128, :]),
                                "sW2%d" % half, 16)

                    for j in range(g0 + 2, g1):
                        w1load(j)
                        w2load(j - 2, j - 2 - g0)
                        pmm(j, pred_act)
                    w2load(g1 - 2, g1 - 2 - g0)
                    w2load(g1 - 1, g1 - 1 - g0)

                    @block.scalar
                    def _(s, tb=tb, e=e, g0=g0, g1=g1, woff=woff,
                          pred_act=pred_act, pred_dve=pred_dve):
                        for j in range(g0, g1):
                            jj = j - g0
                            self.W(s, wcons[(tb, e, j)])
                            if e < 8:
                                if j - 2 >= g0:
                                    self.W(s, pred_dve(j - 2))
                                elif wdve[j % 2]:
                                    self.W(s, wdve[j % 2])
                            if tb == 0 and e == 0 and j == 0:
                                self.W(s, self.t_rT, self.t_agsp)
                            bk = (j % 2) * 2
                            bias = (self.b1T[:, self.b1off[e] + j:self.b1off[e] + j + 1]
                                    if e < 8 else self.sb1T[:, j:j + 1])
                            dst = (self.stl[j % 2][:, :] if e < 8
                                   else self.wtil[:, woff + jj, :])
                            tk = self.inc(s.activation(dst, self.pq[bk // 2][:, :],
                                                       AF.Gelu, bias=bias), "sACT")
                            assert tk == pred_act(j), (tk, pred_act(j))
                            p_act[j % 2] = tk

                    if e < 8:
                        for j in range(g0, g1):
                            jj = j - g0

                            @block.vector
                            def _(v, tb=tb, e=e, j=j, jj=jj, g0=g0, t_ab=t_ab,
                                  woff=woff, pred_act=pred_act, pred_dve=pred_dve):
                                self.W(v, pred_act(j))
                                if j == g0:
                                    self.W(v, t_ab, wdve[0], wdve[1])
                                else:
                                    self.W(v, pred_dve(j - 1))
                                if tb == 0 and e == 0 and j == 0:
                                    self.W(v, self.t_rT)
                                self.vf(v, v.tensor_mul(self.xtraB[:, :], self.abc[:, :],
                                                        self.stl[j % 2][:, :]))
                                tk = self.inc(
                                    v.scalar_tensor_tensor(
                                        self.wtil[:, woff + jj, :], self.bbc[:, :],
                                        self.rT[:, self.b1off[e] + j:self.b1off[e] + j + 1],
                                        self.xtraB[:, :], op0=ALU.mult, op1=ALU.add),
                                    "sDVE")
                                assert tk == pred_dve(j), (tk, pred_dve(j))
                                wdve[j % 2] = tk
                    t_wgrp = self.tokd("sDVE") if e < 8 else self.tokd("sACT")
                    t_w2grp = self.tokd("sW2%d" % half)

                    def emit_pass2(tb=tb, e=e, g0=g0, nj=nj, half=half, woff=woff,
                                   t_wgrp=t_wgrp, t_w2grp=t_w2grp):
                        for ti in range(NTI):
                            tt = tb * NTI + ti

                            @block.tensor
                            def _(t, ti=ti):
                                if ti == 0:
                                    self.W(t, t_wgrp, t_w2grp)
                                bk = 4 + (ti % 2) * 2
                                self.W(t, zb_ev[ti % 2])
                                last = None
                                for h in range(2):
                                    for jj in range(nj):
                                        last = t.matmul(
                                            pb[bk + h][:, :],
                                            self.wtil[:, woff + jj,
                                                      ti * 128:(ti + 1) * 128],
                                            self.w2grp[:, woff + jj,
                                                       h * 512:(h + 1) * 512],
                                            start=(jj == 0), stop=(jj == nj - 1))
                                zbt[0] = self.inc(last, "sPE")

                            if e < 8:
                                @block.vector
                                def _(v, ti=ti, tt=tt):
                                    self.W(v, zbt[0])
                                    if tb == 0 and e == 0 and g0 == 0 and ti == 0:
                                        self.W(v, t_b2bc)
                                    bk = 4 + (ti % 2) * 2
                                    z2 = self.z[:, tt, :].rearrange("p (a c) -> p a c",
                                                                    a=2)
                                    last = None
                                    for h in range(2):
                                        if e == 0 and g0 == 0:
                                            last = v.scalar_tensor_tensor(
                                                z2[:, h, :],
                                                self.B2bc[:, h * 512:(h + 1) * 512],
                                                self.rst[:, tt, 10:11], pb[bk + h][:, :],
                                                op0=ALU.mult, op1=ALU.add)
                                        else:
                                            last = v.tensor_add(z2[:, h, :], z2[:, h, :],
                                                                pb[bk + h][:, :])
                                    zb_ev[ti % 2] = self.inc(last, "sDVE")
                            else:
                                @block.scalar
                                def _(s, ti=ti):
                                    self.W(s, zbt[0], self._shev[ti % 2])
                                    bk = 4 + (ti % 2) * 2
                                    last = None
                                    for h in range(2):
                                        last = s.activation(
                                            self.stl[ti % 2][:, h * 512:(h + 1) * 512],
                                            pb[bk + h][:, :], AF.Copy, scale=0.1)
                                    self.inc(last, "sACT")
                                t_sh = self.tokd("sACT")

                                @block.vector
                                def _(v, ti=ti, tt=tt, t_sh=t_sh):
                                    self.W(v, t_sh, t_sb2[0])
                                    z2 = self.z[:, tt, :].rearrange("p (a c) -> p a c",
                                                                    a=2)
                                    x2 = self.xtraF[:, 0:L].rearrange("p (a c) -> p a c",
                                                                      a=2)
                                    last = None
                                    for h in range(2):
                                        last = v.tensor_add(
                                            z2[:, h, :], z2[:, h, :],
                                            self.stl[ti % 2][:, h * 512:(h + 1) * 512])
                                    self.vf(v, last)
                                    last = None
                                    for h in range(2):
                                        last = v.scalar_tensor_tensor(
                                            z2[:, h, :], x2[:, h, :], 0.1, z2[:, h, :],
                                            op0=ALU.mult, op1=ALU.add)
                                    zb_ev[ti % 2] = self.inc(last, "sDVE")
                                    self._shev[ti % 2] = zb_ev[ti % 2]
                        w2free[half] = zbt[0]
                        if e == 8:
                            wdve[0] = wdve[1] = self.tokd("sDVE")
                        gp[0] += 1
                    pend[0] = emit_pass2
        if pend[0] is not None:
            pend[0]()
            pend[0] = None
        self.t_F_z = self.tokd("sDVE")
        self.t_F_pe = self.tokd("sPE")
        self._dbg_mark("F")

    def _phaseG(self):
        block = self.block
        pb, pbbf = self.pb, self.pbbf
        st = self.st
        trev = {0: self.t_F_z, 1: self.t_F_z}
        zc_tok = {}
        ztt = {}
        P0 = self.cnt["sPE"]
        A0 = self.cnt["sACT"]

        @block.scalar
        def _(s, P0=P0):
            self.W(s, self.t_F_z)
            for tt in range(NT):
                if tt >= 1:
                    s.wait_ge(self.sem["sPE"], P0 + tt)
                self.inc(s.activation(self.zst[:, :], self.z[:, tt, :], AF.Copy), "sACT")
        for tt in range(NT):
            zc_tok[tt] = ("sACT", A0 + tt + 1)
        assert self.cnt["sACT"] == A0 + NT

        for tt in range(NT):
            @block.tensor
            def _(t, tt=tt):
                self.W(t, zc_tok[tt], trev[tt % 2])
                bank = pbbf[6 + tt % 2]
                last = None
                for Lc in range(NL):
                    last = t.transpose(bank[:, Lc * 128:(Lc + 1) * 128],
                                       self.zst[:, Lc * 128:(Lc + 1) * 128], self.identb[:, :])
                ztt[tt] = self.inc(last, "sPE")
                assert ztt[tt][1] == P0 + tt + 1

            @block.vector
            def _(v, tt=tt):
                self.W(v, ztt[tt])
                bank = pbbf[6 + tt % 2]
                trev[tt % 2] = self.inc(
                    v.tensor_copy(self.z16T[:, :, tt * 128:(tt + 1) * 128],
                                  bank[:, 0:NL * 128].rearrange("p (c n) -> p c n", c=NL)),
                    "sDVE")
        t_z16 = self.tokd("sDVE")
        t_zcast = zc_tok[NT - 1]

        t_gw = [None]

        @block.gpsimd
        def _(g):
            self.W(g, self.t_F_pe)
            for fo in range(4):
                for hh in range(2):
                    self.inc(g.dma_start(
                        self.corew[:, hh * 8:(hh + 1) * 8, fo * 512:(fo + 1) * 512],
                        self.core_w[:, fo * 512:(fo + 1) * 512]
                        .rearrange("(c p) n -> p c n", p=128)[:, hh * 8:(hh + 1) * 8, :]),
                        "sGW", 16)
            self.W(g, zc_tok[NL - 1])
            for hh in range(2):
                t_gw[0] = self.inc(g.dma_start(
                    self.upw[:, hh * 4:(hh + 1) * 4, :],
                    self.up_w.rearrange("(c p) n -> p c n", p=128)[:, hh * 4:(hh + 1) * 4, :]),
                    "sGW", 16)

        t_budld = [None]

        @block.sync
        def _(sy):
            self.W(sy, self.t_F_z, t_zcast)
            self.inc(sy.dma_start(self.B_ud[:, :],
                                  self.up_b[None, :].to_broadcast([128, D])), "sBUD", 16)
            t_budld[0] = self.inc(sy.dma_start(
                self.y2t[:, :], self.core_b[None, :].to_broadcast([128, D])), "sBUD", 16)

        @block.vector
        def _(v):
            self.W(v, t_budld[0])
            self.inc(v.tensor_add(self.B_ud[:, :], self.B_ud[:, :], self.y2t[:, :]), "sDVE")
        t_bud = self.tokd("sDVE")

        g2ld, x2ld, out_tok, out_st = {}, {}, {}, {}
        gy = {}
        y_ev = None
        V1 = self.cnt["sDVE"]
        A1 = self.cnt["sACT"]

        @block.scalar
        def _(s, V1=V1):
            for tt in range(NT):
                s.wait_ge(self.sem["sDVE"], V1 + 9 * tt + 1)
                self.inc(s.activation(self.sqbG[:, :], self.y2t[:, :], AF.Square,
                                      accum_out=st[:, 1:2]), "sACT")
                s.wait_ge(self.sem["sDVE"], V1 + 9 * tt + 7)
                self._ln_rstd(s)
        assert self.cnt["sACT"] == A1 + 3 * NT

        for tt in range(NT):
            @block.sync
            def _(sy, tt=tt):
                if tt == 0:
                    self.W(sy, self.t_g2, self.t_F_z, t_zcast)
                if tt >= 2:
                    self.W(sy, gy[tt - 2], out_st[tt - 2])
                g2ld[tt] = self.inc(sy.dma_start(
                    self.g2sl[tt % 2][:, :, :], self.scrG2[tt]),
                    "sGL%d" % (tt % 2), 16)
                if tt < 2:
                    x2ld[tt] = self.inc(sy.dma_start(
                        self.x2tG[tt % 2][:, :], self.scrX2[tt]),
                        "sXG%d" % (tt % 2), 16)

            if tt >= 2:
                @block.gpsimd
                def _(g, tt=tt):
                    self.W(g, gy[tt - 2], out_st[tt - 2])
                    x2ld[tt] = self.inc(g.dma_start(
                        self.x2tG[tt % 2][:, :], self.scrX2[tt]),
                        "sXB%d" % (tt % 2), 16)

            @block.tensor
            def _(t, tt=tt):
                self.W(t, y_ev)
                if tt == 0:
                    self.W(t, t_gw[0], t_z16, t_zcast)
                for db in range(4):
                    for Lc in range(NL):
                        t.matmul(pb[db][:, :],
                                 self.z16T[:, Lc, tt * 128:(tt + 1) * 128],
                                 self.upw[:, Lc, db * 512:(db + 1) * 512],
                                 start=(Lc == 0), stop=False)
                self.W(t, g2ld[tt])
                last = None
                for db in range(4):
                    for fc in range(ND):
                        last = t.matmul(pb[db][:, :],
                                        self.g2sl[tt % 2][:, fc, :],
                                        self.corew[:, fc, db * 512:(db + 1) * 512],
                                        start=False, stop=(fc == ND - 1))
                gy[tt] = self.inc(last, "sPE")

            @block.vector
            def _(v, tt=tt):
                self.W(v, gy[tt])
                if tt == 0:
                    self.W(v, t_bud)
                last = None
                for db in range(4):
                    last = v.scalar_tensor_tensor(
                        self.y2t[:, db * 512:(db + 1) * 512], pb[db][:, :], 1.0,
                        self.B_ud[:, db * 512:(db + 1) * 512],
                        op0=ALU.mult, op1=ALU.add, accum_out=st[:, 8 + db:9 + db])
                self.vf(v, last)
            y_ev = self.tokd("sDVE")
            assert y_ev[1] == V1 + 9 * tt + 1
            t_sq = ("sACT", A1 + 3 * tt + 1)

            @block.vector
            def _(v, tt=tt, t_sq=t_sq):
                st_ = st
                self.vf(v, v.tensor_add(st_[:, 12:13], st_[:, 8:9], st_[:, 9:10]))
                self.vf(v, v.tensor_add(st_[:, 13:14], st_[:, 10:11], st_[:, 11:12]))
                self.vf(v, v.tensor_add(st_[:, 0:1], st_[:, 12:13], st_[:, 13:14]))
                self.vf(v, v.tensor_scalar(st_[:, 2:3], st_[:, 0:1], -SQD, None, op0=ALU.mult))
                self.vf(v, v.tensor_mul(st_[:, 3:4], st_[:, 2:3], st_[:, 2:3]))
                self.W(v, t_sq)
                self.vf(v, v.scalar_tensor_tensor(st_[:, 4:5], st_[:, 1:2], SQD, st_[:, 3:4],
                                                  op0=ALU.mult, op1=ALU.subtract))
            tv = self.tokd("sDVE")
            assert tv[1] == V1 + 9 * tt + 7
            t_r = ("sACT", A1 + 3 * tt + 3)

            @block.vector
            def _(v, tt=tt, t_r=t_r):
                self.W(v, t_r, x2ld[tt])
                self.vf(v, v.tensor_scalar(self.outt[:, :], self.y2t[:, :],
                                           st[:, 2:3], st[:, 6:7],
                                           op0=ALU.add, op1=ALU.mult))
                out_tok[tt] = self.inc(
                    v.tensor_add(self.x2tG[tt % 2][:, :], self.x2tG[tt % 2][:, :],
                                 self.outt[:, :]), "sDVE")

            @block.sync
            def _(sy, tt=tt):
                self.W(sy, out_tok[tt])
                out_st[tt] = self.inc(
                    sy.dma_start(self.out[tt * 128:(tt + 1) * 128, :],
                                 self.x2tG[tt % 2][:, :]), "sOU%d" % (tt % 2), 16)
        self._dbg_mark("G")

    def _dbg_mark(self, phase):
        for (name, ph, view_fn) in self.dbg:
            if ph == phase:
                self.dbg_done.append((name, view_fn, dict(self.cnt)))

    def _dbg_flush(self):
        block, nc = self.block, self.nc

        @block.sync
        def _(sy):
            for (name, view_fn, waits) in self.dbg_done:
                for snm, vv in waits.items():
                    if vv > 0 and snm != "sOUT":
                        sy.wait_ge(self.sem[snm], vv)
                ap = view_fn(self)
                dst = nc.dram_tensor("dbg_" + name, list(ap.shape), ap.dtype,
                                     kind="ExternalOutput").ap()
                self.inc(sy.dma_start(dst, ap), "sOUT", 16)

    def _final_wait(self):
        block = self.block

        @block.sync
        def _(sy):
            for nm in ["sOUT", "sOU0", "sOU1"]:
                if self.cnt[nm] > 0:
                    sy.wait_ge(self.sem[nm], self.cnt[nm])


def _digest_w1(w):
    Lr, H = w.shape
    nj = H // 128
    return np.ascontiguousarray(
        np.asarray(w, np.float32).reshape(Lr // 128, 128, nj, 128)
        .transpose(2, 1, 0, 3))


def _core_inputs(inputs):
    m = {
        "Wq": inputs["L0_attn_Wq"], "Wk": inputs["L0_attn_Wk"],
        "Wv": inputs["L0_attn_Wv"], "Wo": inputs["L0_attn_Wo"],
        "down_w": inputs["L0_ffn_down_w"], "down_b": inputs["L0_ffn_down_b"],
        "up_w": inputs["L0_ffn_up_w"], "up_b": inputs["L0_ffn_up_b"],
        "router_w": inputs["L0_ffn_router_w"], "router_b": inputs["L0_ffn_router_b"],
        "shared_w1": _digest_w1(inputs["L0_ffn_shared_w1"]),
        "shared_b1": inputs["L0_ffn_shared_b1"],
        "shared_w2": inputs["L0_ffn_shared_w2"], "shared_b2": inputs["L0_ffn_shared_b2"],
        "core_w": inputs["L0_ffn_core_w"], "core_b": inputs["L0_ffn_core_b"],
    }
    for e in range(8):
        m[f"e{e}_w1"] = _digest_w1(inputs[f"L0_ffn_e{e}_w1"])
        m[f"e{e}_b1"] = inputs[f"L0_ffn_e{e}_b1"]
        m[f"e{e}_w2"] = inputs[f"L0_ffn_e{e}_w2"]
        m[f"e{e}_b2"] = inputs[f"L0_ffn_e{e}_b2"]
    m["ident"] = np.eye(128, dtype=np.float32)
    cost = np.array([1024 * h + h * 1024 for h in HID], np.float32)
    m["negcost"] = (-COST_LAMBDA * cost)[None, :]
    return {k: np.ascontiguousarray(np.asarray(v, dtype=np.float32)) for k, v in m.items()}


def kernel(**inputs):
    x = np.ascontiguousarray(np.asarray(inputs["L0_attn_x"], dtype=np.float32))
    mk = MK()
    base = _core_inputs(inputs)
    in_maps = []
    for c in range(8):
        mm = dict(base)
        mm["x"] = np.ascontiguousarray(x[c])
        in_maps.append(mm)
    res = run_bass_kernel_spmd(mk.nc, in_maps, core_ids=list(range(8)))
    return np.stack([r["out"] for r in res.results], axis=0)


# version 11
# speedup vs baseline: 1.3727x; 1.0578x over previous
import numpy as np
import concourse.bass as bass
import concourse.mybir as mybir
from concourse.bass_utils import run_bass_kernel_spmd

F32 = mybir.dt.float32
BF16 = mybir.dt.bfloat16
U8 = mybir.dt.uint8
AF = mybir.ActivationFunctionType
ALU = mybir.AluOpType
AX = mybir.AxisListType

T = 2048
D = 2048
L = 1024
NT = T // 128
ND = D // 128
NL = L // 128
HID = [1024, 2048, 3072, 4096, 1024, 2048, 3072, 4096]
NJ = [h // 128 for h in HID]
COST_LAMBDA = 1e-07
EPS = 1e-5
TB = 1024
NTB = T // TB
NTI = TB // 128
JG = 8
NEG_BIG = -1e30
SQD = 1.0 / D

SEM_NAMES = [
    "sCONST", "sX0", "sX1", "sSPQ", "sKL", "sQS0", "sQS1", "sASP",
    "sAL0", "sAL1", "sXD", "sX2S0", "sX2S1", "sBR", "sG2S", "sAGS",
    "sBC", "sABC", "sBUD", "sGL0", "sGL1", "sXG0", "sXG1", "sXB0", "sXB1", "sOUT",
    "sWC", "sWS0", "sWS1", "sWO", "sDW", "sW10", "sW11", "sW12", "sW20", "sW21", "sGW",
    "sOU0", "sOU1",
    "sPE", "sACT", "sDVE",
]


class MK:
    def __init__(self, phase_limit=99, dbg=None):
        self.phase_limit = phase_limit
        self.dbg = dbg or []
        self.dbg_done = []
        nc = bass.Bass("TRN2", target_bir_lowering=False)
        self.nc = nc
        self._dram()
        self._sbuf()
        self._build()

    def _dram(self):
        nc = self.nc

        def din(name, shape):
            return nc.dram_tensor(name, shape, F32, kind="ExternalInput").ap()

        self.x = din("x", [T, D])
        self.Wq = din("Wq", [D, D]); self.Wk = din("Wk", [D, D])
        self.Wv = din("Wv", [D, D]); self.Wo = din("Wo", [D, D])
        self.down_w = din("down_w", [D, L]); self.down_b = din("down_b", [L])
        self.up_w = din("up_w", [L, D]); self.up_b = din("up_b", [D])
        self.router_w = din("router_w", [D, 8]); self.router_b = din("router_b", [8])
        self.shared_w1 = din("shared_w1", [L // 128, 128, NL, 128])
        self.shared_b1 = din("shared_b1", [L])
        self.shared_w2 = din("shared_w2", [L, L]); self.shared_b2 = din("shared_b2", [L])
        self.core_w = din("core_w", [D, D]); self.core_b = din("core_b", [D])
        self.ew1 = [din(f"e{i}_w1", [NJ[i], 128, NL, 128]) for i in range(8)]
        self.eb1 = [din(f"e{i}_b1", [HID[i]]) for i in range(8)]
        self.ew2 = [din(f"e{i}_w2", [HID[i], L]) for i in range(8)]
        self.eb2 = [din(f"e{i}_b2", [L]) for i in range(8)]
        self.ident = din("ident", [128, 128])
        self.negcost = din("negcost", [1, 8])
        self.out = nc.dram_tensor("out", [T, D], F32, kind="ExternalOutput").ap()

        def scr(name, shape, dt):
            return nc.dram_tensor(name, shape, dt, kind="Internal").ap()

        self.scrQ = scr("scrQ", [4, NT, 128, 512], BF16)
        self.scrK = scr("scrK", [4, NT, 128, 512], BF16)
        self.scrATT = scr("scrATT", [NT, 128, ND, 128], BF16)
        self.scrX2 = scr("scrX2", [NT, 128, D], F32)
        self.scrG2 = scr("scrG2", [NT, 128, ND, 128], BF16)
        self.scrAG = scr("scrAG", [9, T], BF16)

    def _sbuf(self):
        nc = self.nc

        def arena(name, nbytes):
            return nc.alloc_sbuf_tensor(name, [128, nbytes], U8)

        def view(ar, off, shape, dt):
            esz = {F32: 4, BF16: 2}[dt]
            n = 1
            for s in shape[1:]:
                n *= s
            assert off + n * esz <= ar.shape[1], (off, n * esz)
            ap = ar[:, off:off + n * esz].bitcast(dt)
            if len(shape) == 3:
                ap = ap.rearrange("p (a c) -> p a c", a=shape[1])
            return ap

        arA = arena("arA", 65536)
        arB = arena("arB", 65536)
        arC = arena("arC", 53504)
        self.view = view

        self.ln1T = view(arA, 0, [128, ND, T], BF16)
        self.Kar = view(arA, 0, [128, NT, D], BF16)
        self.ln2T = view(arA, 0, [128, ND, T], BF16)
        self.wtil = view(arA, 0, [128, 2 * JG, TB], BF16)
        self.w2grp = view(arA, 32768, [128, 2 * JG, L], BF16)
        self.corew = view(arA, 0, [128, ND, D], BF16)
        self.VT = view(arB, 0, [128, ND, T], BF16)
        self.Wosb = view(arB, 0, [128, ND, D], BF16)
        self.z = view(arB, 0, [128, NT, L], F32)
        self.upw = view(arB, 0, [128, NL, D], BF16)
        self.y2t = view(arB, 32768, [128, D], F32)
        self.x2tG = [view(arB, 40960, [128, D], F32), view(arB, 49152, [128, D], F32)]
        self.outt = view(arB, 57344, [128, D], F32)
        self.xtA = [view(arC, 32768, [128, D], F32), view(arC, 32768, [128, D], F32)]
        self.lnA = [view(arC, 40960, [128, D], BF16), view(arC, 45056, [128, D], BF16)]
        self.wslab = [view(arC, 0, [128, ND, 512], BF16), view(arC, 16384, [128, ND, 512], BF16)]
        self.stage = view(arB, 49152, [128, NT, 512], BF16)
        self.stageK = view(arB, 0, [128, NT, 512], BF16)
        self.qslab = [view(arC, 0, [128, NT, 512], BF16), view(arC, 16384, [128, NT, 512], BF16)]
        self.Srow = view(arC, 32768, [128, D], F32)
        self.Ptile = view(arC, 40960, [128, D], BF16)
        self.ATcol = view(arC, 45056, [128, ND, 128], BF16)
        self.attst = view(arC, 49152, [128, D], BF16)
        self.attsl = [view(arC, 0, [128, ND, 128], BF16), view(arC, 4096, [128, ND, 128], BF16)]
        self.xtD = view(arC, 8192, [128, D], F32)
        self.ytD = view(arC, 16384, [128, D], F32)
        self.x2tD = [view(arC, 24576, [128, D], F32), view(arC, 32768, [128, D], F32)]
        self.lnD = [view(arC, 40960, [128, D], BF16), view(arC, 45056, [128, D], BF16)]
        self.GT = view(arC, 0, [128, NL, T], BF16)
        self.dwsl = view(arC, 32768, [128, ND, 512], BF16)
        self.g2st = view(arC, 49152, [128, D], BF16)
        self.agT9 = arC[0:9, 32768:32768 + T * 4].bitcast(F32)
        self.stl = [view(arC, 32768, [128, TB], BF16), view(arC, 34816, [128, TB], BF16)]
        self.xtraB = view(arC, 36864, [128, TB], BF16)
        self.w1sl = [view(arC, 40960, [128, NL, 128], BF16),
                     view(arC, 43008, [128, NL, 128], BF16),
                     view(arC, 38912, [128, NL, 128], BF16)]
        self.xtraF = view(arC, 45056, [128, TB], F32)
        self.B2bc = view(arC, 49152, [128, L], F32)
        self.z16T = view(arC, 0, [128, NL, T], BF16)
        self.g2sl = [view(arC, 32768, [128, ND, 128], BF16), view(arC, 36864, [128, ND, 128], BF16)]
        self.zst = view(arC, 40960, [128, L], BF16)
        self.sqbG = view(arC, 40960, [128, D], BF16)
        self.B_ud = view(arC, 45056, [128, D], F32)

        self.identf = nc.alloc_sbuf_tensor("identf", [128, 128], F32)
        self.identb = nc.alloc_sbuf_tensor("identb", [128, 128], BF16)
        self.st = nc.alloc_sbuf_tensor("stats", [128, 32], F32)
        self.RW = nc.alloc_sbuf_tensor("rw3", [128, 16, 14, 8], F32)
        self.rst = nc.alloc_sbuf_tensor("rst", [128, 16, 12], F32)
        self.gsbc = nc.alloc_sbuf_tensor("gsbc", [128, TB], BF16)
        self.abc = nc.alloc_sbuf_tensor("abc", [128, TB], BF16)
        self.bbc = nc.alloc_sbuf_tensor("bbc", [128, TB], BF16)
        self.dwnbT = nc.alloc_sbuf_tensor("dwnbT", [128, 8], F32)
        self.sb1T = nc.alloc_sbuf_tensor("sb1T", [128, 8], F32)
        self.b1T = nc.alloc_sbuf_tensor("b1T", [128, 160], F32)
        self.rT = nc.alloc_sbuf_tensor("rT", [128, 160], F32)
        self.brows = nc.alloc_sbuf_tensor("brows", [128, 128], F32)
        self.rbbc = nc.alloc_sbuf_tensor("rbbc", [128, 8], F32)
        self.rbtmp = nc.alloc_sbuf_tensor("rbtmp", [128, 8], F32)
        self.rwsb = nc.alloc_sbuf_tensor("rwsb", [128, ND, 8], BF16)
        self.rzc = nc.alloc_sbuf_tensor("rzc", [128, 4], F32)
        self.constv = nc.alloc_sbuf_tensor("constv", [128, 8], F32)
        self.CONSTS = [EPS, -0.5, 0.125, 0.1, 1.0, 0.0]
        for i, cval in enumerate(self.CONSTS):
            nc.const_aps.register(F32, cval, self.constv[:, i:i + 1])
        assert nc.sbuf_bytes_remaining >= 0, nc.sbuf_bytes_remaining

    def _build(self):
        from contextlib import ExitStack
        nc = self.nc
        with ExitStack() as ctx:
            self.pq = [ctx.enter_context(nc.psum_tensor(f"pq{i}", [128, 1024], F32))
                       for i in range(4)]
            self.pb = [self.pq[i // 2][:, (i % 2) * 512:(i % 2 + 1) * 512]
                       for i in range(8)]
            self.pbbf = [self.pb[i].bitcast(BF16) for i in range(8)]
            self.sem = {}
            for nm in SEM_NAMES:
                self.sem[nm] = ctx.enter_context(nc.semaphore(nm))
            self.cnt = {nm: 0 for nm in SEM_NAMES}
            with nc.Block() as block:
                self.block = block
                self._program()

    def inc(self, inst, nm, n=1):
        self.cnt[nm] += n
        inst.then_inc(self.sem[nm], n)
        return (nm, self.cnt[nm])

    def W(self, eng, *toks):
        best = {}
        for tk in toks:
            if tk is not None and tk[1] > 0:
                if best.get(tk[0], 0) < tk[1]:
                    best[tk[0]] = tk[1]
        for nm, v in best.items():
            eng.wait_ge(self.sem[nm], v)

    def tokd(self, nm):
        return (nm, self.cnt[nm])

    def vf(self, v, inst):
        tk = self.inc(inst, "sDVE")
        v.wait_ge(self.sem["sDVE"], tk[1])

    def af(self, s, inst):
        tk = self.inc(inst, "sACT")
        s.wait_ge(self.sem["sACT"], tk[1])

    def _program(self):
        self._setup()
        pl = self.phase_limit
        if pl >= 1:
            self._phaseA()
        if pl >= 2:
            self._phaseB()
        if pl >= 3:
            self._phaseC()
        if pl >= 4:
            self._phaseD()
        if pl >= 5:
            self._phaseE()
        if pl >= 6:
            self._phaseF()
        if pl >= 7:
            self._phaseG()
        self._dbg_flush()
        self._final_wait()

    def _setup(self):
        block = self.block

        @block.sync
        def _(sy):
            self.inc(sy.dma_start(self.identf[:, :], self.ident), "sCONST", 16)
            self.inc(sy.dma_start(self.rbbc[:, :],
                                  self.router_b[None, :].to_broadcast([128, 8])), "sCONST", 16)
            self.t_const = self.inc(sy.dma_start(
                self.rbtmp[:, :], self.negcost.to_broadcast([128, 8])), "sCONST", 16)

        @block.gpsimd
        def _(g):
            self.inc(g.dma_start(self.identb[:, :], self.ident), "sWC", 16)
            self.t_wc = self.inc(g.dma_start(
                self.rwsb[:, :, :],
                self.router_w.rearrange("(c p) e -> p c e", p=128)), "sWC", 16)

        @block.vector
        def _(v):
            for i, cval in enumerate(self.CONSTS):
                v.memset(self.constv[:, i:i + 1], cval)
            self.W(v, self.t_const)
            self.t_rbbc = self.inc(
                v.tensor_add(self.rbbc[:, :], self.rbbc[:, :], self.rbtmp[:, :]), "sDVE")

    def _ln_stats(self, v, xsrc, sqdump):
        st = self.st
        v.reduce_sum(st[:, 0:1], xsrc, axis=AX.X)
        self.vf(v, v.scalar_tensor_tensor(sqdump, xsrc, 1.0, xsrc,
                                          op0=ALU.mult, op1=ALU.mult, accum_out=st[:, 1:2]))
        self.vf(v, v.tensor_scalar(st[:, 2:3], st[:, 0:1], -SQD, None, op0=ALU.mult))
        self.vf(v, v.tensor_mul(st[:, 3:4], st[:, 2:3], st[:, 2:3]))
        self.vf(v, v.scalar_tensor_tensor(st[:, 4:5], st[:, 1:2], SQD, st[:, 3:4],
                                          op0=ALU.mult, op1=ALU.subtract))
        return self.tokd("sDVE")

    def _ln_rstd(self, s):
        st = self.st
        self.af(s, s.activation(st[:, 5:6], st[:, 4:5], AF.Ln, bias=EPS))
        self.af(s, s.activation(st[:, 6:7], st[:, 5:6], AF.Exp, scale=-0.5))

    def _transp16(self, src_bf, dst3, tt, gate, state):
        block = self.block
        pbbf = self.pbbf
        nch = dst3.shape[1]
        per = nch // 2

        for half in range(2):
            @block.tensor
            def _(t, half=half):
                self.W(t, gate, state["trev"][half])
                if not getattr(self, "_identb_ok", False):
                    self.W(t, self.t_wc)
                    self._identb_ok = True
                bank = pbbf[6 + half]
                last = None
                for i in range(per):
                    fc = half * per + i
                    last = t.transpose(bank[:, i * 128:(i + 1) * 128],
                                       src_bf[:, fc * 128:(fc + 1) * 128], self.identb[:, :])
                state["ttok"][half] = self.inc(last, "sPE")

            @block.vector
            def _(v, half=half):
                self.W(v, state["ttok"][half])
                bank = pbbf[6 + half]
                dst = dst3[:, half * per:(half + 1) * per, tt * 128:(tt + 1) * 128]
                state["trev"][half] = self.inc(
                    v.tensor_copy(dst, bank[:, 0:per * 128].rearrange(
                        "p (c n) -> p c n", c=per)), "sDVE")
        return state["ttok"][1]

    def _phaseA(self):
        block = self.block
        self.lnT_tok = {}
        stA = {"ttok": {0: None, 1: None}, "trev": {0: None, 1: None}}
        apply_tok = {}
        tt_pe = {}
        xload = {}
        wtokA = {}

        @block.gpsimd
        def _(g):
            for k in range(2):
                src = [self.Wq, self.Wk][k][:, 0:512]
                for h in range(2):
                    wtokA[k] = self.inc(g.dma_start(
                        self.wslab[k][:, h * 8:(h + 1) * 8, :],
                        src.rearrange("(c p) n -> p c n", p=128)[:, h * 8:(h + 1) * 8, :]),
                        "sWS%d" % k, 16)

        self.Bst = {"slab_pe": {}, "ev_bank": {i: None for i in range(4)}, "pbt": [None]}
        V0 = self.cnt["sDVE"]
        A0 = self.cnt["sACT"]

        @block.scalar
        def _(s, V0=V0):
            for tt in range(NT):
                s.wait_ge(self.sem["sDVE"], V0 + 9 * tt + 4)
                self._ln_rstd(s)
        assert self.cnt["sACT"] == A0 + 2 * NT

        for tt in range(NT):
            @block.sync
            def _(sy, tt=tt):
                if tt >= 1:
                    self.W(sy, apply_tok[tt - 1])
                xload[tt] = self.inc(
                    sy.dma_start(self.xtA[0][:, :], self.x[tt * 128:(tt + 1) * 128, :]),
                    "sX0", 16)

            @block.vector
            def _(v, tt=tt):
                self.W(v, xload[tt])
                if tt >= 2:
                    self.W(v, tt_pe[tt - 2])
                self._ln_stats(v, self.xtA[0][:, :], self.lnA[tt % 2][:, :])
            tv = self.tokd("sDVE")
            assert tv[1] == V0 + 9 * tt + 4
            t_rs = ("sACT", A0 + 2 * (tt + 1))

            @block.vector
            def _(v, tt=tt, t_rs=t_rs):
                self.W(v, t_rs)
                apply_tok[tt] = self.inc(
                    v.tensor_scalar(self.lnA[tt % 2][:, :], self.xtA[0][:, :],
                                    self.st[:, 2:3], self.st[:, 6:7],
                                    op0=ALU.add, op1=ALU.mult), "sDVE")

            tt_pe[tt] = self._transp16(self.lnA[tt % 2], self.ln1T, tt, apply_tok[tt], stA)
            self.lnT_tok[tt] = self.tokd("sDVE")

            for k in range(2):
                bk = tt % 2 if k == 0 else 2 + tt % 2

                @block.tensor
                def _(t, tt=tt, k=k, bk=bk):
                    if tt == 0:
                        self.W(t, wtokA[k])
                    self.W(t, self.lnT_tok[tt], self.Bst["ev_bank"][bk])
                    last = None
                    for fc in range(ND):
                        last = t.matmul(self.pb[bk][:, :],
                                        self.ln1T[:, fc, tt * 128:(tt + 1) * 128],
                                        self.wslab[k][:, fc, :],
                                        start=(fc == 0), stop=(fc == ND - 1))
                    self.Bst["pbt"][0] = self.inc(last, "sPE")
                    if tt == NT - 1:
                        self.Bst["slab_pe"][k] = self.Bst["pbt"][0]

                @block.vector
                def _(v, tt=tt, k=k, bk=bk):
                    self.W(v, self.Bst["pbt"][0])
                    dst = (self.stage if k == 0 else self.stageK)[:, tt, :]
                    self.Bst["ev_bank"][bk] = self.inc(
                        v.tensor_copy(dst, self.pb[bk][:, :]), "sDVE")

        @block.sync
        def _(sy):
            self.W(sy, self.tokd("sDVE"))
            self.inc(sy.dma_start(self.scrQ[0].rearrange("t p n -> p t n"),
                                  self.stage[:, :, :]), "sSPQ", 16)
            self.inc(sy.dma_start(self.scrK[0].rearrange("t p n -> p t n"),
                                  self.stageK[:, :, :]), "sSPQ", 16)
        self.Bst["stage_spill"] = self.tokd("sSPQ")
        self.t_ln1T = self.tokd("sDVE")
        self.t_A_pe = self.tokd("sPE")
        self._dbg_mark("A")

    def _phaseB(self):
        block = self.block
        pb = self.pb
        WQKV = [self.Wq, self.Wk, self.Wv]
        slab_pe = self.Bst["slab_pe"]
        ev_bank = self.Bst["ev_bank"]
        stage_spill = self.Bst["stage_spill"]
        pbt = self.Bst["pbt"]

        for fo in range(4):
            for wi in range(3):
                if fo == 0 and wi < 2:
                    continue
                k = fo * 3 + wi
                wtok = [None]

                @block.gpsimd
                def _(g, fo=fo, wi=wi, k=k):
                    if k >= 2:
                        self.W(g, slab_pe[k - 2])
                    src = WQKV[wi][:, fo * 512:(fo + 1) * 512]
                    snm = "sWS%d" % (k % 2)
                    for h in range(2):
                        wtok[0] = self.inc(g.dma_start(
                            self.wslab[k % 2][:, h * 8:(h + 1) * 8, :],
                            src.rearrange("(c p) n -> p c n", p=128)[:, h * 8:(h + 1) * 8, :]),
                            snm, 16)

                if wi < 2:
                    for tt in range(NT):
                        @block.tensor
                        def _(t, k=k, tt=tt):
                            if tt == 0:
                                self.W(t, wtok[0])
                            if k < 2:
                                self.W(t, self.lnT_tok[tt])
                            bk = tt % 2
                            self.W(t, ev_bank[bk])
                            last = None
                            for fc in range(ND):
                                last = t.matmul(pb[bk][:, :],
                                                self.ln1T[:, fc, tt * 128:(tt + 1) * 128],
                                                self.wslab[k % 2][:, fc, :],
                                                start=(fc == 0), stop=(fc == ND - 1))
                            pbt[0] = self.inc(last, "sPE")
                            if tt == NT - 1:
                                slab_pe[k] = pbt[0]

                        @block.vector
                        def _(v, tt=tt):
                            self.W(v, pbt[0])
                            if tt == 0:
                                self.W(v, stage_spill)
                            ev_bank[tt % 2] = self.inc(
                                v.tensor_copy(self.stage[:, tt, :], pb[tt % 2][:, :]), "sDVE")

                    @block.sync
                    def _(sy, fo=fo, wi=wi):
                        self.W(sy, self.tokd("sDVE"))
                        dst = (self.scrQ if wi == 0 else self.scrK)[fo]
                        self.inc(sy.dma_start(dst.rearrange("t p n -> p t n"),
                                              self.stage[:, :, :]), "sSPQ", 16)
                    stage_spill = self.tokd("sSPQ")
                else:
                    for i in range(16):
                        fc4 = i // 4

                        @block.tensor
                        def _(t, k=k, fo=fo, fc4=fc4, i=i):
                            tb4 = i % 4
                            if i == 0:
                                self.W(t, wtok[0])
                            if k < 3:
                                self.W(t, self.lnT_tok[4 * tb4 + 3])
                            bk = 2 + i % 2
                            self.W(t, ev_bank[bk])
                            last = None
                            for fc in range(ND):
                                last = t.matmul(pb[bk][:, :],
                                                self.wslab[k % 2][:, fc, fc4 * 128:(fc4 + 1) * 128],
                                                self.ln1T[:, fc, tb4 * 512:(tb4 + 1) * 512],
                                                start=(fc == 0), stop=(fc == ND - 1))
                            pbt[0] = self.inc(last, "sPE")
                            if i == 15:
                                slab_pe[k] = pbt[0]

                        @block.vector
                        def _(v, fo=fo, fc4=fc4, i=i):
                            tb4 = i % 4
                            self.W(v, pbt[0])
                            if i == 0 and fo in (0, 3):
                                self.W(v, stage_spill)
                            dst = self.VT[:, fo * 4 + fc4, tb4 * 512:(tb4 + 1) * 512]
                            ev_bank[2 + i % 2] = self.inc(
                                v.tensor_copy(dst, pb[2 + i % 2][:, :]), "sDVE")
        self.t_B_pe = self.tokd("sPE")
        self.t_B_spill = self.tokd("sSPQ")
        self.t_B_vt = self.tokd("sDVE")
        self._dbg_mark("B")

    def _phaseC(self):
        block = self.block
        pb = self.pb
        st = self.st
        t_Kload = [None]

        @block.sync
        def _(sy):
            self.W(sy, self.t_B_pe, self.t_B_spill)
            for fo in range(4):
                for hh in range(2):
                    t_Kload[0] = self.inc(sy.dma_start(
                        self.Kar[:, hh * 8:(hh + 1) * 8, fo * 512:(fo + 1) * 512],
                        self.scrK[fo][hh * 8:(hh + 1) * 8].rearrange("t p n -> p t n")),
                        "sKL", 16)

        qload, qcons = {}, {}
        s_evac = {}
        t_exp = {}
        t_rz = {}
        av_ev = {0: None, 1: None}
        att_sp = None
        stC = {"ttok": {0: None, 1: None}, "trev": {0: None, 1: None}}
        Stok = {}
        avtok = [None]

        def emit_S(ec):
            fo = ec // 4
            if ec % 4 == 0:
                @block.sync
                def _(sy, fo=fo):
                    if fo >= 2:
                        self.W(sy, qcons[fo - 2])
                    for hh in range(2):
                        qload[fo] = self.inc(sy.dma_start(
                            self.qslab[fo % 2][:, hh * 8:(hh + 1) * 8, :],
                            self.scrQ[fo][hh * 8:(hh + 1) * 8].rearrange("t p n -> p t n")),
                            "sQS%d" % (fo % 2), 16)

            @block.tensor
            def _(t, ec=ec, fo=fo):
                self.W(t, qload[fo], t_Kload[0], s_evac.get(ec - 1))
                if ec == 0:
                    self.W(t, self.t_B_vt)
                esub = ec % 4
                last = None
                for fb in range(4):
                    for tt in range(NT):
                        last = t.matmul(pb[fb][:, :],
                                        self.qslab[fo % 2][:, tt, esub * 128:(esub + 1) * 128],
                                        self.Kar[:, tt, fb * 512:(fb + 1) * 512],
                                        start=(tt == 0), stop=(tt == NT - 1))
                Stok[ec] = self.inc(last, "sPE")
                if esub == 3:
                    qcons[fo] = Stok[ec]

        def emit_TAV_mm(ec):
            self._transp16(self.Ptile, self.ATcol, 0, t_exp[ec], stC)
            t_at = self.tokd("sDVE")
            A0 = self.cnt["sACT"]
            avt = []
            for tb4 in range(4):
                @block.tensor
                def _(t, ec=ec, tb4=tb4, t_at=t_at):
                    if tb4 == 0:
                        self.W(t, t_at)
                    bk = 4 + tb4 % 2
                    self.W(t, av_ev[tb4 % 2])
                    last = None
                    for fc in range(ND):
                        last = t.matmul(pb[bk][:, :], self.ATcol[:, fc, :],
                                        self.VT[:, fc, tb4 * 512:(tb4 + 1) * 512],
                                        start=(fc == 0), stop=(fc == ND - 1))
                    avt.append(self.inc(last, "sPE"))
                av_ev[tb4 % 2] = ("sACT", A0 + tb4 + 2)
            return avt, A0

        def emit_scalar(ec_tav, avt, ec_sm):
            nonlocal att_sp
            V0 = self.cnt["sDVE"]

            @block.scalar
            def _(sc, ec_tav=ec_tav, avt=avt, ec_sm=ec_sm, V0=V0):
                if ec_sm is not None:
                    self.W(sc, Stok[ec_sm])
                    if ec_sm >= 1:
                        self.W(sc, t_exp[ec_sm - 1])
                    last = None
                    for fb in range(4):
                        last = sc.activation(self.Srow[:, fb * 512:(fb + 1) * 512],
                                             pb[fb][:, :], AF.Copy, scale=0.125)
                    self.inc(last, "sACT")
                    s_evac[ec_sm] = self.tokd("sACT")
                if avt is not None:
                    for tb4 in range(4):
                        self.W(sc, avt[tb4], t_rz[ec_tav])
                        if tb4 == 0:
                            self.W(sc, att_sp)
                        self.inc(sc.activation(
                            self.attst[:, tb4 * 512:(tb4 + 1) * 512],
                            pb[4 + tb4 % 2][:, :], AF.Copy,
                            scale=self.rzc[:, ec_tav % 2:ec_tav % 2 + 1]), "sACT")
                if ec_sm is not None:
                    sc.wait_ge(self.sem["sDVE"], V0 + 1)
                    if ec_sm >= 1:
                        self.W(sc, stC["ttok"][1])
                    self.inc(sc.activation(self.Ptile[:, :], self.Srow[:, :], AF.Exp,
                                           bias=st[:, 16 + ec_sm % 2:17 + ec_sm % 2],
                                           accum_out=st[:, 18 + ec_sm % 2:19 + ec_sm % 2]),
                             "sACT")
                    t_exp[ec_sm] = self.tokd("sACT")

            if ec_sm is not None:
                @block.vector
                def _(v, ec=ec_sm, V0=V0):
                    self.W(v, s_evac[ec])
                    self.vf(v, v.reduce_max(st[:, 16 + ec % 2:17 + ec % 2], self.Srow[:, :],
                                            axis=AX.X, negate=True))
                    assert self.cnt["sDVE"] == V0 + 1
                    self.W(v, t_exp[ec])
                    self.vf(v, v.reciprocal(self.rzc[:, ec % 2:ec % 2 + 1],
                                            st[:, 18 + ec % 2:19 + ec % 2]))
                    t_rz[ec] = self.tokd("sDVE")

            if avt is not None:
                @block.sync
                def _(sy, ec=ec_tav):
                    self.W(sy, self.tokd("sACT") if ec_sm is None
                           else av_ev[1])
                    self.inc(sy.dma_start(
                        self.scrATT[:, :, ec, :].rearrange("t p n -> p t n"),
                        self.attst.rearrange("p (t n) -> p t n", t=NT)), "sASP", 16)
                att_sp = self.tokd("sASP")

        emit_S(0)
        emit_scalar(None, None, 0)
        for ec in range(1, 16):
            emit_S(ec)
            avt, _ = emit_TAV_mm(ec - 1)
            emit_scalar(ec - 1, avt, ec)
        avt, A0f = emit_TAV_mm(15)
        for b in (0, 1):
            av_ev[b] = ("sACT", av_ev[b][1] - 1)
        emit_scalar(15, avt, None)
        self.t_C_sp = self.tokd("sASP")
        self.t_C_pe = self.tokd("sPE")
        self.t_C_sev = s_evac[15]
        self.t_C_avev = max(av_ev[0], av_ev[1], key=lambda x: x[1])
        self._dbg_mark("C")

    def _phaseD(self):
        block = self.block
        pb = self.pb
        st = self.st
        t_wo = [None]

        @block.gpsimd
        def _(g):
            self.W(g, self.t_C_pe)
            for fo in range(4):
                for hh in range(2):
                    t_wo[0] = self.inc(g.dma_start(
                        self.Wosb[:, hh * 8:(hh + 1) * 8, fo * 512:(fo + 1) * 512],
                        self.Wo[:, fo * 512:(fo + 1) * 512]
                        .rearrange("(c p) n -> p c n", p=128)[:, hh * 8:(hh + 1) * 8, :]),
                        "sWO", 16)

        att_ld, x_ld, x2_sp = {}, {}, {}
        ytok, x2tok, apply2, ttokD = {}, {}, {}, {}
        stD = {"ttok": {0: None, 1: None}, "trev": {0: None, 1: None}}
        V0D = self.cnt["sDVE"]
        A0D = self.cnt["sACT"]

        def emit_loads(tt):
            @block.sync
            def _(sy, tt=tt):
                if tt == 0:
                    self.W(sy, self.t_C_sp)
                if tt >= 2:
                    self.W(sy, ytok[tt - 2])
                att_ld[tt] = self.inc(sy.dma_start(
                    self.attsl[tt % 2][:, :, :], self.scrATT[tt]),
                    "sAL%d" % (tt % 2), 16)

            @block.gpsimd
            def _(g, tt=tt):
                if tt >= 1:
                    self.W(g, ("sDVE", V0D + 15 * (tt - 1) + 8))
                x_ld[tt] = self.inc(g.dma_start(
                    self.xtD[:, :], self.x[tt * 128:(tt + 1) * 128, :]), "sXD", 16)

        def emit_Y(tt):
            @block.tensor
            def _(t, tt=tt):
                self.W(t, att_ld[tt])
                if tt == 0:
                    self.W(t, t_wo[0], self.t_C_sev)
                else:
                    self.W(t, ("sACT", A0D + 7 * (tt - 1) + 1))
                last = None
                for db in range(4):
                    for ec in range(ND):
                        last = t.matmul(pb[db][:, :], self.attsl[tt % 2][:, ec, :],
                                        self.Wosb[:, ec, db * 512:(db + 1) * 512],
                                        start=(ec == 0), stop=(ec == ND - 1))
                ytok[tt] = self.inc(last, "sPE")

        def emit_chain(tt):
            V0 = self.cnt["sDVE"]
            A0 = self.cnt["sACT"]
            assert V0 == V0D + 15 * tt and A0 == A0D + 7 * tt

            @block.scalar
            def _(s, tt=tt, V0=V0):
                self.W(s, ytok[tt])
                if tt >= 1:
                    self.W(s, x2tok[tt - 1])
                if tt >= 2:
                    self.W(s, x2_sp[tt - 2], apply2[tt - 2])
                last = None
                for db in range(4):
                    last = s.activation(self.ytD[:, db * 512:(db + 1) * 512],
                                        pb[db][:, :], AF.Copy,
                                        accum_out=st[:, 8 + db:9 + db])
                self.af(s, last)
                self.inc(s.activation(self.x2tD[tt % 2][:, :], self.ytD[:, :], AF.Square,
                                      accum_out=st[:, 1:2]), "sACT")
                s.wait_ge(self.sem["sDVE"], V0 + 6)
                self.af(s, s.activation(st[:, 5:6], st[:, 4:5], AF.Ln, bias=EPS))
                self.af(s, s.activation(st[:, 6:7], st[:, 5:6], AF.Exp, scale=-0.5))
                s.wait_ge(self.sem["sDVE"], V0 + 8)
                self.inc(s.activation(self.ytD[:, :], self.x2tD[tt % 2][:, :], AF.Square,
                                      accum_out=st[:, 17:18]), "sACT")
                s.wait_ge(self.sem["sDVE"], V0 + 12)
                self.af(s, s.activation(st[:, 21:22], st[:, 20:21], AF.Ln, bias=EPS))
                self.af(s, s.activation(st[:, 22:23], st[:, 21:22], AF.Exp, scale=-0.5))
            t_evsq = ("sACT", A0 + 2)
            t_r = ("sACT", A0 + 4)
            t_sq2 = ("sACT", A0 + 5)
            t_rs2 = ("sACT", A0 + 7)
            assert self.cnt["sACT"] == A0 + 7

            @block.vector
            def _(v, tt=tt, t_evsq=t_evsq):
                self.W(v, t_evsq)
                self.vf(v, v.tensor_add(st[:, 12:13], st[:, 8:9], st[:, 9:10]))
                self.vf(v, v.tensor_add(st[:, 13:14], st[:, 10:11], st[:, 11:12]))
                self.vf(v, v.tensor_add(st[:, 0:1], st[:, 12:13], st[:, 13:14]))
                self.vf(v, v.tensor_scalar(st[:, 2:3], st[:, 0:1], -SQD, None, op0=ALU.mult))
                self.vf(v, v.tensor_mul(st[:, 3:4], st[:, 2:3], st[:, 2:3]))
                self.vf(v, v.scalar_tensor_tensor(st[:, 4:5], st[:, 1:2], SQD, st[:, 3:4],
                                                  op0=ALU.mult, op1=ALU.subtract))

            @block.vector
            def _(v, tt=tt, t_r=t_r):
                self.W(v, t_r, x_ld[tt])
                self.vf(v, v.tensor_scalar(self.x2tD[tt % 2][:, :], self.ytD[:, :],
                                           st[:, 2:3], st[:, 6:7],
                                           op0=ALU.add, op1=ALU.mult))
                self.vf(v, v.tensor_add(self.x2tD[tt % 2][:, :], self.x2tD[tt % 2][:, :],
                                        self.xtD[:, :]))
                x2tok[tt] = self.tokd("sDVE")
            assert x2tok[tt][1] == V0 + 8

            @block.sync
            def _(sy, tt=tt):
                self.W(sy, x2tok[tt])
                x2_sp[tt] = self.inc(
                    sy.dma_start(self.scrX2[tt], self.x2tD[tt % 2][:, :]),
                    "sX2S%d" % (tt % 2), 16)

            @block.vector
            def _(v, tt=tt, t_sq2=t_sq2):
                self.vf(v, v.reduce_sum(st[:, 16:17], self.x2tD[tt % 2][:, :], axis=AX.X))
                self.vf(v, v.tensor_scalar(st[:, 18:19], st[:, 16:17], -SQD, None,
                                           op0=ALU.mult))
                self.vf(v, v.tensor_mul(st[:, 19:20], st[:, 18:19], st[:, 18:19]))
                self.W(v, t_sq2)
                self.vf(v, v.scalar_tensor_tensor(st[:, 20:21], st[:, 17:18], SQD,
                                                  st[:, 19:20],
                                                  op0=ALU.mult, op1=ALU.subtract))
            assert self.cnt["sDVE"] == V0 + 12

            @block.vector
            def _(v, tt=tt, t_rs2=t_rs2):
                self.W(v, t_rs2)
                if tt >= 2:
                    self.W(v, ttokD[tt - 2])
                apply2[tt] = self.inc(
                    v.tensor_scalar(self.lnD[tt % 2][:, :], self.x2tD[tt % 2][:, :],
                                    st[:, 18:19], st[:, 22:23],
                                    op0=ALU.add, op1=ALU.mult), "sDVE")

            ttokD[tt] = self._transp16(self.lnD[tt % 2], self.ln2T, tt, apply2[tt], stD)

        for tt in range(NT):
            emit_loads(tt)
            emit_Y(tt)
            if tt >= 1:
                emit_chain(tt - 1)
        emit_chain(NT - 1)
        self.t_ln2T = self.tokd("sDVE")
        self.t_D_x2sp = [x2_sp[NT - 2], x2_sp[NT - 1]]
        self.t_D_yev = None
        self._dbg_mark("D")

    def _phaseE(self):
        block = self.block
        pb = self.pb
        t_bias1 = [None]

        @block.sync
        def _(sy):
            self.inc(sy.dma_start(self.brows[0:8, :],
                                  self.down_b.rearrange("(c n) -> c n", c=8)), "sBR", 16)
            t_bias1[0] = self.inc(sy.dma_start(
                self.brows[8:16, :],
                self.shared_b1.rearrange("(c n) -> c n", c=8)), "sBR", 16)

        bt = [None]

        @block.tensor
        def _(t):
            self.W(t, t_bias1[0], self.t_C_avev)
            bt[0] = self.inc(t.transpose(pb[5][:, 0:128], self.brows[:, :],
                                         self.identf[:, :]), "sPE")

        @block.vector
        def _(v):
            self.W(v, bt[0])
            v.tensor_copy(self.dwnbT[:, 0:8], pb[5][:, 0:8])
            self.inc(v.tensor_copy(self.sb1T[:, 0:8], pb[5][:, 8:16]), "sDVE")
        t_btv = self.tokd("sDVE")

        self.b1off = {}
        boff = 0
        b1t = {}
        for e in range(8):
            self.b1off[e] = boff

            @block.sync
            def _(sy, e=e):
                if e >= 1:
                    self.W(sy, b1t[e - 1])
                else:
                    self.W(sy, bt[0])
                self.inc(sy.dma_start(self.brows[0:NJ[e], :],
                                      self.eb1[e].rearrange("(c n) -> c n", c=NJ[e])),
                         "sBR", 16)

            @block.tensor
            def _(t, e=e):
                self.W(t, self.tokd("sBR"), self.tokd("sDVE"))
                b1t[e] = self.inc(t.transpose(pb[5][:, 0:128], self.brows[:, :],
                                              self.identf[:, :]), "sPE")

            @block.vector
            def _(v, e=e, boff=boff):
                self.W(v, b1t[e])
                self.inc(v.tensor_copy(self.b1T[:, boff:boff + NJ[e]],
                                       pb[5][:, 0:NJ[e]]), "sDVE")
            boff += NJ[e]
        t_b1 = self.tokd("sDVE")

        @block.scalar
        def _(s):
            self.W(s, t_b1)
            self.inc(s.activation(self.rT[:, 0:160], self.b1T[:, 0:160], AF.Gelu), "sACT")
        self.t_rT = self.tokd("sACT")

        gcons = {}
        gt = [None]
        gev = {0: self.t_D_yev, 1: self.t_D_yev}
        dwl = {}
        S0 = self.cnt["sG2S"]
        g2src = self.g2st.rearrange("p (t n) -> p t n", t=NT)
        for lh in range(2):
            @block.gpsimd
            def _(g, lh=lh):
                if lh == 0:
                    self.W(g, self.t_ln2T, self.t_D_x2sp[0], self.t_D_x2sp[1])
                else:
                    self.W(g, gcons[0])
                for hh in range(2):
                    dwl[lh] = self.inc(g.dma_start(
                        self.dwsl[:, hh * 8:(hh + 1) * 8, :],
                        self.down_w[:, lh * 512:(lh + 1) * 512]
                        .rearrange("(c p) n -> p c n", p=128)[:, hh * 8:(hh + 1) * 8, :]),
                        "sDW", 16)

            A0 = self.cnt["sACT"]
            order = []
            for i in range(16):
                order.append(("ev", i))
                if i % 2 == 1:
                    order.append(("g2", lh * 8 + i // 2))
            tok_of = {it: ("sACT", A0 + pos + 1) for pos, it in enumerate(order)}

            gtl = []
            for i in range(16):
                lc4, tb4 = i // 4, i % 4

                @block.tensor
                def _(t, lh=lh, lc4=lc4, tb4=tb4, i=i):
                    if i == 0:
                        self.W(t, dwl[lh])
                    bk = i % 2
                    self.W(t, gev[bk])
                    last = None
                    for fc in range(ND):
                        last = t.matmul(pb[bk][:, :],
                                        self.dwsl[:, fc, lc4 * 128:(lc4 + 1) * 128],
                                        self.ln2T[:, fc, tb4 * 512:(tb4 + 1) * 512],
                                        start=(fc == 0), stop=(fc == ND - 1))
                    gtl.append(self.inc(last, "sPE"))
                    if i == 15:
                        gcons[lh] = gtl[-1]
                gev[i % 2] = tok_of[("ev", i)]

            @block.scalar
            def _(sc, lh=lh, gtl=gtl, order=order, S0=S0):
                for it in order:
                    if it[0] == "ev":
                        i = it[1]
                        lc4, tb4 = i // 4, i % 4
                        Lc = lh * 4 + lc4
                        self.W(sc, gtl[i])
                        if lh == 0 and i == 0:
                            self.W(sc, t_btv, self.t_D_x2sp[0], self.t_D_x2sp[1],
                                   self.t_ln2T)
                        self.inc(sc.activation(
                            self.GT[:, Lc, tb4 * 512:(tb4 + 1) * 512], pb[i % 2][:, :],
                            AF.Gelu, bias=self.dwnbT[:, Lc:Lc + 1]), "sACT")
                    else:
                        fc = it[1]
                        if fc >= 1:
                            sc.wait_ge(self.sem["sG2S"], S0 + 16 * fc)
                        self.inc(sc.activation(self.g2st[:, :], self.ln2T[:, fc, :],
                                               AF.Gelu), "sACT")
            assert self.cnt["sACT"] == A0 + 24

            @block.sync
            def _(sy, order=order, tok_of=tok_of):
                for it in order:
                    if it[0] != "g2":
                        continue
                    fc = it[1]
                    self.W(sy, tok_of[it])
                    self.inc(sy.dma_start(
                        self.scrG2[:, :, fc, :].rearrange("t p n -> p t n"),
                        g2src), "sG2S", 16)
        self.t_g2 = self.tokd("sG2S")
        assert self.t_g2[1] == S0 + 16 * ND
        self.t_GT = self.tokd("sACT")


        lgev = {}
        lgt = [None]
        for tt in range(NT):
            @block.tensor
            def _(t, tt=tt):
                if tt == 0:
                    self.W(t, self.t_wc, self.t_C_avev)
                bk = 4 + tt % 2
                if tt >= 2:
                    self.W(t, lgev[tt - 2])
                last = None
                for fc in range(ND):
                    last = t.matmul(pb[bk][:, 0:8],
                                    self.ln2T[:, fc, tt * 128:(tt + 1) * 128],
                                    self.rwsb[:, fc, :], start=(fc == 0), stop=(fc == ND - 1))
                lgt[0] = self.inc(last, "sPE")

            @block.vector
            def _(v, tt=tt):
                self.W(v, lgt[0])
                if tt == 0:
                    self.W(v, self.t_rbbc)
                lgev[tt] = self.inc(
                    v.tensor_add(self.RW[:, tt, 0, :], pb[4 + tt % 2][:, 0:8],
                                 self.rbbc[:, :]), "sDVE")
        t_logit = self.tokd("sDVE")

        RW, rst = self.RW, self.rst

        def bc(col):
            base = rst[:, :, col:col + 1]
            return bass.AP(base.tensor, base.offset,
                           [list(base.ap[0]), list(base.ap[1]), [0, 8]])

        names = ["LG", "PR", "GE", "C1", "C2", "C3", "LE", "M1", "M2", "P2", "AL", "EX", "EX2"]
        S = {nm: RW[:, :, i, :] for i, nm in enumerate(names)}
        t_mid = [None]

        @block.vector
        def _(v):
            f = lambda inst: self.vf(v, inst)
            self.W(v, t_logit)
            f(v.reduce_max(rst[:, :, 0:1], S["LG"], axis=AX.X, negate=True))
            f(v.tensor_tensor(S["EX"], S["LG"], bc(0), op=ALU.add))
            t_mid[0] = self.tokd("sDVE")

        @block.scalar
        def _(s):
            self.W(s, t_mid[0])
            self.inc(s.activation(S["EX2"], S["EX"], AF.Exp), "sACT")
        t_e = self.tokd("sACT")

        @block.vector
        def _(v):
            f = lambda inst: self.vf(v, inst)
            self.W(v, t_e)
            f(v.reduce_sum(rst[:, :, 1:2], S["EX2"], axis=AX.X))
            f(v.reciprocal(rst[:, :, 2:3], rst[:, :, 1:2]))
            f(v.tensor_tensor(S["PR"], S["EX2"], bc(2), op=ALU.mult))

            def firstmax(dst, src, vcol):
                f(v.reduce_max(rst[:, :, vcol:vcol + 1], src, axis=AX.X))
                f(v.tensor_tensor(S["GE"], src, bc(vcol), op=ALU.is_ge))
                f(v.tensor_copy(S["C1"][:, :, 0:1], S["GE"][:, :, 0:1]))
                f(v.tensor_add(S["C1"][:, :, 1:8], S["GE"][:, :, 1:8], S["GE"][:, :, 0:7]))
                f(v.tensor_copy(S["C2"][:, :, 0:2], S["C1"][:, :, 0:2]))
                f(v.tensor_add(S["C2"][:, :, 2:8], S["C1"][:, :, 2:8], S["C1"][:, :, 0:6]))
                f(v.tensor_copy(S["C3"][:, :, 0:4], S["C2"][:, :, 0:4]))
                f(v.tensor_add(S["C3"][:, :, 4:8], S["C2"][:, :, 4:8], S["C2"][:, :, 0:4]))
                f(v.tensor_single_scalar(S["LE"], S["C3"], 1.0, op=ALU.is_le))
                f(v.tensor_mul(dst, S["GE"], S["LE"]))

            firstmax(S["M1"], S["PR"], 3)
            f(v.scalar_tensor_tensor(S["P2"], S["M1"], NEG_BIG, S["PR"],
                                     op0=ALU.mult, op1=ALU.add))
            firstmax(S["M2"], S["P2"], 4)
            f(v.tensor_sub(rst[:, :, 5:6], rst[:, :, 4:5], rst[:, :, 3:4]))
            t_mid[0] = self.tokd("sDVE")

        @block.scalar
        def _(s):
            self.W(s, t_mid[0])
            self.inc(s.activation(rst[:, :, 6:7], rst[:, :, 5:6], AF.Exp), "sACT")
        t_e2 = self.tokd("sACT")

        @block.vector
        def _(v):
            f = lambda inst: self.vf(v, inst)
            self.W(v, t_e2)
            f(v.tensor_single_scalar(rst[:, :, 7:8], rst[:, :, 6:7], 1.0, op=ALU.add))
            f(v.reciprocal(rst[:, :, 8:9], rst[:, :, 7:8]))
            f(v.tensor_mul(rst[:, :, 9:10], rst[:, :, 6:7], rst[:, :, 8:9]))
            f(v.tensor_add(rst[:, :, 10:11], rst[:, :, 8:9], rst[:, :, 9:10]))
            f(v.tensor_tensor(S["GE"], S["M1"], bc(8), op=ALU.mult))
            f(v.tensor_tensor(S["M2"], S["M2"], bc(9), op=ALU.mult))
            f(v.tensor_add(S["AL"], S["GE"], S["M2"]))
            f(v.tensor_copy(S["EX"][:, :, 0:1], rst[:, :, 10:11]))
        t_alpha = self.tokd("sDVE")

        agev = {}
        agt = [None]
        for tt in range(NT):
            @block.tensor
            def _(t, tt=tt):
                if tt == 0:
                    self.W(t, t_alpha, gcons[1])
                bk = 4 + tt % 2
                if tt < 2:
                    self.W(t, lgev[14 + tt])
                else:
                    self.W(t, agev[tt - 2])
                base = self.RW[:, tt, 10, 0:1]
                ag9 = bass.AP(base.tensor, base.offset, [list(base.ap[0]), [1, 9]])
                agt[0] = self.inc(
                    t.transpose(pb[bk][0:9, 0:128], ag9, self.identf[:, :]), "sPE")

            @block.vector
            def _(v, tt=tt):
                self.W(v, agt[0])
                if tt == 0:
                    self.W(v, self.t_GT)
                agev[tt] = self.inc(
                    v.tensor_copy(self.agT9[0:9, tt * 128:(tt + 1) * 128],
                                  pb[4 + tt % 2][0:9, 0:128]), "sDVE")
        self.t_agT = self.tokd("sDVE")

        @block.gpsimd
        def _(g):
            self.W(g, self.t_agT)
            self.t_agsp = self.inc(g.dma_start(self.scrAG, self.agT9[:, :]), "sAGS", 16)
        self.t_E_pe = self.tokd("sPE")
        self.t_E_gcons = gcons[1]
        self._dbg_mark("E")

    def _phaseF(self):
        block = self.block
        pb = self.pb
        zb_ev = {0: self.t_agT, 1: self.t_agT}
        p_act = {0: self.t_GT, 1: self.t_GT}
        wdve = {0: None, 1: None}
        w1gate = {0: None, 1: None}
        w2free = {0: self.t_E_pe, 1: self.t_E_pe}
        gp = [0]
        w1c = [0]
        w1slot = {0: None, 1: None, 2: None}
        w1of = {}
        zbt = [None]
        wcons = {}
        pend = [None]
        self._shev = {0: None, 1: None}

        t_bc = [None]

        @block.sync
        def _(sy):
            self.W(sy, self.t_g2)
            t_bc[0] = self.inc(sy.dma_start(
                self.B2bc[:, :], self.eb2[0][None, :].to_broadcast([128, L])), "sBC", 16)
        for e in range(1, 8):
            @block.sync
            def _(sy, e=e):
                self.W(sy, t_bc[0], self.tokd("sDVE"))
                if e == 1:
                    self.W(sy, self.t_agsp)
                t_bc[0] = self.inc(sy.dma_start(
                    self.xtraF[:, 0:L], self.eb2[e][None, :].to_broadcast([128, L])),
                    "sBC", 16)

            @block.vector
            def _(v, e=e):
                self.W(v, t_bc[0])
                self.inc(v.tensor_add(self.B2bc[:, :], self.B2bc[:, :],
                                      self.xtraF[:, 0:L]), "sDVE")
        t_b2bc = self.tokd("sDVE")

        gs_ld = {}
        abc_ld = {}
        t_sb2 = [None]
        for tb in range(NTB):
            @block.sync
            def _(sy, tb=tb):
                self.W(sy, self.t_agsp, self.tokd("sDVE"))
                gs_ld[tb] = self.inc(sy.dma_start(
                    self.gsbc[:, :],
                    self.scrAG[8:9, tb * TB:(tb + 1) * TB].to_broadcast([128, TB])),
                    "sABC", 16)

            for e in range(9):
                njc = NJ[e] if e < 8 else 8
                groups = [(g0, min(g0 + JG, njc)) for g0 in range(0, njc, JG)]
                if e < 8:
                    @block.sync
                    def _(sy, tb=tb, e=e):
                        self.W(sy, self.tokd("sDVE"))
                        abc_ld[(tb, e)] = self.inc(sy.dma_start(
                            self.abc[:, :],
                            self.scrAG[e:e + 1, tb * TB:(tb + 1) * TB]
                            .to_broadcast([128, TB])), "sABC", 16)

                    @block.vector
                    def _(v, tb=tb, e=e):
                        self.W(v, abc_ld[(tb, e)], gs_ld[tb])
                        self.inc(v.scalar_tensor_tensor(self.bbc[:, :], self.abc[:, :], -1.0,
                                                        self.gsbc[:, :], op0=ALU.mult,
                                                        op1=ALU.add), "sDVE")
                    t_ab = self.tokd("sDVE")
                else:
                    @block.sync
                    def _(sy, tb=tb):
                        self.W(sy, self.tokd("sDVE"))
                        t_sb2[0] = self.inc(sy.dma_start(
                            self.xtraF[:, 0:L],
                            self.shared_b2[None, :].to_broadcast([128, L])), "sBC", 16)
                    t_ab = None

                for (g0, g1) in groups:
                    nj = g1 - g0
                    assert nj == JG
                    w1s = self.ew1[e] if e < 8 else self.shared_w1
                    w2s = self.ew2[e] if e < 8 else self.shared_w2
                    t_w = {}
                    T_base = self.cnt["sPE"]
                    nb = NTI if pend[0] is not None else 0

                    def pred_pe(j, g0=g0, T_base=T_base, nb=nb):
                        d = j - g0 + 1
                        return ("sPE", T_base + d + (nb if d > 2 else 0))

                    def w1load(j, tb=tb, e=e, g0=g0, w1s=w1s, t_w=t_w):
                        slot = w1c[0] % 3
                        w1c[0] += 1
                        w1of[(tb, e, j)] = slot

                        @block.gpsimd
                        def _(g, j=j, slot=slot):
                            if w1slot[slot] is not None:
                                self.W(g, w1slot[slot])
                            elif slot == 2:
                                self.W(g, self.t_agsp, self.t_E_gcons)
                            if tb == 0 and e == 0 and j == 0:
                                self.W(g, self.t_E_gcons)
                            t_w[j] = self.inc(g.dma_start(
                                self.w1sl[slot][:, :, :], w1s[j]),
                                "sW1%d" % slot, 16)

                    def pmm(j, pred_act, tb=tb, e=e, g0=g0, pred_pe=pred_pe,
                            t_w=t_w):
                        slot = w1of[(tb, e, j)]

                        @block.tensor
                        def _(t, j=j, slot=slot):
                            self.W(t, t_w[j])
                            if j - 2 >= g0:
                                self.W(t, pred_act(j - 2))
                            else:
                                self.W(t, p_act[j % 2])
                            bk = (j % 2) * 2
                            last = None
                            for h in range(2):
                                for Lc in range(NL):
                                    last = t.matmul(
                                        pb[bk + h][:, :], self.w1sl[slot][:, Lc, :],
                                        self.GT[:, Lc,
                                                tb * TB + h * 512:tb * TB + (h + 1) * 512],
                                        start=(Lc == 0), stop=(Lc == NL - 1))
                            wcons[(tb, e, j)] = self.inc(last, "sPE")
                            assert wcons[(tb, e, j)] == pred_pe(j), \
                                (wcons[(tb, e, j)], pred_pe(j))
                            w1gate[j % 2] = wcons[(tb, e, j)]
                        w1slot[slot] = wcons[(tb, e, j)]

                    w1load(g0)
                    w1load(g0 + 1)
                    pmm(g0, None)
                    pmm(g0 + 1, None)
                    if pend[0] is not None:
                        pend[0]()
                        pend[0] = None

                    half = gp[0] % 2
                    woff = half * JG
                    A_base = self.cnt["sACT"]
                    V_base = self.cnt["sDVE"]

                    def pred_act(j, g0=g0, A_base=A_base):
                        return ("sACT", A_base + (j - g0 + 1))

                    def pred_dve(j, g0=g0, V_base=V_base):
                        return ("sDVE", V_base + 2 * (j - g0 + 1))

                    def w2load(j, jj, tb=tb, e=e, half=half, woff=woff, w2s=w2s):
                        @block.gpsimd
                        def _(g, j=j, jj=jj):
                            if jj == 0:
                                self.W(g, w2free[half])
                            if tb == 0 and e == 0 and j == 0:
                                self.W(g, self.t_g2)
                            self.inc(g.dma_start(
                                self.w2grp[:, woff + jj, :],
                                w2s[j * 128:(j + 1) * 128, :]),
                                "sW2%d" % half, 16)

                    for j in range(g0 + 2, g1):
                        w1load(j)
                        w2load(j - 2, j - 2 - g0)
                        pmm(j, pred_act)
                    w2load(g1 - 2, g1 - 2 - g0)
                    w2load(g1 - 1, g1 - 1 - g0)

                    @block.scalar
                    def _(s, tb=tb, e=e, g0=g0, g1=g1, woff=woff,
                          pred_act=pred_act, pred_dve=pred_dve):
                        for j in range(g0, g1):
                            jj = j - g0
                            self.W(s, wcons[(tb, e, j)])
                            if e < 8:
                                if j - 2 >= g0:
                                    self.W(s, pred_dve(j - 2))
                                elif wdve[j % 2]:
                                    self.W(s, wdve[j % 2])
                            if tb == 0 and e == 0 and j == 0:
                                self.W(s, self.t_rT, self.t_agsp)
                            bk = (j % 2) * 2
                            bias = (self.b1T[:, self.b1off[e] + j:self.b1off[e] + j + 1]
                                    if e < 8 else self.sb1T[:, j:j + 1])
                            dst = (self.stl[j % 2][:, :] if e < 8
                                   else self.wtil[:, woff + jj, :])
                            tk = self.inc(s.activation(dst, self.pq[bk // 2][:, :],
                                                       AF.Gelu, bias=bias), "sACT")
                            assert tk == pred_act(j), (tk, pred_act(j))
                            p_act[j % 2] = tk

                    if e < 8:
                        for j in range(g0, g1):
                            jj = j - g0

                            @block.vector
                            def _(v, tb=tb, e=e, j=j, jj=jj, g0=g0, t_ab=t_ab,
                                  woff=woff, pred_act=pred_act, pred_dve=pred_dve):
                                self.W(v, pred_act(j))
                                if j == g0:
                                    self.W(v, t_ab, wdve[0], wdve[1])
                                else:
                                    self.W(v, pred_dve(j - 1))
                                if tb == 0 and e == 0 and j == 0:
                                    self.W(v, self.t_rT)
                                self.vf(v, v.tensor_mul(self.xtraB[:, :], self.abc[:, :],
                                                        self.stl[j % 2][:, :]))
                                tk = self.inc(
                                    v.scalar_tensor_tensor(
                                        self.wtil[:, woff + jj, :], self.bbc[:, :],
                                        self.rT[:, self.b1off[e] + j:self.b1off[e] + j + 1],
                                        self.xtraB[:, :], op0=ALU.mult, op1=ALU.add),
                                    "sDVE")
                                assert tk == pred_dve(j), (tk, pred_dve(j))
                                wdve[j % 2] = tk
                    t_wgrp = self.tokd("sDVE") if e < 8 else self.tokd("sACT")
                    t_w2grp = self.tokd("sW2%d" % half)

                    def emit_pass2(tb=tb, e=e, g0=g0, nj=nj, half=half, woff=woff,
                                   t_wgrp=t_wgrp, t_w2grp=t_w2grp):
                        for ti in range(NTI):
                            tt = tb * NTI + ti

                            @block.tensor
                            def _(t, ti=ti):
                                if ti == 0:
                                    self.W(t, t_wgrp, t_w2grp)
                                bk = 4 + (ti % 2) * 2
                                self.W(t, zb_ev[ti % 2])
                                last = None
                                for h in range(2):
                                    for jj in range(nj):
                                        last = t.matmul(
                                            pb[bk + h][:, :],
                                            self.wtil[:, woff + jj,
                                                      ti * 128:(ti + 1) * 128],
                                            self.w2grp[:, woff + jj,
                                                       h * 512:(h + 1) * 512],
                                            start=(jj == 0), stop=(jj == nj - 1))
                                zbt[0] = self.inc(last, "sPE")

                            if e < 8:
                                @block.vector
                                def _(v, ti=ti, tt=tt):
                                    self.W(v, zbt[0])
                                    if tb == 0 and e == 0 and g0 == 0 and ti == 0:
                                        self.W(v, t_b2bc)
                                    bk = 4 + (ti % 2) * 2
                                    z2 = self.z[:, tt, :].rearrange("p (a c) -> p a c",
                                                                    a=2)
                                    last = None
                                    for h in range(2):
                                        if e == 0 and g0 == 0:
                                            last = v.scalar_tensor_tensor(
                                                z2[:, h, :],
                                                self.B2bc[:, h * 512:(h + 1) * 512],
                                                self.rst[:, tt, 10:11], pb[bk + h][:, :],
                                                op0=ALU.mult, op1=ALU.add)
                                        else:
                                            last = v.tensor_add(z2[:, h, :], z2[:, h, :],
                                                                pb[bk + h][:, :])
                                    zb_ev[ti % 2] = self.inc(last, "sDVE")
                            else:
                                @block.scalar
                                def _(s, ti=ti):
                                    self.W(s, zbt[0], self._shev[ti % 2])
                                    bk = 4 + (ti % 2) * 2
                                    last = None
                                    for h in range(2):
                                        last = s.activation(
                                            self.stl[ti % 2][:, h * 512:(h + 1) * 512],
                                            pb[bk + h][:, :], AF.Copy, scale=0.1)
                                    self.inc(last, "sACT")
                                t_sh = self.tokd("sACT")

                                @block.vector
                                def _(v, ti=ti, tt=tt, t_sh=t_sh):
                                    self.W(v, t_sh, t_sb2[0])
                                    z2 = self.z[:, tt, :].rearrange("p (a c) -> p a c",
                                                                    a=2)
                                    x2 = self.xtraF[:, 0:L].rearrange("p (a c) -> p a c",
                                                                      a=2)
                                    last = None
                                    for h in range(2):
                                        last = v.tensor_add(
                                            z2[:, h, :], z2[:, h, :],
                                            self.stl[ti % 2][:, h * 512:(h + 1) * 512])
                                    self.vf(v, last)
                                    last = None
                                    for h in range(2):
                                        last = v.scalar_tensor_tensor(
                                            z2[:, h, :], x2[:, h, :], 0.1, z2[:, h, :],
                                            op0=ALU.mult, op1=ALU.add)
                                    zb_ev[ti % 2] = self.inc(last, "sDVE")
                                    self._shev[ti % 2] = zb_ev[ti % 2]
                        w2free[half] = zbt[0]
                        if e == 8:
                            wdve[0] = wdve[1] = self.tokd("sDVE")
                        gp[0] += 1
                    pend[0] = emit_pass2
        if pend[0] is not None:
            pend[0]()
            pend[0] = None
        self.t_F_z = self.tokd("sDVE")
        self.t_F_pe = self.tokd("sPE")
        self._dbg_mark("F")

    def _phaseG(self):
        block = self.block
        pb, pbbf = self.pb, self.pbbf
        st = self.st
        trev = {0: self.t_F_z, 1: self.t_F_z}
        zc_tok = {}
        ztt = {}
        P0 = self.cnt["sPE"]
        A0 = self.cnt["sACT"]

        @block.scalar
        def _(s, P0=P0):
            self.W(s, self.t_F_z)
            for tt in range(NT):
                if tt >= 1:
                    s.wait_ge(self.sem["sPE"], P0 + tt)
                self.inc(s.activation(self.zst[:, :], self.z[:, tt, :], AF.Copy), "sACT")
        for tt in range(NT):
            zc_tok[tt] = ("sACT", A0 + tt + 1)
        assert self.cnt["sACT"] == A0 + NT

        for tt in range(NT):
            @block.tensor
            def _(t, tt=tt):
                self.W(t, zc_tok[tt], trev[tt % 2])
                bank = pbbf[6 + tt % 2]
                last = None
                for Lc in range(NL):
                    last = t.transpose(bank[:, Lc * 128:(Lc + 1) * 128],
                                       self.zst[:, Lc * 128:(Lc + 1) * 128], self.identb[:, :])
                ztt[tt] = self.inc(last, "sPE")
                assert ztt[tt][1] == P0 + tt + 1

            @block.vector
            def _(v, tt=tt):
                self.W(v, ztt[tt])
                bank = pbbf[6 + tt % 2]
                trev[tt % 2] = self.inc(
                    v.tensor_copy(self.z16T[:, :, tt * 128:(tt + 1) * 128],
                                  bank[:, 0:NL * 128].rearrange("p (c n) -> p c n", c=NL)),
                    "sDVE")
        t_z16 = self.tokd("sDVE")
        t_zcast = zc_tok[NT - 1]

        t_gw = [None]

        @block.gpsimd
        def _(g):
            self.W(g, self.t_F_pe)
            for fo in range(4):
                for hh in range(2):
                    self.inc(g.dma_start(
                        self.corew[:, hh * 8:(hh + 1) * 8, fo * 512:(fo + 1) * 512],
                        self.core_w[:, fo * 512:(fo + 1) * 512]
                        .rearrange("(c p) n -> p c n", p=128)[:, hh * 8:(hh + 1) * 8, :]),
                        "sGW", 16)
            self.W(g, zc_tok[NL - 1])
            for hh in range(2):
                t_gw[0] = self.inc(g.dma_start(
                    self.upw[:, hh * 4:(hh + 1) * 4, :],
                    self.up_w.rearrange("(c p) n -> p c n", p=128)[:, hh * 4:(hh + 1) * 4, :]),
                    "sGW", 16)

        t_budld = [None]

        @block.sync
        def _(sy):
            self.W(sy, self.t_F_z, t_zcast)
            self.inc(sy.dma_start(self.B_ud[:, :],
                                  self.up_b[None, :].to_broadcast([128, D])), "sBUD", 16)
            t_budld[0] = self.inc(sy.dma_start(
                self.y2t[:, :], self.core_b[None, :].to_broadcast([128, D])), "sBUD", 16)

        @block.vector
        def _(v):
            self.W(v, t_budld[0])
            self.inc(v.tensor_add(self.B_ud[:, :], self.B_ud[:, :], self.y2t[:, :]), "sDVE")
        t_bud = self.tokd("sDVE")

        g2ld, x2ld, out_tok, out_st = {}, {}, {}, {}
        gy = {}
        y_ev = None
        y_evs = {}
        V1 = self.cnt["sDVE"]
        A1 = self.cnt["sACT"]

        @block.scalar
        def _(s, V1=V1):
            for tt in range(NT):
                s.wait_ge(self.sem["sDVE"], V1 + 9 * tt + 1)
                self.inc(s.activation(self.sqbG[:, :], self.y2t[:, :], AF.Square,
                                      accum_out=st[:, 1:2]), "sACT")
                s.wait_ge(self.sem["sDVE"], V1 + 9 * tt + 7)
                self._ln_rstd(s)
        assert self.cnt["sACT"] == A1 + 3 * NT

        for tt in range(NT):
            @block.sync
            def _(sy, tt=tt):
                if tt == 0:
                    self.W(sy, self.t_g2, self.t_F_z, t_zcast)
                if tt >= 2:
                    self.W(sy, gy[tt - 2], out_st[tt - 2])
                g2ld[tt] = self.inc(sy.dma_start(
                    self.g2sl[tt % 2][:, :, :], self.scrG2[tt]),
                    "sGL%d" % (tt % 2), 16)
                if tt < 2:
                    x2ld[tt] = self.inc(sy.dma_start(
                        self.x2tG[tt % 2][:, :], self.scrX2[tt]),
                        "sXG%d" % (tt % 2), 16)

            if tt >= 2:
                @block.gpsimd
                def _(g, tt=tt):
                    self.W(g, gy[tt - 2], out_st[tt - 2])
                    x2ld[tt] = self.inc(g.dma_start(
                        self.x2tG[tt % 2][:, :], self.scrX2[tt]),
                        "sXB%d" % (tt % 2), 16)

            @block.tensor
            def _(t, tt=tt):
                bq = (tt % 2) * 4
                self.W(t, y_evs.get(tt - 2))
                if tt <= 1:
                    self.W(t, t_gw[0], t_z16, t_zcast)
                for db in range(4):
                    for Lc in range(NL):
                        t.matmul(pb[bq + db][:, :],
                                 self.z16T[:, Lc, tt * 128:(tt + 1) * 128],
                                 self.upw[:, Lc, db * 512:(db + 1) * 512],
                                 start=(Lc == 0), stop=False)
                self.W(t, g2ld[tt])
                last = None
                for db in range(4):
                    for fc in range(ND):
                        last = t.matmul(pb[bq + db][:, :],
                                        self.g2sl[tt % 2][:, fc, :],
                                        self.corew[:, fc, db * 512:(db + 1) * 512],
                                        start=False, stop=(fc == ND - 1))
                gy[tt] = self.inc(last, "sPE")

            @block.vector
            def _(v, tt=tt):
                bq = (tt % 2) * 4
                self.W(v, gy[tt])
                if tt == 0:
                    self.W(v, t_bud)
                last = None
                for db in range(4):
                    last = v.scalar_tensor_tensor(
                        self.y2t[:, db * 512:(db + 1) * 512], pb[bq + db][:, :], 1.0,
                        self.B_ud[:, db * 512:(db + 1) * 512],
                        op0=ALU.mult, op1=ALU.add, accum_out=st[:, 8 + db:9 + db])
                self.vf(v, last)
            y_ev = self.tokd("sDVE")
            y_evs[tt] = y_ev
            assert y_ev[1] == V1 + 9 * tt + 1
            t_sq = ("sACT", A1 + 3 * tt + 1)

            @block.vector
            def _(v, tt=tt, t_sq=t_sq):
                st_ = st
                self.vf(v, v.tensor_add(st_[:, 12:13], st_[:, 8:9], st_[:, 9:10]))
                self.vf(v, v.tensor_add(st_[:, 13:14], st_[:, 10:11], st_[:, 11:12]))
                self.vf(v, v.tensor_add(st_[:, 0:1], st_[:, 12:13], st_[:, 13:14]))
                self.vf(v, v.tensor_scalar(st_[:, 2:3], st_[:, 0:1], -SQD, None, op0=ALU.mult))
                self.vf(v, v.tensor_mul(st_[:, 3:4], st_[:, 2:3], st_[:, 2:3]))
                self.W(v, t_sq)
                self.vf(v, v.scalar_tensor_tensor(st_[:, 4:5], st_[:, 1:2], SQD, st_[:, 3:4],
                                                  op0=ALU.mult, op1=ALU.subtract))
            tv = self.tokd("sDVE")
            assert tv[1] == V1 + 9 * tt + 7
            t_r = ("sACT", A1 + 3 * tt + 3)

            @block.vector
            def _(v, tt=tt, t_r=t_r):
                self.W(v, t_r, x2ld[tt])
                self.vf(v, v.tensor_scalar(self.outt[:, :], self.y2t[:, :],
                                           st[:, 2:3], st[:, 6:7],
                                           op0=ALU.add, op1=ALU.mult))
                out_tok[tt] = self.inc(
                    v.tensor_add(self.x2tG[tt % 2][:, :], self.x2tG[tt % 2][:, :],
                                 self.outt[:, :]), "sDVE")

            @block.sync
            def _(sy, tt=tt):
                self.W(sy, out_tok[tt])
                out_st[tt] = self.inc(
                    sy.dma_start(self.out[tt * 128:(tt + 1) * 128, :],
                                 self.x2tG[tt % 2][:, :]), "sOU%d" % (tt % 2), 16)
        self._dbg_mark("G")

    def _dbg_mark(self, phase):
        for (name, ph, view_fn) in self.dbg:
            if ph == phase:
                self.dbg_done.append((name, view_fn, dict(self.cnt)))

    def _dbg_flush(self):
        block, nc = self.block, self.nc

        @block.sync
        def _(sy):
            for (name, view_fn, waits) in self.dbg_done:
                for snm, vv in waits.items():
                    if vv > 0 and snm != "sOUT":
                        sy.wait_ge(self.sem[snm], vv)
                ap = view_fn(self)
                dst = nc.dram_tensor("dbg_" + name, list(ap.shape), ap.dtype,
                                     kind="ExternalOutput").ap()
                self.inc(sy.dma_start(dst, ap), "sOUT", 16)

    def _final_wait(self):
        block = self.block

        @block.sync
        def _(sy):
            for nm in ["sOUT", "sOU0", "sOU1"]:
                if self.cnt[nm] > 0:
                    sy.wait_ge(self.sem[nm], self.cnt[nm])


def _digest_w1(w):
    Lr, H = w.shape
    nj = H // 128
    return np.ascontiguousarray(
        np.asarray(w, np.float32).reshape(Lr // 128, 128, nj, 128)
        .transpose(2, 1, 0, 3))


def _core_inputs(inputs):
    m = {
        "Wq": inputs["L0_attn_Wq"], "Wk": inputs["L0_attn_Wk"],
        "Wv": inputs["L0_attn_Wv"], "Wo": inputs["L0_attn_Wo"],
        "down_w": inputs["L0_ffn_down_w"], "down_b": inputs["L0_ffn_down_b"],
        "up_w": inputs["L0_ffn_up_w"], "up_b": inputs["L0_ffn_up_b"],
        "router_w": inputs["L0_ffn_router_w"], "router_b": inputs["L0_ffn_router_b"],
        "shared_w1": _digest_w1(inputs["L0_ffn_shared_w1"]),
        "shared_b1": inputs["L0_ffn_shared_b1"],
        "shared_w2": inputs["L0_ffn_shared_w2"], "shared_b2": inputs["L0_ffn_shared_b2"],
        "core_w": inputs["L0_ffn_core_w"], "core_b": inputs["L0_ffn_core_b"],
    }
    for e in range(8):
        m[f"e{e}_w1"] = _digest_w1(inputs[f"L0_ffn_e{e}_w1"])
        m[f"e{e}_b1"] = inputs[f"L0_ffn_e{e}_b1"]
        m[f"e{e}_w2"] = inputs[f"L0_ffn_e{e}_w2"]
        m[f"e{e}_b2"] = inputs[f"L0_ffn_e{e}_b2"]
    m["ident"] = np.eye(128, dtype=np.float32)
    cost = np.array([1024 * h + h * 1024 for h in HID], np.float32)
    m["negcost"] = (-COST_LAMBDA * cost)[None, :]
    return {k: np.ascontiguousarray(np.asarray(v, dtype=np.float32)) for k, v in m.items()}


def kernel(**inputs):
    x = np.ascontiguousarray(np.asarray(inputs["L0_attn_x"], dtype=np.float32))
    mk = MK()
    base = _core_inputs(inputs)
    in_maps = []
    for c in range(8):
        mm = dict(base)
        mm["x"] = np.ascontiguousarray(x[c])
        in_maps.append(mm)
    res = run_bass_kernel_spmd(mk.nc, in_maps, core_ids=list(range(8)))
    return np.stack([r["out"] for r in res.results], axis=0)


# version 12 (final)
# speedup vs baseline: 1.3867x; 1.0101x over previous
import numpy as np
import concourse.bass as bass
import concourse.mybir as mybir
from concourse.bass_utils import run_bass_kernel_spmd

F32 = mybir.dt.float32
BF16 = mybir.dt.bfloat16
U8 = mybir.dt.uint8
AF = mybir.ActivationFunctionType
ALU = mybir.AluOpType
AX = mybir.AxisListType

T = 2048
D = 2048
L = 1024
NT = T // 128
ND = D // 128
NL = L // 128
HID = [1024, 2048, 3072, 4096, 1024, 2048, 3072, 4096]
NJ = [h // 128 for h in HID]
COST_LAMBDA = 1e-07
EPS = 1e-5
TB = 1024
NTB = T // TB
NTI = TB // 128
JG = 8
NEG_BIG = -1e30
SQD = 1.0 / D

SEM_NAMES = [
    "sCONST", "sX0", "sX1", "sSPQ", "sKL", "sQS0", "sQS1", "sASP",
    "sAL0", "sAL1", "sXD", "sX2S0", "sX2S1", "sBR", "sG2S", "sAGS",
    "sBC", "sABC", "sBUD", "sGL0", "sGL1", "sXG0", "sXG1", "sXB0", "sXB1", "sOUT",
    "sWC", "sWS0", "sWS1", "sWO", "sDW", "sW10", "sW11", "sW12", "sW20", "sW21", "sGW",
    "sOU0", "sOU1",
    "sPE", "sACT", "sDVE",
]


class MK:
    def __init__(self, phase_limit=99, dbg=None):
        self.phase_limit = phase_limit
        self.dbg = dbg or []
        self.dbg_done = []
        nc = bass.Bass("TRN2", target_bir_lowering=False)
        self.nc = nc
        self._dram()
        self._sbuf()
        self._build()

    def _dram(self):
        nc = self.nc

        def din(name, shape):
            return nc.dram_tensor(name, shape, F32, kind="ExternalInput").ap()

        self.x = din("x", [T, D])
        self.Wq = din("Wq", [D, D]); self.Wk = din("Wk", [D, D])
        self.Wv = din("Wv", [D, D]); self.Wo = din("Wo", [D, D])
        self.down_w = din("down_w", [D, L]); self.down_b = din("down_b", [L])
        self.up_w = din("up_w", [L, D]); self.up_b = din("up_b", [D])
        self.router_w = din("router_w", [D, 8]); self.router_b = din("router_b", [8])
        self.shared_w1 = din("shared_w1", [L // 128, 128, NL, 128])
        self.shared_b1 = din("shared_b1", [L])
        self.shared_w2 = din("shared_w2", [L, L]); self.shared_b2 = din("shared_b2", [L])
        self.core_w = din("core_w", [D, D]); self.core_b = din("core_b", [D])
        self.ew1 = [din(f"e{i}_w1", [NJ[i], 128, NL, 128]) for i in range(8)]
        self.eb1 = [din(f"e{i}_b1", [HID[i]]) for i in range(8)]
        self.ew2 = [din(f"e{i}_w2", [HID[i], L]) for i in range(8)]
        self.eb2 = [din(f"e{i}_b2", [L]) for i in range(8)]
        self.ident = din("ident", [128, 128])
        self.negcost = din("negcost", [1, 8])
        self.out = nc.dram_tensor("out", [T, D], F32, kind="ExternalOutput").ap()

        def scr(name, shape, dt):
            return nc.dram_tensor(name, shape, dt, kind="Internal").ap()

        self.scrQ = scr("scrQ", [4, NT, 128, 512], BF16)
        self.scrK = scr("scrK", [4, NT, 128, 512], BF16)
        self.scrATT = scr("scrATT", [NT, 128, ND, 128], BF16)
        self.scrX2 = scr("scrX2", [NT, 128, D], F32)
        self.scrG2 = scr("scrG2", [NT, 128, ND, 128], BF16)
        self.scrAG = scr("scrAG", [9, T], BF16)

    def _sbuf(self):
        nc = self.nc

        def arena(name, nbytes):
            return nc.alloc_sbuf_tensor(name, [128, nbytes], U8)

        def view(ar, off, shape, dt):
            esz = {F32: 4, BF16: 2}[dt]
            n = 1
            for s in shape[1:]:
                n *= s
            assert off + n * esz <= ar.shape[1], (off, n * esz)
            ap = ar[:, off:off + n * esz].bitcast(dt)
            if len(shape) == 3:
                ap = ap.rearrange("p (a c) -> p a c", a=shape[1])
            return ap

        arA = arena("arA", 65536)
        arB = arena("arB", 65536)
        arC = arena("arC", 53504)
        self.view = view

        self.ln1T = view(arA, 0, [128, ND, T], BF16)
        self.Kar = view(arA, 0, [128, NT, D], BF16)
        self.ln2T = view(arA, 0, [128, ND, T], BF16)
        self.wtil = view(arA, 0, [128, 2 * JG, TB], BF16)
        self.w2grp = view(arA, 32768, [128, 2 * JG, L], BF16)
        self.corew = view(arA, 0, [128, ND, D], BF16)
        self.VT = view(arB, 0, [128, ND, T], BF16)
        self.Wosb = view(arB, 0, [128, ND, D], BF16)
        self.z = view(arB, 0, [128, NT, L], F32)
        self.upw = view(arB, 0, [128, NL, D], BF16)
        self.y2t = view(arB, 32768, [128, D], F32)
        self.x2tG = [view(arB, 40960, [128, D], F32), view(arB, 49152, [128, D], F32)]
        self.outt = view(arB, 57344, [128, D], F32)
        self.xtA = [view(arC, 32768, [128, D], F32), view(arC, 32768, [128, D], F32)]
        self.lnA = [view(arC, 40960, [128, D], BF16), view(arC, 45056, [128, D], BF16)]
        self.wslab = [view(arC, 0, [128, ND, 512], BF16), view(arC, 16384, [128, ND, 512], BF16)]
        self.stage = view(arB, 49152, [128, NT, 512], BF16)
        self.stageK = view(arB, 0, [128, NT, 512], BF16)
        self.qslab = [view(arC, 0, [128, NT, 512], BF16), view(arC, 16384, [128, NT, 512], BF16)]
        self.Srow = view(arC, 32768, [128, D], F32)
        self.Ptile = view(arC, 40960, [128, D], BF16)
        self.ATcol = view(arC, 45056, [128, ND, 128], BF16)
        self.attst = view(arC, 49152, [128, D], BF16)
        self.attsl = [view(arC, 0, [128, ND, 128], BF16), view(arC, 4096, [128, ND, 128], BF16)]
        self.xtD = view(arC, 8192, [128, D], F32)
        self.ytD = view(arC, 16384, [128, D], F32)
        self.x2tD = [view(arC, 24576, [128, D], F32), view(arC, 32768, [128, D], F32)]
        self.lnD = [view(arC, 40960, [128, D], BF16), view(arC, 45056, [128, D], BF16)]
        self.GT = view(arC, 0, [128, NL, T], BF16)
        self.dwsl = view(arC, 32768, [128, ND, 512], BF16)
        self.g2st = view(arC, 49152, [128, D], BF16)
        self.agT9 = arC[0:9, 32768:32768 + T * 4].bitcast(F32)
        self.stl = [view(arC, 32768, [128, TB], BF16), view(arC, 34816, [128, TB], BF16)]
        self.xtraB = view(arC, 36864, [128, TB], BF16)
        self.w1sl = [view(arC, 40960, [128, NL, 128], BF16),
                     view(arC, 43008, [128, NL, 128], BF16),
                     view(arC, 38912, [128, NL, 128], BF16)]
        self.xtraF = view(arC, 45056, [128, TB], F32)
        self.B2bc = view(arC, 49152, [128, L], F32)
        self.z16T = view(arC, 0, [128, NL, T], BF16)
        self.g2sl = [view(arC, 32768, [128, ND, 128], BF16), view(arC, 36864, [128, ND, 128], BF16)]
        self.zst = view(arC, 40960, [128, L], BF16)
        self.sqbG = view(arC, 40960, [128, D], BF16)
        self.B_ud = view(arC, 45056, [128, D], F32)

        self.identf = nc.alloc_sbuf_tensor("identf", [128, 128], F32)
        self.identb = nc.alloc_sbuf_tensor("identb", [128, 128], BF16)
        self.st = nc.alloc_sbuf_tensor("stats", [128, 32], F32)
        self.RW = nc.alloc_sbuf_tensor("rw3", [128, 16, 14, 8], F32)
        self.rst = nc.alloc_sbuf_tensor("rst", [128, 16, 12], F32)
        self.gsbc = nc.alloc_sbuf_tensor("gsbc", [128, TB], BF16)
        self.abc = nc.alloc_sbuf_tensor("abc", [128, TB], BF16)
        self.bbc = nc.alloc_sbuf_tensor("bbc", [128, TB], BF16)
        self.dwnbT = nc.alloc_sbuf_tensor("dwnbT", [128, 8], F32)
        self.sb1T = nc.alloc_sbuf_tensor("sb1T", [128, 8], F32)
        self.b1T = nc.alloc_sbuf_tensor("b1T", [128, 160], F32)
        self.rT = nc.alloc_sbuf_tensor("rT", [128, 160], F32)
        self.brows = nc.alloc_sbuf_tensor("brows", [128, 128], F32)
        self.rbbc = nc.alloc_sbuf_tensor("rbbc", [128, 8], F32)
        self.rbtmp = nc.alloc_sbuf_tensor("rbtmp", [128, 8], F32)
        self.rwsb = nc.alloc_sbuf_tensor("rwsb", [128, ND, 8], BF16)
        self.rzc = nc.alloc_sbuf_tensor("rzc", [128, 4], F32)
        self.constv = nc.alloc_sbuf_tensor("constv", [128, 8], F32)
        self.CONSTS = [EPS, -0.5, 0.125, 0.1, 1.0, 0.0]
        for i, cval in enumerate(self.CONSTS):
            nc.const_aps.register(F32, cval, self.constv[:, i:i + 1])
        assert nc.sbuf_bytes_remaining >= 0, nc.sbuf_bytes_remaining

    def _build(self):
        from contextlib import ExitStack
        nc = self.nc
        with ExitStack() as ctx:
            self.pq = [ctx.enter_context(nc.psum_tensor(f"pq{i}", [128, 1024], F32))
                       for i in range(4)]
            self.pb = [self.pq[i // 2][:, (i % 2) * 512:(i % 2 + 1) * 512]
                       for i in range(8)]
            self.pbbf = [self.pb[i].bitcast(BF16) for i in range(8)]
            self.sem = {}
            for nm in SEM_NAMES:
                self.sem[nm] = ctx.enter_context(nc.semaphore(nm))
            self.cnt = {nm: 0 for nm in SEM_NAMES}
            with nc.Block() as block:
                self.block = block
                self._program()

    def inc(self, inst, nm, n=1):
        self.cnt[nm] += n
        inst.then_inc(self.sem[nm], n)
        return (nm, self.cnt[nm])

    def W(self, eng, *toks):
        best = {}
        for tk in toks:
            if tk is not None and tk[1] > 0:
                if best.get(tk[0], 0) < tk[1]:
                    best[tk[0]] = tk[1]
        for nm, v in best.items():
            eng.wait_ge(self.sem[nm], v)

    def tokd(self, nm):
        return (nm, self.cnt[nm])

    def vf(self, v, inst):
        tk = self.inc(inst, "sDVE")
        v.wait_ge(self.sem["sDVE"], tk[1])

    def af(self, s, inst):
        tk = self.inc(inst, "sACT")
        s.wait_ge(self.sem["sACT"], tk[1])

    def _program(self):
        self._setup()
        pl = self.phase_limit
        if pl >= 1:
            self._phaseA()
        if pl >= 2:
            self._phaseB()
        if pl >= 3:
            self._phaseC()
        if pl >= 4:
            self._phaseD()
        if pl >= 5:
            self._phaseE()
        if pl >= 6:
            self._phaseF()
        if pl >= 7:
            self._phaseG()
        self._dbg_flush()
        self._final_wait()

    def _setup(self):
        block = self.block

        @block.sync
        def _(sy):
            self.inc(sy.dma_start(self.identf[:, :], self.ident), "sCONST", 16)
            self.inc(sy.dma_start(self.rbbc[:, :],
                                  self.router_b[None, :].to_broadcast([128, 8])), "sCONST", 16)
            self.t_const = self.inc(sy.dma_start(
                self.rbtmp[:, :], self.negcost.to_broadcast([128, 8])), "sCONST", 16)

        @block.gpsimd
        def _(g):
            self.inc(g.dma_start(self.identb[:, :], self.ident), "sWC", 16)
            self.t_wc = self.inc(g.dma_start(
                self.rwsb[:, :, :],
                self.router_w.rearrange("(c p) e -> p c e", p=128)), "sWC", 16)

        @block.vector
        def _(v):
            for i, cval in enumerate(self.CONSTS):
                v.memset(self.constv[:, i:i + 1], cval)
            self.W(v, self.t_const)
            self.t_rbbc = self.inc(
                v.tensor_add(self.rbbc[:, :], self.rbbc[:, :], self.rbtmp[:, :]), "sDVE")

    def _ln_stats(self, v, xsrc, sqdump):
        st = self.st
        v.reduce_sum(st[:, 0:1], xsrc, axis=AX.X)
        self.vf(v, v.scalar_tensor_tensor(sqdump, xsrc, 1.0, xsrc,
                                          op0=ALU.mult, op1=ALU.mult, accum_out=st[:, 1:2]))
        self.vf(v, v.tensor_scalar(st[:, 2:3], st[:, 0:1], -SQD, None, op0=ALU.mult))
        self.vf(v, v.tensor_mul(st[:, 3:4], st[:, 2:3], st[:, 2:3]))
        self.vf(v, v.scalar_tensor_tensor(st[:, 4:5], st[:, 1:2], SQD, st[:, 3:4],
                                          op0=ALU.mult, op1=ALU.subtract))
        return self.tokd("sDVE")

    def _ln_rstd(self, s):
        st = self.st
        self.af(s, s.activation(st[:, 5:6], st[:, 4:5], AF.Ln, bias=EPS))
        self.af(s, s.activation(st[:, 6:7], st[:, 5:6], AF.Exp, scale=-0.5))

    def _transp16(self, src_bf, dst3, tt, gate, state):
        block = self.block
        pbbf = self.pbbf
        nch = dst3.shape[1]
        per = nch // 2

        for half in range(2):
            @block.tensor
            def _(t, half=half):
                self.W(t, gate, state["trev"][half])
                if not getattr(self, "_identb_ok", False):
                    self.W(t, self.t_wc)
                    self._identb_ok = True
                bank = pbbf[6 + half]
                last = None
                for i in range(per):
                    fc = half * per + i
                    last = t.transpose(bank[:, i * 128:(i + 1) * 128],
                                       src_bf[:, fc * 128:(fc + 1) * 128], self.identb[:, :])
                state["ttok"][half] = self.inc(last, "sPE")

            @block.vector
            def _(v, half=half):
                self.W(v, state["ttok"][half])
                bank = pbbf[6 + half]
                dst = dst3[:, half * per:(half + 1) * per, tt * 128:(tt + 1) * 128]
                state["trev"][half] = self.inc(
                    v.tensor_copy(dst, bank[:, 0:per * 128].rearrange(
                        "p (c n) -> p c n", c=per)), "sDVE")
        return state["ttok"][1]

    def _phaseA(self):
        block = self.block
        self.lnT_tok = {}
        stA = {"ttok": {0: None, 1: None}, "trev": {0: None, 1: None}}
        apply_tok = {}
        tt_pe = {}
        xload = {}
        wtokA = {}

        @block.gpsimd
        def _(g):
            for k in range(2):
                src = [self.Wq, self.Wk][k][:, 0:512]
                for h in range(2):
                    wtokA[k] = self.inc(g.dma_start(
                        self.wslab[k][:, h * 8:(h + 1) * 8, :],
                        src.rearrange("(c p) n -> p c n", p=128)[:, h * 8:(h + 1) * 8, :]),
                        "sWS%d" % k, 16)

        self.Bst = {"slab_pe": {}, "ev_bank": {i: None for i in range(4)}, "pbt": [None]}
        V0 = self.cnt["sDVE"]
        A0 = self.cnt["sACT"]

        @block.scalar
        def _(s, V0=V0):
            for tt in range(NT):
                s.wait_ge(self.sem["sDVE"], V0 + 9 * tt + 4)
                self._ln_rstd(s)
        assert self.cnt["sACT"] == A0 + 2 * NT

        for tt in range(NT):
            @block.sync
            def _(sy, tt=tt):
                if tt >= 1:
                    self.W(sy, apply_tok[tt - 1])
                xload[tt] = self.inc(
                    sy.dma_start(self.xtA[0][:, :], self.x[tt * 128:(tt + 1) * 128, :]),
                    "sX0", 16)

            @block.vector
            def _(v, tt=tt):
                self.W(v, xload[tt])
                if tt >= 2:
                    self.W(v, tt_pe[tt - 2])
                self._ln_stats(v, self.xtA[0][:, :], self.lnA[tt % 2][:, :])
            tv = self.tokd("sDVE")
            assert tv[1] == V0 + 9 * tt + 4
            t_rs = ("sACT", A0 + 2 * (tt + 1))

            @block.vector
            def _(v, tt=tt, t_rs=t_rs):
                self.W(v, t_rs)
                apply_tok[tt] = self.inc(
                    v.tensor_scalar(self.lnA[tt % 2][:, :], self.xtA[0][:, :],
                                    self.st[:, 2:3], self.st[:, 6:7],
                                    op0=ALU.add, op1=ALU.mult), "sDVE")

            tt_pe[tt] = self._transp16(self.lnA[tt % 2], self.ln1T, tt, apply_tok[tt], stA)
            self.lnT_tok[tt] = self.tokd("sDVE")

            for k in range(2):
                bk = tt % 2 if k == 0 else 2 + tt % 2

                @block.tensor
                def _(t, tt=tt, k=k, bk=bk):
                    if tt == 0:
                        self.W(t, wtokA[k])
                    self.W(t, self.lnT_tok[tt], self.Bst["ev_bank"][bk])
                    last = None
                    for fc in range(ND):
                        last = t.matmul(self.pb[bk][:, :],
                                        self.ln1T[:, fc, tt * 128:(tt + 1) * 128],
                                        self.wslab[k][:, fc, :],
                                        start=(fc == 0), stop=(fc == ND - 1))
                    self.Bst["pbt"][0] = self.inc(last, "sPE")
                    if tt == NT - 1:
                        self.Bst["slab_pe"][k] = self.Bst["pbt"][0]

                @block.vector
                def _(v, tt=tt, k=k, bk=bk):
                    self.W(v, self.Bst["pbt"][0])
                    dst = (self.stage if k == 0 else self.stageK)[:, tt, :]
                    self.Bst["ev_bank"][bk] = self.inc(
                        v.tensor_copy(dst, self.pb[bk][:, :]), "sDVE")

        @block.sync
        def _(sy):
            self.W(sy, self.tokd("sDVE"))
            self.inc(sy.dma_start(self.scrQ[0].rearrange("t p n -> p t n"),
                                  self.stage[:, :, :]), "sSPQ", 16)
            self.inc(sy.dma_start(self.scrK[0].rearrange("t p n -> p t n"),
                                  self.stageK[:, :, :]), "sSPQ", 16)
        self.Bst["stage_spill"] = self.tokd("sSPQ")
        self.t_ln1T = self.tokd("sDVE")
        self.t_A_pe = self.tokd("sPE")
        self._dbg_mark("A")

    def _phaseB(self):
        block = self.block
        pb = self.pb
        WQKV = [self.Wq, self.Wk, self.Wv]
        slab_pe = self.Bst["slab_pe"]
        ev_bank = self.Bst["ev_bank"]
        stage_spill = self.Bst["stage_spill"]
        pbt = self.Bst["pbt"]

        for fo in range(4):
            for wi in range(3):
                if fo == 0 and wi < 2:
                    continue
                k = fo * 3 + wi
                wtok = [None]

                @block.gpsimd
                def _(g, fo=fo, wi=wi, k=k):
                    if k >= 2:
                        self.W(g, slab_pe[k - 2])
                    src = WQKV[wi][:, fo * 512:(fo + 1) * 512]
                    snm = "sWS%d" % (k % 2)
                    for h in range(2):
                        wtok[0] = self.inc(g.dma_start(
                            self.wslab[k % 2][:, h * 8:(h + 1) * 8, :],
                            src.rearrange("(c p) n -> p c n", p=128)[:, h * 8:(h + 1) * 8, :]),
                            snm, 16)

                if wi < 2:
                    for tt in range(NT):
                        @block.tensor
                        def _(t, k=k, tt=tt):
                            if tt == 0:
                                self.W(t, wtok[0])
                            if k < 2:
                                self.W(t, self.lnT_tok[tt])
                            bk = tt % 2
                            self.W(t, ev_bank[bk])
                            last = None
                            for fc in range(ND):
                                last = t.matmul(pb[bk][:, :],
                                                self.ln1T[:, fc, tt * 128:(tt + 1) * 128],
                                                self.wslab[k % 2][:, fc, :],
                                                start=(fc == 0), stop=(fc == ND - 1))
                            pbt[0] = self.inc(last, "sPE")
                            if tt == NT - 1:
                                slab_pe[k] = pbt[0]

                        @block.vector
                        def _(v, tt=tt):
                            self.W(v, pbt[0])
                            if tt == 0:
                                self.W(v, stage_spill)
                            ev_bank[tt % 2] = self.inc(
                                v.tensor_copy(self.stage[:, tt, :], pb[tt % 2][:, :]), "sDVE")

                    @block.sync
                    def _(sy, fo=fo, wi=wi):
                        self.W(sy, self.tokd("sDVE"))
                        dst = (self.scrQ if wi == 0 else self.scrK)[fo]
                        self.inc(sy.dma_start(dst.rearrange("t p n -> p t n"),
                                              self.stage[:, :, :]), "sSPQ", 16)
                    stage_spill = self.tokd("sSPQ")
                else:
                    for i in range(16):
                        fc4 = i // 4

                        @block.tensor
                        def _(t, k=k, fo=fo, fc4=fc4, i=i):
                            tb4 = i % 4
                            if i == 0:
                                self.W(t, wtok[0])
                            if k < 3:
                                self.W(t, self.lnT_tok[4 * tb4 + 3])
                            bk = 2 + i % 2
                            self.W(t, ev_bank[bk])
                            last = None
                            for fc in range(ND):
                                last = t.matmul(pb[bk][:, :],
                                                self.wslab[k % 2][:, fc, fc4 * 128:(fc4 + 1) * 128],
                                                self.ln1T[:, fc, tb4 * 512:(tb4 + 1) * 512],
                                                start=(fc == 0), stop=(fc == ND - 1))
                            pbt[0] = self.inc(last, "sPE")
                            if i == 15:
                                slab_pe[k] = pbt[0]

                        @block.vector
                        def _(v, fo=fo, fc4=fc4, i=i):
                            tb4 = i % 4
                            self.W(v, pbt[0])
                            if i == 0 and fo in (0, 3):
                                self.W(v, stage_spill)
                            dst = self.VT[:, fo * 4 + fc4, tb4 * 512:(tb4 + 1) * 512]
                            ev_bank[2 + i % 2] = self.inc(
                                v.tensor_copy(dst, pb[2 + i % 2][:, :]), "sDVE")
        self.t_B_pe = self.tokd("sPE")
        self.t_B_spill = self.tokd("sSPQ")
        self.t_B_vt = self.tokd("sDVE")
        self._dbg_mark("B")

    def _phaseC(self):
        block = self.block
        pb = self.pb
        st = self.st
        t_Kload = [None]

        @block.sync
        def _(sy):
            self.W(sy, self.t_B_pe, self.t_B_spill)
            for fo in range(4):
                for hh in range(2):
                    t_Kload[0] = self.inc(sy.dma_start(
                        self.Kar[:, hh * 8:(hh + 1) * 8, fo * 512:(fo + 1) * 512],
                        self.scrK[fo][hh * 8:(hh + 1) * 8].rearrange("t p n -> p t n")),
                        "sKL", 16)

        qload, qcons = {}, {}
        s_evac = {}
        t_exp = {}
        t_rz = {}
        av_ev = {0: None, 1: None}
        att_sp = None
        stC = {"ttok": {0: None, 1: None}, "trev": {0: None, 1: None}}
        Stok = {}
        avtok = [None]

        def emit_S(ec):
            fo = ec // 4
            if ec % 4 == 0:
                @block.sync
                def _(sy, fo=fo):
                    if fo >= 2:
                        self.W(sy, qcons[fo - 2])
                    for hh in range(2):
                        qload[fo] = self.inc(sy.dma_start(
                            self.qslab[fo % 2][:, hh * 8:(hh + 1) * 8, :],
                            self.scrQ[fo][hh * 8:(hh + 1) * 8].rearrange("t p n -> p t n")),
                            "sQS%d" % (fo % 2), 16)

            @block.tensor
            def _(t, ec=ec, fo=fo):
                self.W(t, qload[fo], t_Kload[0], s_evac.get(ec - 1))
                if ec == 0:
                    self.W(t, self.t_B_vt)
                esub = ec % 4
                last = None
                for fb in range(4):
                    for tt in range(NT):
                        last = t.matmul(pb[fb][:, :],
                                        self.qslab[fo % 2][:, tt, esub * 128:(esub + 1) * 128],
                                        self.Kar[:, tt, fb * 512:(fb + 1) * 512],
                                        start=(tt == 0), stop=(tt == NT - 1))
                Stok[ec] = self.inc(last, "sPE")
                if esub == 3:
                    qcons[fo] = Stok[ec]

        def emit_TAV_mm(ec):
            self._transp16(self.Ptile, self.ATcol, 0, t_exp[ec], stC)
            t_at = self.tokd("sDVE")
            A0 = self.cnt["sACT"]
            avt = []
            for tb4 in range(4):
                @block.tensor
                def _(t, ec=ec, tb4=tb4, t_at=t_at):
                    if tb4 == 0:
                        self.W(t, t_at)
                    bk = 4 + tb4 % 2
                    self.W(t, av_ev[tb4 % 2])
                    last = None
                    for fc in range(ND):
                        last = t.matmul(pb[bk][:, :], self.ATcol[:, fc, :],
                                        self.VT[:, fc, tb4 * 512:(tb4 + 1) * 512],
                                        start=(fc == 0), stop=(fc == ND - 1))
                    avt.append(self.inc(last, "sPE"))
                av_ev[tb4 % 2] = ("sACT", A0 + tb4 + 2)
            return avt, A0

        def emit_scalar(ec_tav, avt, ec_sm):
            nonlocal att_sp
            V0 = self.cnt["sDVE"]

            @block.scalar
            def _(sc, ec_tav=ec_tav, avt=avt, ec_sm=ec_sm, V0=V0):
                if ec_sm is not None:
                    self.W(sc, Stok[ec_sm])
                    if ec_sm >= 1:
                        self.W(sc, t_exp[ec_sm - 1])
                    last = None
                    for fb in range(4):
                        last = sc.activation(self.Srow[:, fb * 512:(fb + 1) * 512],
                                             pb[fb][:, :], AF.Copy, scale=0.125)
                    self.inc(last, "sACT")
                    s_evac[ec_sm] = self.tokd("sACT")
                if avt is not None:
                    for tb4 in range(4):
                        self.W(sc, avt[tb4], t_rz[ec_tav])
                        if tb4 == 0:
                            self.W(sc, att_sp)
                        self.inc(sc.activation(
                            self.attst[:, tb4 * 512:(tb4 + 1) * 512],
                            pb[4 + tb4 % 2][:, :], AF.Copy,
                            scale=self.rzc[:, ec_tav % 2:ec_tav % 2 + 1]), "sACT")
                if ec_sm is not None:
                    sc.wait_ge(self.sem["sDVE"], V0 + 1)
                    if ec_sm >= 1:
                        self.W(sc, stC["ttok"][1])
                    self.inc(sc.activation(self.Ptile[:, :], self.Srow[:, :], AF.Exp,
                                           bias=st[:, 16 + ec_sm % 2:17 + ec_sm % 2],
                                           accum_out=st[:, 18 + ec_sm % 2:19 + ec_sm % 2]),
                             "sACT")
                    t_exp[ec_sm] = self.tokd("sACT")

            if ec_sm is not None:
                @block.vector
                def _(v, ec=ec_sm, V0=V0):
                    self.W(v, s_evac[ec])
                    self.vf(v, v.reduce_max(st[:, 16 + ec % 2:17 + ec % 2], self.Srow[:, :],
                                            axis=AX.X, negate=True))
                    assert self.cnt["sDVE"] == V0 + 1
                    self.W(v, t_exp[ec])
                    self.vf(v, v.reciprocal(self.rzc[:, ec % 2:ec % 2 + 1],
                                            st[:, 18 + ec % 2:19 + ec % 2]))
                    t_rz[ec] = self.tokd("sDVE")

            if avt is not None:
                @block.sync
                def _(sy, ec=ec_tav):
                    self.W(sy, self.tokd("sACT") if ec_sm is None
                           else av_ev[1])
                    self.inc(sy.dma_start(
                        self.scrATT[:, :, ec, :].rearrange("t p n -> p t n"),
                        self.attst.rearrange("p (t n) -> p t n", t=NT)), "sASP", 16)
                att_sp = self.tokd("sASP")

        emit_S(0)
        emit_scalar(None, None, 0)
        for ec in range(1, 16):
            emit_S(ec)
            avt, _ = emit_TAV_mm(ec - 1)
            emit_scalar(ec - 1, avt, ec)
        avt, A0f = emit_TAV_mm(15)
        for b in (0, 1):
            av_ev[b] = ("sACT", av_ev[b][1] - 1)
        emit_scalar(15, avt, None)
        self.t_C_sp = self.tokd("sASP")
        self.t_C_pe = self.tokd("sPE")
        self.t_C_sev = s_evac[15]
        self.t_C_avev = max(av_ev[0], av_ev[1], key=lambda x: x[1])
        self._dbg_mark("C")

    def _phaseD(self):
        block = self.block
        pb = self.pb
        st = self.st
        t_wo = [None]

        @block.gpsimd
        def _(g):
            self.W(g, self.t_C_pe)
            for fo in range(4):
                for hh in range(2):
                    t_wo[0] = self.inc(g.dma_start(
                        self.Wosb[:, hh * 8:(hh + 1) * 8, fo * 512:(fo + 1) * 512],
                        self.Wo[:, fo * 512:(fo + 1) * 512]
                        .rearrange("(c p) n -> p c n", p=128)[:, hh * 8:(hh + 1) * 8, :]),
                        "sWO", 16)

        att_ld, x_ld, x2_sp = {}, {}, {}
        ytok, x2tok, apply2, ttokD = {}, {}, {}, {}
        stD = {"ttok": {0: None, 1: None}, "trev": {0: None, 1: None}}
        V0D = self.cnt["sDVE"]
        A0D = self.cnt["sACT"]

        def emit_loads(tt):
            @block.sync
            def _(sy, tt=tt):
                if tt == 0:
                    self.W(sy, self.t_C_sp)
                if tt >= 2:
                    self.W(sy, ytok[tt - 2])
                att_ld[tt] = self.inc(sy.dma_start(
                    self.attsl[tt % 2][:, :, :], self.scrATT[tt]),
                    "sAL%d" % (tt % 2), 16)

            @block.gpsimd
            def _(g, tt=tt):
                if tt >= 1:
                    self.W(g, ("sDVE", V0D + 15 * (tt - 1) + 8))
                x_ld[tt] = self.inc(g.dma_start(
                    self.xtD[:, :], self.x[tt * 128:(tt + 1) * 128, :]), "sXD", 16)

        def emit_Y(tt):
            @block.tensor
            def _(t, tt=tt):
                self.W(t, att_ld[tt])
                if tt == 0:
                    self.W(t, t_wo[0], self.t_C_sev)
                else:
                    self.W(t, ("sACT", A0D + 7 * (tt - 1) + 1))
                last = None
                for db in range(4):
                    for ec in range(ND):
                        last = t.matmul(pb[db][:, :], self.attsl[tt % 2][:, ec, :],
                                        self.Wosb[:, ec, db * 512:(db + 1) * 512],
                                        start=(ec == 0), stop=(ec == ND - 1))
                ytok[tt] = self.inc(last, "sPE")

        def emit_chain(tt):
            V0 = self.cnt["sDVE"]
            A0 = self.cnt["sACT"]
            assert V0 == V0D + 15 * tt and A0 == A0D + 7 * tt

            @block.scalar
            def _(s, tt=tt, V0=V0):
                self.W(s, ytok[tt])
                if tt >= 1:
                    self.W(s, x2tok[tt - 1])
                if tt >= 2:
                    self.W(s, x2_sp[tt - 2], apply2[tt - 2])
                last = None
                for db in range(4):
                    last = s.activation(self.ytD[:, db * 512:(db + 1) * 512],
                                        pb[db][:, :], AF.Copy,
                                        accum_out=st[:, 8 + db:9 + db])
                self.af(s, last)
                self.inc(s.activation(self.x2tD[tt % 2][:, :], self.ytD[:, :], AF.Square,
                                      accum_out=st[:, 1:2]), "sACT")
                s.wait_ge(self.sem["sDVE"], V0 + 6)
                self.af(s, s.activation(st[:, 5:6], st[:, 4:5], AF.Ln, bias=EPS))
                self.af(s, s.activation(st[:, 6:7], st[:, 5:6], AF.Exp, scale=-0.5))
                s.wait_ge(self.sem["sDVE"], V0 + 8)
                self.inc(s.activation(self.ytD[:, :], self.x2tD[tt % 2][:, :], AF.Square,
                                      accum_out=st[:, 17:18]), "sACT")
                s.wait_ge(self.sem["sDVE"], V0 + 12)
                self.af(s, s.activation(st[:, 21:22], st[:, 20:21], AF.Ln, bias=EPS))
                self.af(s, s.activation(st[:, 22:23], st[:, 21:22], AF.Exp, scale=-0.5))
            t_evsq = ("sACT", A0 + 2)
            t_r = ("sACT", A0 + 4)
            t_sq2 = ("sACT", A0 + 5)
            t_rs2 = ("sACT", A0 + 7)
            assert self.cnt["sACT"] == A0 + 7

            @block.vector
            def _(v, tt=tt, t_evsq=t_evsq):
                self.W(v, t_evsq)
                self.vf(v, v.tensor_add(st[:, 12:13], st[:, 8:9], st[:, 9:10]))
                self.vf(v, v.tensor_add(st[:, 13:14], st[:, 10:11], st[:, 11:12]))
                self.vf(v, v.tensor_add(st[:, 0:1], st[:, 12:13], st[:, 13:14]))
                self.vf(v, v.tensor_scalar(st[:, 2:3], st[:, 0:1], -SQD, None, op0=ALU.mult))
                self.vf(v, v.tensor_mul(st[:, 3:4], st[:, 2:3], st[:, 2:3]))
                self.vf(v, v.scalar_tensor_tensor(st[:, 4:5], st[:, 1:2], SQD, st[:, 3:4],
                                                  op0=ALU.mult, op1=ALU.subtract))

            @block.vector
            def _(v, tt=tt, t_r=t_r):
                self.W(v, t_r, x_ld[tt])
                self.vf(v, v.tensor_scalar(self.x2tD[tt % 2][:, :], self.ytD[:, :],
                                           st[:, 2:3], st[:, 6:7],
                                           op0=ALU.add, op1=ALU.mult))
                self.vf(v, v.tensor_add(self.x2tD[tt % 2][:, :], self.x2tD[tt % 2][:, :],
                                        self.xtD[:, :]))
                x2tok[tt] = self.tokd("sDVE")
            assert x2tok[tt][1] == V0 + 8

            @block.sync
            def _(sy, tt=tt):
                self.W(sy, x2tok[tt])
                x2_sp[tt] = self.inc(
                    sy.dma_start(self.scrX2[tt], self.x2tD[tt % 2][:, :]),
                    "sX2S%d" % (tt % 2), 16)

            @block.vector
            def _(v, tt=tt, t_sq2=t_sq2):
                self.vf(v, v.reduce_sum(st[:, 16:17], self.x2tD[tt % 2][:, :], axis=AX.X))
                self.vf(v, v.tensor_scalar(st[:, 18:19], st[:, 16:17], -SQD, None,
                                           op0=ALU.mult))
                self.vf(v, v.tensor_mul(st[:, 19:20], st[:, 18:19], st[:, 18:19]))
                self.W(v, t_sq2)
                self.vf(v, v.scalar_tensor_tensor(st[:, 20:21], st[:, 17:18], SQD,
                                                  st[:, 19:20],
                                                  op0=ALU.mult, op1=ALU.subtract))
            assert self.cnt["sDVE"] == V0 + 12

            @block.vector
            def _(v, tt=tt, t_rs2=t_rs2):
                self.W(v, t_rs2)
                if tt >= 2:
                    self.W(v, ttokD[tt - 2])
                apply2[tt] = self.inc(
                    v.tensor_scalar(self.lnD[tt % 2][:, :], self.x2tD[tt % 2][:, :],
                                    st[:, 18:19], st[:, 22:23],
                                    op0=ALU.add, op1=ALU.mult), "sDVE")

            ttokD[tt] = self._transp16(self.lnD[tt % 2], self.ln2T, tt, apply2[tt], stD)

        for tt in range(NT):
            emit_loads(tt)
            emit_Y(tt)
            if tt >= 1:
                emit_chain(tt - 1)
        emit_chain(NT - 1)
        self.t_ln2T = self.tokd("sDVE")
        self.t_D_x2sp = [x2_sp[NT - 2], x2_sp[NT - 1]]
        self.t_D_yev = None
        self._dbg_mark("D")

    def _phaseE(self):
        block = self.block
        pb = self.pb
        t_bias1 = [None]

        @block.sync
        def _(sy):
            self.inc(sy.dma_start(self.brows[0:8, :],
                                  self.down_b.rearrange("(c n) -> c n", c=8)), "sBR", 16)
            t_bias1[0] = self.inc(sy.dma_start(
                self.brows[8:16, :],
                self.shared_b1.rearrange("(c n) -> c n", c=8)), "sBR", 16)

        bt = [None]

        @block.tensor
        def _(t):
            self.W(t, t_bias1[0], self.t_C_avev)
            bt[0] = self.inc(t.transpose(pb[5][:, 0:128], self.brows[:, :],
                                         self.identf[:, :]), "sPE")

        @block.vector
        def _(v):
            self.W(v, bt[0])
            v.tensor_copy(self.dwnbT[:, 0:8], pb[5][:, 0:8])
            self.inc(v.tensor_copy(self.sb1T[:, 0:8], pb[5][:, 8:16]), "sDVE")
        t_btv = self.tokd("sDVE")

        self.b1off = {}
        boff = 0
        b1t = {}
        for e in range(8):
            self.b1off[e] = boff

            @block.sync
            def _(sy, e=e):
                if e >= 1:
                    self.W(sy, b1t[e - 1])
                else:
                    self.W(sy, bt[0])
                self.inc(sy.dma_start(self.brows[0:NJ[e], :],
                                      self.eb1[e].rearrange("(c n) -> c n", c=NJ[e])),
                         "sBR", 16)

            @block.tensor
            def _(t, e=e):
                self.W(t, self.tokd("sBR"), self.tokd("sDVE"))
                b1t[e] = self.inc(t.transpose(pb[5][:, 0:128], self.brows[:, :],
                                              self.identf[:, :]), "sPE")

            @block.vector
            def _(v, e=e, boff=boff):
                self.W(v, b1t[e])
                self.inc(v.tensor_copy(self.b1T[:, boff:boff + NJ[e]],
                                       pb[5][:, 0:NJ[e]]), "sDVE")
            boff += NJ[e]
        t_b1 = self.tokd("sDVE")

        @block.scalar
        def _(s):
            self.W(s, t_b1)
            self.inc(s.activation(self.rT[:, 0:160], self.b1T[:, 0:160], AF.Gelu), "sACT")
        self.t_rT = self.tokd("sACT")

        gcons = {}
        gt = [None]
        gev = {0: self.t_D_yev, 1: self.t_D_yev}
        dwl = {}
        S0 = self.cnt["sG2S"]
        g2src = self.g2st.rearrange("p (t n) -> p t n", t=NT)
        for lh in range(2):
            @block.gpsimd
            def _(g, lh=lh):
                if lh == 0:
                    self.W(g, self.t_ln2T, self.t_D_x2sp[0], self.t_D_x2sp[1])
                else:
                    self.W(g, gcons[0])
                for hh in range(2):
                    dwl[lh] = self.inc(g.dma_start(
                        self.dwsl[:, hh * 8:(hh + 1) * 8, :],
                        self.down_w[:, lh * 512:(lh + 1) * 512]
                        .rearrange("(c p) n -> p c n", p=128)[:, hh * 8:(hh + 1) * 8, :]),
                        "sDW", 16)

            A0 = self.cnt["sACT"]
            order = []
            for i in range(16):
                order.append(("ev", i))
                if i % 2 == 1:
                    order.append(("g2", lh * 8 + i // 2))
            tok_of = {it: ("sACT", A0 + pos + 1) for pos, it in enumerate(order)}

            gtl = []
            for i in range(16):
                lc4, tb4 = i // 4, i % 4

                @block.tensor
                def _(t, lh=lh, lc4=lc4, tb4=tb4, i=i):
                    if i == 0:
                        self.W(t, dwl[lh])
                    bk = i % 2
                    self.W(t, gev[bk])
                    last = None
                    for fc in range(ND):
                        last = t.matmul(pb[bk][:, :],
                                        self.dwsl[:, fc, lc4 * 128:(lc4 + 1) * 128],
                                        self.ln2T[:, fc, tb4 * 512:(tb4 + 1) * 512],
                                        start=(fc == 0), stop=(fc == ND - 1))
                    gtl.append(self.inc(last, "sPE"))
                    if i == 15:
                        gcons[lh] = gtl[-1]
                gev[i % 2] = tok_of[("ev", i)]

            @block.scalar
            def _(sc, lh=lh, gtl=gtl, order=order, S0=S0):
                for it in order:
                    if it[0] == "ev":
                        i = it[1]
                        lc4, tb4 = i // 4, i % 4
                        Lc = lh * 4 + lc4
                        self.W(sc, gtl[i])
                        if lh == 0 and i == 0:
                            self.W(sc, t_btv, self.t_D_x2sp[0], self.t_D_x2sp[1],
                                   self.t_ln2T)
                        self.inc(sc.activation(
                            self.GT[:, Lc, tb4 * 512:(tb4 + 1) * 512], pb[i % 2][:, :],
                            AF.Gelu, bias=self.dwnbT[:, Lc:Lc + 1]), "sACT")
                    else:
                        fc = it[1]
                        if fc >= 1:
                            sc.wait_ge(self.sem["sG2S"], S0 + 16 * fc)
                        self.inc(sc.activation(self.g2st[:, :], self.ln2T[:, fc, :],
                                               AF.Gelu), "sACT")
            assert self.cnt["sACT"] == A0 + 24

            @block.sync
            def _(sy, order=order, tok_of=tok_of):
                for it in order:
                    if it[0] != "g2":
                        continue
                    fc = it[1]
                    self.W(sy, tok_of[it])
                    self.inc(sy.dma_start(
                        self.scrG2[:, :, fc, :].rearrange("t p n -> p t n"),
                        g2src), "sG2S", 16)
        self.t_g2 = self.tokd("sG2S")
        assert self.t_g2[1] == S0 + 16 * ND
        self.t_GT = self.tokd("sACT")


        lgev = {}
        lgt = [None]
        for tt in range(NT):
            @block.tensor
            def _(t, tt=tt):
                if tt == 0:
                    self.W(t, self.t_wc, self.t_C_avev)
                bk = 4 + tt % 2
                if tt >= 2:
                    self.W(t, lgev[tt - 2])
                last = None
                for fc in range(ND):
                    last = t.matmul(pb[bk][:, 0:8],
                                    self.ln2T[:, fc, tt * 128:(tt + 1) * 128],
                                    self.rwsb[:, fc, :], start=(fc == 0), stop=(fc == ND - 1))
                lgt[0] = self.inc(last, "sPE")

            @block.vector
            def _(v, tt=tt):
                self.W(v, lgt[0])
                if tt == 0:
                    self.W(v, self.t_rbbc)
                lgev[tt] = self.inc(
                    v.tensor_add(self.RW[:, tt, 0, :], pb[4 + tt % 2][:, 0:8],
                                 self.rbbc[:, :]), "sDVE")
        t_logit = self.tokd("sDVE")

        RW, rst = self.RW, self.rst

        def bc(col):
            base = rst[:, :, col:col + 1]
            return bass.AP(base.tensor, base.offset,
                           [list(base.ap[0]), list(base.ap[1]), [0, 8]])

        names = ["LG", "PR", "GE", "C1", "C2", "C3", "LE", "M1", "M2", "P2", "AL", "EX", "EX2"]
        S = {nm: RW[:, :, i, :] for i, nm in enumerate(names)}
        t_mid = [None]

        @block.vector
        def _(v):
            f = lambda inst: self.vf(v, inst)
            self.W(v, t_logit)
            f(v.reduce_max(rst[:, :, 0:1], S["LG"], axis=AX.X, negate=True))
            f(v.tensor_tensor(S["EX"], S["LG"], bc(0), op=ALU.add))
            t_mid[0] = self.tokd("sDVE")

        @block.scalar
        def _(s):
            self.W(s, t_mid[0])
            self.inc(s.activation(S["EX2"], S["EX"], AF.Exp), "sACT")
        t_e = self.tokd("sACT")

        @block.vector
        def _(v):
            f = lambda inst: self.vf(v, inst)
            self.W(v, t_e)
            f(v.reduce_sum(rst[:, :, 1:2], S["EX2"], axis=AX.X))
            f(v.reciprocal(rst[:, :, 2:3], rst[:, :, 1:2]))
            f(v.tensor_tensor(S["PR"], S["EX2"], bc(2), op=ALU.mult))

            def firstmax(dst, src, vcol):
                f(v.reduce_max(rst[:, :, vcol:vcol + 1], src, axis=AX.X))
                f(v.tensor_tensor(S["GE"], src, bc(vcol), op=ALU.is_ge))
                f(v.tensor_copy(S["C1"][:, :, 0:1], S["GE"][:, :, 0:1]))
                f(v.tensor_add(S["C1"][:, :, 1:8], S["GE"][:, :, 1:8], S["GE"][:, :, 0:7]))
                f(v.tensor_copy(S["C2"][:, :, 0:2], S["C1"][:, :, 0:2]))
                f(v.tensor_add(S["C2"][:, :, 2:8], S["C1"][:, :, 2:8], S["C1"][:, :, 0:6]))
                f(v.tensor_copy(S["C3"][:, :, 0:4], S["C2"][:, :, 0:4]))
                f(v.tensor_add(S["C3"][:, :, 4:8], S["C2"][:, :, 4:8], S["C2"][:, :, 0:4]))
                f(v.tensor_single_scalar(S["LE"], S["C3"], 1.0, op=ALU.is_le))
                f(v.tensor_mul(dst, S["GE"], S["LE"]))

            firstmax(S["M1"], S["PR"], 3)
            f(v.scalar_tensor_tensor(S["P2"], S["M1"], NEG_BIG, S["PR"],
                                     op0=ALU.mult, op1=ALU.add))
            firstmax(S["M2"], S["P2"], 4)
            f(v.tensor_sub(rst[:, :, 5:6], rst[:, :, 4:5], rst[:, :, 3:4]))
            t_mid[0] = self.tokd("sDVE")

        @block.scalar
        def _(s):
            self.W(s, t_mid[0])
            self.inc(s.activation(rst[:, :, 6:7], rst[:, :, 5:6], AF.Exp), "sACT")
        t_e2 = self.tokd("sACT")

        @block.vector
        def _(v):
            f = lambda inst: self.vf(v, inst)
            self.W(v, t_e2)
            f(v.tensor_single_scalar(rst[:, :, 7:8], rst[:, :, 6:7], 1.0, op=ALU.add))
            f(v.reciprocal(rst[:, :, 8:9], rst[:, :, 7:8]))
            f(v.tensor_mul(rst[:, :, 9:10], rst[:, :, 6:7], rst[:, :, 8:9]))
            f(v.tensor_add(rst[:, :, 10:11], rst[:, :, 8:9], rst[:, :, 9:10]))
            f(v.tensor_tensor(S["GE"], S["M1"], bc(8), op=ALU.mult))
            f(v.tensor_tensor(S["M2"], S["M2"], bc(9), op=ALU.mult))
            f(v.tensor_add(S["AL"], S["GE"], S["M2"]))
            f(v.tensor_copy(S["EX"][:, :, 0:1], rst[:, :, 10:11]))
        t_alpha = self.tokd("sDVE")

        agev = {}
        agt = [None]
        for tt in range(NT):
            @block.tensor
            def _(t, tt=tt):
                if tt == 0:
                    self.W(t, t_alpha, gcons[1])
                bk = 4 + tt % 2
                if tt < 2:
                    self.W(t, lgev[14 + tt])
                else:
                    self.W(t, agev[tt - 2])
                base = self.RW[:, tt, 10, 0:1]
                ag9 = bass.AP(base.tensor, base.offset, [list(base.ap[0]), [1, 9]])
                agt[0] = self.inc(
                    t.transpose(pb[bk][0:9, 0:128], ag9, self.identf[:, :]), "sPE")

            @block.vector
            def _(v, tt=tt):
                self.W(v, agt[0])
                if tt == 0:
                    self.W(v, self.t_GT)
                agev[tt] = self.inc(
                    v.tensor_copy(self.agT9[0:9, tt * 128:(tt + 1) * 128],
                                  pb[4 + tt % 2][0:9, 0:128]), "sDVE")
        self.t_agT = self.tokd("sDVE")

        @block.gpsimd
        def _(g):
            self.W(g, self.t_agT)
            self.t_agsp = self.inc(g.dma_start(self.scrAG, self.agT9[:, :]), "sAGS", 16)
        self.t_E_pe = self.tokd("sPE")
        self.t_E_gcons = gcons[1]
        self._dbg_mark("E")

    def _phaseF(self):
        block = self.block
        pb = self.pb
        zb_ev = {0: self.t_agT, 1: self.t_agT}
        p_act = {0: self.t_GT, 1: self.t_GT}
        wdve = {0: None, 1: None}
        w1gate = {0: None, 1: None}
        w2free = {0: self.t_E_pe, 1: self.t_E_pe}
        gp = [0]
        w1c = [0]
        w1slot = {0: None, 1: None, 2: None}
        w1of = {}
        zbt = [None]
        wcons = {}
        pend = [None]
        self._shev = {0: None, 1: None}

        t_bc = [None]

        @block.sync
        def _(sy):
            self.W(sy, self.t_g2)
            t_bc[0] = self.inc(sy.dma_start(
                self.B2bc[:, :], self.eb2[0][None, :].to_broadcast([128, L])), "sBC", 16)
        for e in range(1, 8):
            @block.sync
            def _(sy, e=e):
                self.W(sy, t_bc[0], self.tokd("sDVE"))
                if e == 1:
                    self.W(sy, self.t_agsp)
                t_bc[0] = self.inc(sy.dma_start(
                    self.xtraF[:, 0:L], self.eb2[e][None, :].to_broadcast([128, L])),
                    "sBC", 16)

            @block.vector
            def _(v, e=e):
                self.W(v, t_bc[0])
                self.inc(v.tensor_add(self.B2bc[:, :], self.B2bc[:, :],
                                      self.xtraF[:, 0:L]), "sDVE")
        t_b2bc = self.tokd("sDVE")

        gs_ld = {}
        abc_ld = {}
        t_sb2 = [None]
        for tb in range(NTB):
            @block.sync
            def _(sy, tb=tb):
                self.W(sy, self.t_agsp, self.tokd("sDVE"))
                gs_ld[tb] = self.inc(sy.dma_start(
                    self.gsbc[:, :],
                    self.scrAG[8:9, tb * TB:(tb + 1) * TB].to_broadcast([128, TB])),
                    "sABC", 16)

            for e in range(9):
                njc = NJ[e] if e < 8 else 8
                groups = [(g0, min(g0 + JG, njc)) for g0 in range(0, njc, JG)]
                if e < 8:
                    @block.sync
                    def _(sy, tb=tb, e=e):
                        self.W(sy, self.tokd("sDVE"))
                        abc_ld[(tb, e)] = self.inc(sy.dma_start(
                            self.abc[:, :],
                            self.scrAG[e:e + 1, tb * TB:(tb + 1) * TB]
                            .to_broadcast([128, TB])), "sABC", 16)

                    @block.vector
                    def _(v, tb=tb, e=e):
                        self.W(v, abc_ld[(tb, e)], gs_ld[tb])
                        self.inc(v.scalar_tensor_tensor(self.bbc[:, :], self.abc[:, :], -1.0,
                                                        self.gsbc[:, :], op0=ALU.mult,
                                                        op1=ALU.add), "sDVE")
                    t_ab = self.tokd("sDVE")
                else:
                    @block.sync
                    def _(sy, tb=tb):
                        self.W(sy, self.tokd("sDVE"))
                        t_sb2[0] = self.inc(sy.dma_start(
                            self.xtraF[:, 0:L],
                            self.shared_b2[None, :].to_broadcast([128, L])), "sBC", 16)
                    t_ab = None

                for (g0, g1) in groups:
                    nj = g1 - g0
                    assert nj == JG
                    w1s = self.ew1[e] if e < 8 else self.shared_w1
                    w2s = self.ew2[e] if e < 8 else self.shared_w2
                    t_w = {}
                    T_base = self.cnt["sPE"]
                    nb = NTI if pend[0] is not None else 0

                    def pred_pe(j, g0=g0, T_base=T_base, nb=nb):
                        d = j - g0 + 1
                        return ("sPE", T_base + d + (nb if d > 2 else 0))

                    def w1load(j, tb=tb, e=e, g0=g0, w1s=w1s, t_w=t_w):
                        slot = w1c[0] % 3
                        w1c[0] += 1
                        w1of[(tb, e, j)] = slot

                        @block.gpsimd
                        def _(g, j=j, slot=slot):
                            if w1slot[slot] is not None:
                                self.W(g, w1slot[slot])
                            elif slot == 2:
                                self.W(g, self.t_agsp, self.t_E_gcons)
                            if tb == 0 and e == 0 and j == 0:
                                self.W(g, self.t_E_gcons)
                            t_w[j] = self.inc(g.dma_start(
                                self.w1sl[slot][:, :, :], w1s[j]),
                                "sW1%d" % slot, 16)

                    def pmm(j, pred_act, tb=tb, e=e, g0=g0, pred_pe=pred_pe,
                            t_w=t_w):
                        slot = w1of[(tb, e, j)]

                        @block.tensor
                        def _(t, j=j, slot=slot):
                            self.W(t, t_w[j])
                            if j - 2 >= g0:
                                self.W(t, pred_act(j - 2))
                            else:
                                self.W(t, p_act[j % 2])
                            bk = (j % 2) * 2
                            last = None
                            for h in range(2):
                                for Lc in range(NL):
                                    last = t.matmul(
                                        pb[bk + h][:, :], self.w1sl[slot][:, Lc, :],
                                        self.GT[:, Lc,
                                                tb * TB + h * 512:tb * TB + (h + 1) * 512],
                                        start=(Lc == 0), stop=(Lc == NL - 1))
                            wcons[(tb, e, j)] = self.inc(last, "sPE")
                            assert wcons[(tb, e, j)] == pred_pe(j), \
                                (wcons[(tb, e, j)], pred_pe(j))
                            w1gate[j % 2] = wcons[(tb, e, j)]
                        w1slot[slot] = wcons[(tb, e, j)]

                    w1load(g0)
                    w1load(g0 + 1)
                    pmm(g0, None)
                    pmm(g0 + 1, None)
                    if pend[0] is not None:
                        pend[0]()
                        pend[0] = None

                    half = gp[0] % 2
                    woff = half * JG
                    A_base = self.cnt["sACT"]
                    V_base = self.cnt["sDVE"]

                    def pred_act(j, g0=g0, A_base=A_base):
                        return ("sACT", A_base + (j - g0 + 1))

                    def pred_dve(j, g0=g0, V_base=V_base):
                        return ("sDVE", V_base + 2 * (j - g0 + 1))

                    def w2load(j, jj, tb=tb, e=e, half=half, woff=woff, w2s=w2s):
                        @block.gpsimd
                        def _(g, j=j, jj=jj):
                            if jj == 0:
                                self.W(g, w2free[half])
                            if tb == 0 and e == 0 and j == 0:
                                self.W(g, self.t_g2)
                            self.inc(g.dma_start(
                                self.w2grp[:, woff + jj, :],
                                w2s[j * 128:(j + 1) * 128, :]),
                                "sW2%d" % half, 16)

                    for j in range(g0 + 2, g1):
                        w1load(j)
                        w2load(j - 2, j - 2 - g0)
                        pmm(j, pred_act)
                    w2load(g1 - 2, g1 - 2 - g0)
                    w2load(g1 - 1, g1 - 1 - g0)

                    @block.scalar
                    def _(s, tb=tb, e=e, g0=g0, g1=g1, woff=woff,
                          pred_act=pred_act, pred_dve=pred_dve):
                        for j in range(g0, g1):
                            jj = j - g0
                            self.W(s, wcons[(tb, e, j)])
                            if e < 8:
                                if j - 2 >= g0:
                                    self.W(s, pred_dve(j - 2))
                                elif wdve[j % 2]:
                                    self.W(s, wdve[j % 2])
                            if tb == 0 and e == 0 and j == 0:
                                self.W(s, self.t_rT, self.t_agsp)
                            bk = (j % 2) * 2
                            bias = (self.b1T[:, self.b1off[e] + j:self.b1off[e] + j + 1]
                                    if e < 8 else self.sb1T[:, j:j + 1])
                            dst = (self.stl[j % 2][:, :] if e < 8
                                   else self.wtil[:, woff + jj, :])
                            tk = self.inc(s.activation(dst, self.pq[bk // 2][:, :],
                                                       AF.Gelu, bias=bias), "sACT")
                            assert tk == pred_act(j), (tk, pred_act(j))
                            p_act[j % 2] = tk

                    if e < 8:
                        for j in range(g0, g1):
                            jj = j - g0

                            @block.vector
                            def _(v, tb=tb, e=e, j=j, jj=jj, g0=g0, t_ab=t_ab,
                                  woff=woff, pred_act=pred_act, pred_dve=pred_dve):
                                self.W(v, pred_act(j))
                                if j == g0:
                                    self.W(v, t_ab, wdve[0], wdve[1])
                                else:
                                    self.W(v, pred_dve(j - 1))
                                if tb == 0 and e == 0 and j == 0:
                                    self.W(v, self.t_rT)
                                self.vf(v, v.tensor_mul(self.xtraB[:, :], self.abc[:, :],
                                                        self.stl[j % 2][:, :]))
                                tk = self.inc(
                                    v.scalar_tensor_tensor(
                                        self.wtil[:, woff + jj, :], self.bbc[:, :],
                                        self.rT[:, self.b1off[e] + j:self.b1off[e] + j + 1],
                                        self.xtraB[:, :], op0=ALU.mult, op1=ALU.add),
                                    "sDVE")
                                assert tk == pred_dve(j), (tk, pred_dve(j))
                                wdve[j % 2] = tk
                    t_wgrp = self.tokd("sDVE") if e < 8 else self.tokd("sACT")
                    t_w2grp = self.tokd("sW2%d" % half)

                    def emit_pass2(tb=tb, e=e, g0=g0, nj=nj, half=half, woff=woff,
                                   t_wgrp=t_wgrp, t_w2grp=t_w2grp):
                        for ti in range(NTI):
                            tt = tb * NTI + ti

                            @block.tensor
                            def _(t, ti=ti):
                                if ti == 0:
                                    self.W(t, t_wgrp, t_w2grp)
                                bk = 4 + (ti % 2) * 2
                                self.W(t, zb_ev[ti % 2])
                                last = None
                                for h in range(2):
                                    for jj in range(nj):
                                        last = t.matmul(
                                            pb[bk + h][:, :],
                                            self.wtil[:, woff + jj,
                                                      ti * 128:(ti + 1) * 128],
                                            self.w2grp[:, woff + jj,
                                                       h * 512:(h + 1) * 512],
                                            start=(jj == 0), stop=(jj == nj - 1))
                                zbt[0] = self.inc(last, "sPE")

                            if e < 8:
                                @block.vector
                                def _(v, ti=ti, tt=tt):
                                    self.W(v, zbt[0])
                                    if tb == 0 and e == 0 and g0 == 0 and ti == 0:
                                        self.W(v, t_b2bc)
                                    bk = 4 + (ti % 2) * 2
                                    z2 = self.z[:, tt, :].rearrange("p (a c) -> p a c",
                                                                    a=2)
                                    last = None
                                    for h in range(2):
                                        if e == 0 and g0 == 0:
                                            last = v.scalar_tensor_tensor(
                                                z2[:, h, :],
                                                self.B2bc[:, h * 512:(h + 1) * 512],
                                                self.rst[:, tt, 10:11], pb[bk + h][:, :],
                                                op0=ALU.mult, op1=ALU.add)
                                        else:
                                            last = v.tensor_add(z2[:, h, :], z2[:, h, :],
                                                                pb[bk + h][:, :])
                                    zb_ev[ti % 2] = self.inc(last, "sDVE")
                            else:
                                @block.scalar
                                def _(s, ti=ti):
                                    self.W(s, zbt[0], self._shev[ti % 2])
                                    bk = 4 + (ti % 2) * 2
                                    last = None
                                    for h in range(2):
                                        last = s.activation(
                                            self.stl[ti % 2][:, h * 512:(h + 1) * 512],
                                            pb[bk + h][:, :], AF.Copy, scale=0.1)
                                    self.inc(last, "sACT")
                                t_sh = self.tokd("sACT")

                                @block.vector
                                def _(v, ti=ti, tt=tt, t_sh=t_sh):
                                    self.W(v, t_sh, t_sb2[0])
                                    z2 = self.z[:, tt, :].rearrange("p (a c) -> p a c",
                                                                    a=2)
                                    x2 = self.xtraF[:, 0:L].rearrange("p (a c) -> p a c",
                                                                      a=2)
                                    last = None
                                    for h in range(2):
                                        last = v.tensor_add(
                                            z2[:, h, :], z2[:, h, :],
                                            self.stl[ti % 2][:, h * 512:(h + 1) * 512])
                                    self.vf(v, last)
                                    last = None
                                    for h in range(2):
                                        last = v.scalar_tensor_tensor(
                                            z2[:, h, :], x2[:, h, :], 0.1, z2[:, h, :],
                                            op0=ALU.mult, op1=ALU.add)
                                    zb_ev[ti % 2] = self.inc(last, "sDVE")
                                    self._shev[ti % 2] = zb_ev[ti % 2]
                        w2free[half] = zbt[0]
                        if e == 8:
                            wdve[0] = wdve[1] = self.tokd("sDVE")
                        gp[0] += 1
                    pend[0] = emit_pass2
        if pend[0] is not None:
            pend[0]()
            pend[0] = None
        self.t_F_z = self.tokd("sDVE")
        self.t_F_pe = self.tokd("sPE")
        self._dbg_mark("F")

    def _phaseG(self):
        block = self.block
        pb, pbbf = self.pb, self.pbbf
        st = self.st
        trev = {0: self.t_F_z, 1: self.t_F_z}
        zc_tok = {}
        ztt = {}
        P0 = self.cnt["sPE"]
        A0 = self.cnt["sACT"]

        @block.scalar
        def _(s, P0=P0):
            self.W(s, self.t_F_z)
            for tt in range(NT):
                if tt >= 1:
                    s.wait_ge(self.sem["sPE"], P0 + tt)
                self.inc(s.activation(self.zst[:, :], self.z[:, tt, :], AF.Copy), "sACT")
        for tt in range(NT):
            zc_tok[tt] = ("sACT", A0 + tt + 1)
        assert self.cnt["sACT"] == A0 + NT

        for tt in range(NT):
            @block.tensor
            def _(t, tt=tt):
                self.W(t, zc_tok[tt], trev[tt % 2])
                bank = pbbf[6 + tt % 2]
                last = None
                for Lc in range(NL):
                    last = t.transpose(bank[:, Lc * 128:(Lc + 1) * 128],
                                       self.zst[:, Lc * 128:(Lc + 1) * 128], self.identb[:, :])
                ztt[tt] = self.inc(last, "sPE")
                assert ztt[tt][1] == P0 + tt + 1

            @block.vector
            def _(v, tt=tt):
                self.W(v, ztt[tt])
                bank = pbbf[6 + tt % 2]
                trev[tt % 2] = self.inc(
                    v.tensor_copy(self.z16T[:, :, tt * 128:(tt + 1) * 128],
                                  bank[:, 0:NL * 128].rearrange("p (c n) -> p c n", c=NL)),
                    "sDVE")
        t_z16 = self.tokd("sDVE")
        t_zcast = zc_tok[NT - 1]

        t_gw = [None]

        @block.gpsimd
        def _(g):
            self.W(g, self.t_F_pe)
            for fo in range(4):
                for hh in range(2):
                    self.inc(g.dma_start(
                        self.corew[:, hh * 8:(hh + 1) * 8, fo * 512:(fo + 1) * 512],
                        self.core_w[:, fo * 512:(fo + 1) * 512]
                        .rearrange("(c p) n -> p c n", p=128)[:, hh * 8:(hh + 1) * 8, :]),
                        "sGW", 16)
            self.W(g, zc_tok[NL - 1])
            for hh in range(2):
                t_gw[0] = self.inc(g.dma_start(
                    self.upw[:, hh * 4:(hh + 1) * 4, :],
                    self.up_w.rearrange("(c p) n -> p c n", p=128)[:, hh * 4:(hh + 1) * 4, :]),
                    "sGW", 16)

        t_budld = [None]

        @block.sync
        def _(sy):
            self.W(sy, self.t_F_z, t_zcast)
            self.inc(sy.dma_start(self.B_ud[:, :],
                                  self.up_b[None, :].to_broadcast([128, D])), "sBUD", 16)
            t_budld[0] = self.inc(sy.dma_start(
                self.y2t[:, :], self.core_b[None, :].to_broadcast([128, D])), "sBUD", 16)

        @block.vector
        def _(v):
            self.W(v, t_budld[0])
            self.inc(v.tensor_add(self.B_ud[:, :], self.B_ud[:, :], self.y2t[:, :]), "sDVE")
        t_bud = self.tokd("sDVE")

        g2ld, x2ld, out_tok, out_st = {}, {}, {}, {}
        gy = {}
        y_ev = None
        y_evs = {}
        V1 = self.cnt["sDVE"]
        A1 = self.cnt["sACT"]

        @block.scalar
        def _(s, V1=V1):
            for tt in range(NT):
                s.wait_ge(self.sem["sDVE"], V1 + 9 * tt + 1)
                self.inc(s.activation(self.sqbG[:, :], self.y2t[:, :], AF.Square,
                                      accum_out=st[:, 1:2]), "sACT")
                s.wait_ge(self.sem["sDVE"], V1 + 9 * tt + 7)
                self._ln_rstd(s)
        assert self.cnt["sACT"] == A1 + 3 * NT

        for tt in range(NT):
            @block.sync
            def _(sy, tt=tt):
                if tt == 0:
                    self.W(sy, self.t_g2, self.t_F_z, t_zcast)
                if tt >= 2:
                    self.W(sy, gy[tt - 2])
                g2ld[tt] = self.inc(sy.dma_start(
                    self.g2sl[tt % 2][:, :, :], self.scrG2[tt]),
                    "sGL%d" % (tt % 2), 16)
                if tt < 2:
                    x2ld[tt] = self.inc(sy.dma_start(
                        self.x2tG[tt % 2][:, :], self.scrX2[tt]),
                        "sXG%d" % (tt % 2), 16)

            if tt >= 2:
                @block.gpsimd
                def _(g, tt=tt):
                    self.W(g, gy[tt - 2], out_st[tt - 2])
                    x2ld[tt] = self.inc(g.dma_start(
                        self.x2tG[tt % 2][:, :], self.scrX2[tt]),
                        "sXB%d" % (tt % 2), 16)

            @block.tensor
            def _(t, tt=tt):
                bq = (tt % 2) * 4
                self.W(t, y_evs.get(tt - 2))
                if tt <= 1:
                    self.W(t, t_gw[0], t_z16, t_zcast)
                for db in range(4):
                    for Lc in range(NL):
                        t.matmul(pb[bq + db][:, :],
                                 self.z16T[:, Lc, tt * 128:(tt + 1) * 128],
                                 self.upw[:, Lc, db * 512:(db + 1) * 512],
                                 start=(Lc == 0), stop=False)
                self.W(t, g2ld[tt])
                last = None
                for db in range(4):
                    for fc in range(ND):
                        last = t.matmul(pb[bq + db][:, :],
                                        self.g2sl[tt % 2][:, fc, :],
                                        self.corew[:, fc, db * 512:(db + 1) * 512],
                                        start=False, stop=(fc == ND - 1))
                gy[tt] = self.inc(last, "sPE")

            @block.vector
            def _(v, tt=tt):
                bq = (tt % 2) * 4
                self.W(v, gy[tt])
                if tt == 0:
                    self.W(v, t_bud)
                last = None
                for db in range(4):
                    last = v.scalar_tensor_tensor(
                        self.y2t[:, db * 512:(db + 1) * 512], pb[bq + db][:, :], 1.0,
                        self.B_ud[:, db * 512:(db + 1) * 512],
                        op0=ALU.mult, op1=ALU.add, accum_out=st[:, 8 + db:9 + db])
                self.vf(v, last)
            y_ev = self.tokd("sDVE")
            y_evs[tt] = y_ev
            assert y_ev[1] == V1 + 9 * tt + 1
            t_sq = ("sACT", A1 + 3 * tt + 1)

            @block.vector
            def _(v, tt=tt, t_sq=t_sq):
                st_ = st
                self.vf(v, v.tensor_add(st_[:, 12:13], st_[:, 8:9], st_[:, 9:10]))
                self.vf(v, v.tensor_add(st_[:, 13:14], st_[:, 10:11], st_[:, 11:12]))
                self.vf(v, v.tensor_add(st_[:, 0:1], st_[:, 12:13], st_[:, 13:14]))
                self.vf(v, v.tensor_scalar(st_[:, 2:3], st_[:, 0:1], -SQD, None, op0=ALU.mult))
                self.vf(v, v.tensor_mul(st_[:, 3:4], st_[:, 2:3], st_[:, 2:3]))
                self.W(v, t_sq)
                self.vf(v, v.scalar_tensor_tensor(st_[:, 4:5], st_[:, 1:2], SQD, st_[:, 3:4],
                                                  op0=ALU.mult, op1=ALU.subtract))
            tv = self.tokd("sDVE")
            assert tv[1] == V1 + 9 * tt + 7
            t_r = ("sACT", A1 + 3 * tt + 3)

            @block.vector
            def _(v, tt=tt, t_r=t_r):
                self.W(v, t_r, x2ld[tt])
                self.vf(v, v.tensor_scalar(self.outt[:, :], self.y2t[:, :],
                                           st[:, 2:3], st[:, 6:7],
                                           op0=ALU.add, op1=ALU.mult))
                out_tok[tt] = self.inc(
                    v.tensor_add(self.x2tG[tt % 2][:, :], self.x2tG[tt % 2][:, :],
                                 self.outt[:, :]), "sDVE")

            @block.sync
            def _(sy, tt=tt):
                self.W(sy, out_tok[tt])
                out_st[tt] = self.inc(
                    sy.dma_start(self.out[tt * 128:(tt + 1) * 128, :],
                                 self.x2tG[tt % 2][:, :]), "sOU%d" % (tt % 2), 16)
        self._dbg_mark("G")

    def _dbg_mark(self, phase):
        for (name, ph, view_fn) in self.dbg:
            if ph == phase:
                self.dbg_done.append((name, view_fn, dict(self.cnt)))

    def _dbg_flush(self):
        block, nc = self.block, self.nc

        @block.sync
        def _(sy):
            for (name, view_fn, waits) in self.dbg_done:
                for snm, vv in waits.items():
                    if vv > 0 and snm != "sOUT":
                        sy.wait_ge(self.sem[snm], vv)
                ap = view_fn(self)
                dst = nc.dram_tensor("dbg_" + name, list(ap.shape), ap.dtype,
                                     kind="ExternalOutput").ap()
                self.inc(sy.dma_start(dst, ap), "sOUT", 16)

    def _final_wait(self):
        block = self.block

        @block.sync
        def _(sy):
            for nm in ["sOUT", "sOU0", "sOU1"]:
                if self.cnt[nm] > 0:
                    sy.wait_ge(self.sem[nm], self.cnt[nm])


def _digest_w1(w):
    Lr, H = w.shape
    nj = H // 128
    return np.ascontiguousarray(
        np.asarray(w, np.float32).reshape(Lr // 128, 128, nj, 128)
        .transpose(2, 1, 0, 3))


def _core_inputs(inputs):
    m = {
        "Wq": inputs["L0_attn_Wq"], "Wk": inputs["L0_attn_Wk"],
        "Wv": inputs["L0_attn_Wv"], "Wo": inputs["L0_attn_Wo"],
        "down_w": inputs["L0_ffn_down_w"], "down_b": inputs["L0_ffn_down_b"],
        "up_w": inputs["L0_ffn_up_w"], "up_b": inputs["L0_ffn_up_b"],
        "router_w": inputs["L0_ffn_router_w"], "router_b": inputs["L0_ffn_router_b"],
        "shared_w1": _digest_w1(inputs["L0_ffn_shared_w1"]),
        "shared_b1": inputs["L0_ffn_shared_b1"],
        "shared_w2": inputs["L0_ffn_shared_w2"], "shared_b2": inputs["L0_ffn_shared_b2"],
        "core_w": inputs["L0_ffn_core_w"], "core_b": inputs["L0_ffn_core_b"],
    }
    for e in range(8):
        m[f"e{e}_w1"] = _digest_w1(inputs[f"L0_ffn_e{e}_w1"])
        m[f"e{e}_b1"] = inputs[f"L0_ffn_e{e}_b1"]
        m[f"e{e}_w2"] = inputs[f"L0_ffn_e{e}_w2"]
        m[f"e{e}_b2"] = inputs[f"L0_ffn_e{e}_b2"]
    m["ident"] = np.eye(128, dtype=np.float32)
    cost = np.array([1024 * h + h * 1024 for h in HID], np.float32)
    m["negcost"] = (-COST_LAMBDA * cost)[None, :]
    return {k: np.ascontiguousarray(np.asarray(v, dtype=np.float32)) for k, v in m.items()}


def kernel(**inputs):
    x = np.ascontiguousarray(np.asarray(inputs["L0_attn_x"], dtype=np.float32))
    mk = MK()
    base = _core_inputs(inputs)
    in_maps = []
    for c in range(8):
        mm = dict(base)
        mm["x"] = np.ascontiguousarray(x[c])
        in_maps.append(mm)
    res = run_bass_kernel_spmd(mk.nc, in_maps, core_ids=list(range(8)))
    return np.stack([r["out"] for r in res.results], axis=0)


# version 13
# speedup vs baseline: 1.3897x; 1.0022x over previous
import numpy as np
import concourse.bass as bass
import concourse.mybir as mybir
from concourse.bass_utils import run_bass_kernel_spmd

F32 = mybir.dt.float32
BF16 = mybir.dt.bfloat16
U8 = mybir.dt.uint8
AF = mybir.ActivationFunctionType
ALU = mybir.AluOpType
AX = mybir.AxisListType

T = 2048
D = 2048
L = 1024
NT = T // 128
ND = D // 128
NL = L // 128
HID = [1024, 2048, 3072, 4096, 1024, 2048, 3072, 4096]
NJ = [h // 128 for h in HID]
COST_LAMBDA = 1e-07
EPS = 1e-5
TB = 1024
NTB = T // TB
NTI = TB // 128
JG = 8
NEG_BIG = -1e30
SQD = 1.0 / D

SEM_NAMES = [
    "sCONST", "sX0", "sX1", "sSPQ", "sKL", "sQS0", "sQS1", "sASP",
    "sAL0", "sAL1", "sXD", "sX2S0", "sX2S1", "sBR", "sG2S", "sAGS",
    "sBC", "sABC", "sBUD", "sGL0", "sGL1", "sXG0", "sXG1", "sXB0", "sXB1", "sOUT",
    "sWC", "sWS0", "sWS1", "sWO", "sDW", "sW10", "sW11", "sW12", "sW20", "sW21", "sGW",
    "sOU0", "sOU1",
    "sPE", "sACT", "sDVE",
]


class MK:
    def __init__(self, phase_limit=99, dbg=None):
        self.phase_limit = phase_limit
        self.dbg = dbg or []
        self.dbg_done = []
        nc = bass.Bass("TRN2", target_bir_lowering=False)
        self.nc = nc
        self._dram()
        self._sbuf()
        self._build()

    def _dram(self):
        nc = self.nc

        def din(name, shape):
            return nc.dram_tensor(name, shape, F32, kind="ExternalInput").ap()

        self.x = din("x", [T, D])
        self.Wq = din("Wq", [D, D]); self.Wk = din("Wk", [D, D])
        self.Wv = din("Wv", [D, D]); self.Wo = din("Wo", [D, D])
        self.down_w = din("down_w", [D, L]); self.down_b = din("down_b", [L])
        self.up_w = din("up_w", [L, D]); self.up_b = din("up_b", [D])
        self.router_w = din("router_w", [D, 8]); self.router_b = din("router_b", [8])
        self.shared_w1 = din("shared_w1", [L // 128, 128, NL, 128])
        self.shared_b1 = din("shared_b1", [L])
        self.shared_w2 = din("shared_w2", [L, L]); self.shared_b2 = din("shared_b2", [L])
        self.core_w = din("core_w", [D, D]); self.core_b = din("core_b", [D])
        self.ew1 = [din(f"e{i}_w1", [NJ[i], 128, NL, 128]) for i in range(8)]
        self.eb1 = [din(f"e{i}_b1", [HID[i]]) for i in range(8)]
        self.ew2 = [din(f"e{i}_w2", [HID[i], L]) for i in range(8)]
        self.eb2 = [din(f"e{i}_b2", [L]) for i in range(8)]
        self.ident = din("ident", [128, 128])
        self.negcost = din("negcost", [1, 8])
        self.out = nc.dram_tensor("out", [T, D], F32, kind="ExternalOutput").ap()

        def scr(name, shape, dt):
            return nc.dram_tensor(name, shape, dt, kind="Internal").ap()

        self.scrQ = scr("scrQ", [4, NT, 128, 512], BF16)
        self.scrK = scr("scrK", [4, NT, 128, 512], BF16)
        self.scrATT = scr("scrATT", [NT, 128, ND, 128], BF16)
        self.scrX2 = scr("scrX2", [NT, 128, D], F32)
        self.scrG2 = scr("scrG2", [NT, 128, ND, 128], BF16)
        self.scrAG = scr("scrAG", [9, T], BF16)

    def _sbuf(self):
        nc = self.nc

        def arena(name, nbytes):
            return nc.alloc_sbuf_tensor(name, [128, nbytes], U8)

        def view(ar, off, shape, dt):
            esz = {F32: 4, BF16: 2}[dt]
            n = 1
            for s in shape[1:]:
                n *= s
            assert off + n * esz <= ar.shape[1], (off, n * esz)
            ap = ar[:, off:off + n * esz].bitcast(dt)
            if len(shape) == 3:
                ap = ap.rearrange("p (a c) -> p a c", a=shape[1])
            return ap

        arA = arena("arA", 65536)
        arB = arena("arB", 65536)
        arC = arena("arC", 53504)
        self.view = view

        self.ln1T = view(arA, 0, [128, ND, T], BF16)
        self.Kar = view(arA, 0, [128, NT, D], BF16)
        self.ln2T = view(arA, 0, [128, ND, T], BF16)
        self.wtil = view(arA, 0, [128, 2 * JG, TB], BF16)
        self.w2grp = view(arA, 32768, [128, 2 * JG, L], BF16)
        self.corew = view(arA, 0, [128, ND, D], BF16)
        self.VT = view(arB, 0, [128, ND, T], BF16)
        self.Wosb = view(arB, 0, [128, ND, D], BF16)
        self.z = view(arB, 0, [128, NT, L], F32)
        self.upw = view(arB, 0, [128, NL, D], BF16)
        self.y2t = view(arB, 32768, [128, D], F32)
        self.x2tG = [view(arB, 40960, [128, D], F32), view(arB, 49152, [128, D], F32)]
        self.outt = view(arB, 57344, [128, D], F32)
        self.xtA = [view(arC, 32768, [128, D], F32), view(arC, 32768, [128, D], F32)]
        self.lnA = [view(arC, 40960, [128, D], BF16), view(arC, 45056, [128, D], BF16)]
        self.wslab = [view(arC, 0, [128, ND, 512], BF16), view(arC, 16384, [128, ND, 512], BF16)]
        self.stage = view(arB, 49152, [128, NT, 512], BF16)
        self.stageK = view(arB, 0, [128, NT, 512], BF16)
        self.qslab = [view(arC, 0, [128, NT, 512], BF16), view(arC, 16384, [128, NT, 512], BF16)]
        self.Srow = view(arC, 32768, [128, D], F32)
        self.Ptile = view(arC, 40960, [128, D], BF16)
        self.ATcol = view(arC, 45056, [128, ND, 128], BF16)
        self.attst = view(arC, 49152, [128, D], BF16)
        self.attsl = [view(arC, 0, [128, ND, 128], BF16), view(arC, 4096, [128, ND, 128], BF16)]
        self.xtD = view(arC, 8192, [128, D], F32)
        self.ytD = view(arC, 16384, [128, D], F32)
        self.x2tD = [view(arC, 24576, [128, D], F32), view(arC, 32768, [128, D], F32)]
        self.lnD = [view(arC, 40960, [128, D], BF16), view(arC, 45056, [128, D], BF16)]
        self.GT = view(arC, 0, [128, NL, T], BF16)
        self.dwsl = view(arC, 32768, [128, ND, 512], BF16)
        self.g2st = view(arC, 49152, [128, D], BF16)
        self.agT9 = arC[0:9, 32768:32768 + T * 4].bitcast(F32)
        self.stl = [view(arC, 32768, [128, TB], BF16), view(arC, 34816, [128, TB], BF16)]
        self.xtraB = view(arC, 36864, [128, TB], BF16)
        self.w1sl = [view(arC, 40960, [128, NL, 128], BF16),
                     view(arC, 43008, [128, NL, 128], BF16),
                     view(arC, 38912, [128, NL, 128], BF16)]
        self.xtraF = view(arC, 45056, [128, TB], F32)
        self.B2bc = view(arC, 49152, [128, L], F32)
        self.z16T = view(arC, 0, [128, NL, T], BF16)
        self.g2sl = [view(arC, 32768, [128, ND, 128], BF16), view(arC, 36864, [128, ND, 128], BF16)]
        self.zst = view(arC, 40960, [128, L], BF16)
        self.sqbG = view(arC, 40960, [128, D], BF16)
        self.B_ud = view(arC, 45056, [128, D], F32)

        self.identf = nc.alloc_sbuf_tensor("identf", [128, 128], F32)
        self.identb = nc.alloc_sbuf_tensor("identb", [128, 128], BF16)
        self.st = nc.alloc_sbuf_tensor("stats", [128, 32], F32)
        self.RW = nc.alloc_sbuf_tensor("rw3", [128, 16, 14, 8], F32)
        self.rst = nc.alloc_sbuf_tensor("rst", [128, 16, 12], F32)
        self.gsbc = nc.alloc_sbuf_tensor("gsbc", [128, TB], BF16)
        self.abc = nc.alloc_sbuf_tensor("abc", [128, TB], BF16)
        self.bbc = nc.alloc_sbuf_tensor("bbc", [128, TB], BF16)
        self.dwnbT = nc.alloc_sbuf_tensor("dwnbT", [128, 8], F32)
        self.sb1T = nc.alloc_sbuf_tensor("sb1T", [128, 8], F32)
        self.b1T = nc.alloc_sbuf_tensor("b1T", [128, 160], F32)
        self.rT = nc.alloc_sbuf_tensor("rT", [128, 160], F32)
        self.brows = nc.alloc_sbuf_tensor("brows", [128, 128], F32)
        self.rbbc = nc.alloc_sbuf_tensor("rbbc", [128, 8], F32)
        self.rbtmp = nc.alloc_sbuf_tensor("rbtmp", [128, 8], F32)
        self.rwsb = nc.alloc_sbuf_tensor("rwsb", [128, ND, 8], BF16)
        self.rzc = nc.alloc_sbuf_tensor("rzc", [128, 4], F32)
        self.constv = nc.alloc_sbuf_tensor("constv", [128, 8], F32)
        self.CONSTS = [EPS, -0.5, 0.125, 0.1, 1.0, 0.0]
        for i, cval in enumerate(self.CONSTS):
            nc.const_aps.register(F32, cval, self.constv[:, i:i + 1])
        assert nc.sbuf_bytes_remaining >= 0, nc.sbuf_bytes_remaining

    def _build(self):
        from contextlib import ExitStack
        nc = self.nc
        with ExitStack() as ctx:
            self.pq = [ctx.enter_context(nc.psum_tensor(f"pq{i}", [128, 1024], F32))
                       for i in range(4)]
            self.pb = [self.pq[i // 2][:, (i % 2) * 512:(i % 2 + 1) * 512]
                       for i in range(8)]
            self.pbbf = [self.pb[i].bitcast(BF16) for i in range(8)]
            self.sem = {}
            for nm in SEM_NAMES:
                self.sem[nm] = ctx.enter_context(nc.semaphore(nm))
            self.cnt = {nm: 0 for nm in SEM_NAMES}
            with nc.Block() as block:
                self.block = block
                self._program()

    def inc(self, inst, nm, n=1):
        self.cnt[nm] += n
        inst.then_inc(self.sem[nm], n)
        return (nm, self.cnt[nm])

    def W(self, eng, *toks):
        best = {}
        for tk in toks:
            if tk is not None and tk[1] > 0:
                if best.get(tk[0], 0) < tk[1]:
                    best[tk[0]] = tk[1]
        for nm, v in best.items():
            eng.wait_ge(self.sem[nm], v)

    def tokd(self, nm):
        return (nm, self.cnt[nm])

    def vf(self, v, inst):
        tk = self.inc(inst, "sDVE")
        v.wait_ge(self.sem["sDVE"], tk[1])

    def af(self, s, inst):
        tk = self.inc(inst, "sACT")
        s.wait_ge(self.sem["sACT"], tk[1])

    def _program(self):
        self._setup()
        pl = self.phase_limit
        if pl >= 1:
            self._phaseA()
        if pl >= 2:
            self._phaseB()
        if pl >= 3:
            self._phaseC()
        if pl >= 4:
            self._phaseD()
        if pl >= 5:
            self._phaseE()
        if pl >= 6:
            self._phaseF()
        if pl >= 7:
            self._phaseG()
        self._dbg_flush()
        self._final_wait()

    def _setup(self):
        block = self.block

        @block.sync
        def _(sy):
            self.inc(sy.dma_start(self.identf[:, :], self.ident), "sCONST", 16)
            self.inc(sy.dma_start(self.rbbc[:, :],
                                  self.router_b[None, :].to_broadcast([128, 8])), "sCONST", 16)
            self.t_const = self.inc(sy.dma_start(
                self.rbtmp[:, :], self.negcost.to_broadcast([128, 8])), "sCONST", 16)

        @block.gpsimd
        def _(g):
            self.inc(g.dma_start(self.identb[:, :], self.ident), "sWC", 16)
            self.t_wc = self.inc(g.dma_start(
                self.rwsb[:, :, :],
                self.router_w.rearrange("(c p) e -> p c e", p=128)), "sWC", 16)

        @block.vector
        def _(v):
            for i, cval in enumerate(self.CONSTS):
                v.memset(self.constv[:, i:i + 1], cval)
            self.W(v, self.t_const)
            self.t_rbbc = self.inc(
                v.tensor_add(self.rbbc[:, :], self.rbbc[:, :], self.rbtmp[:, :]), "sDVE")

    def _ln_stats(self, v, xsrc, sqdump):
        st = self.st
        v.reduce_sum(st[:, 0:1], xsrc, axis=AX.X)
        self.vf(v, v.scalar_tensor_tensor(sqdump, xsrc, 1.0, xsrc,
                                          op0=ALU.mult, op1=ALU.mult, accum_out=st[:, 1:2]))
        self.vf(v, v.tensor_scalar(st[:, 2:3], st[:, 0:1], -SQD, None, op0=ALU.mult))
        self.vf(v, v.tensor_mul(st[:, 3:4], st[:, 2:3], st[:, 2:3]))
        self.vf(v, v.scalar_tensor_tensor(st[:, 4:5], st[:, 1:2], SQD, st[:, 3:4],
                                          op0=ALU.mult, op1=ALU.subtract))
        return self.tokd("sDVE")

    def _ln_rstd(self, s):
        st = self.st
        self.af(s, s.activation(st[:, 5:6], st[:, 4:5], AF.Ln, bias=EPS))
        self.af(s, s.activation(st[:, 6:7], st[:, 5:6], AF.Exp, scale=-0.5))

    def _transp16(self, src_bf, dst3, tt, gate, state):
        block = self.block
        pbbf = self.pbbf
        nch = dst3.shape[1]
        per = nch // 2

        for half in range(2):
            @block.tensor
            def _(t, half=half):
                self.W(t, gate, state["trev"][half])
                if not getattr(self, "_identb_ok", False):
                    self.W(t, self.t_wc)
                    self._identb_ok = True
                bank = pbbf[6 + half]
                last = None
                for i in range(per):
                    fc = half * per + i
                    last = t.transpose(bank[:, i * 128:(i + 1) * 128],
                                       src_bf[:, fc * 128:(fc + 1) * 128], self.identb[:, :])
                state["ttok"][half] = self.inc(last, "sPE")

            @block.vector
            def _(v, half=half):
                self.W(v, state["ttok"][half])
                bank = pbbf[6 + half]
                dst = dst3[:, half * per:(half + 1) * per, tt * 128:(tt + 1) * 128]
                state["trev"][half] = self.inc(
                    v.tensor_copy(dst, bank[:, 0:per * 128].rearrange(
                        "p (c n) -> p c n", c=per)), "sDVE")
        return state["ttok"][1]

    def _phaseA(self):
        block = self.block
        st = self.st
        self.lnT_tok = {}
        stA = {"ttok": {0: None, 1: None}, "trev": {0: None, 1: None}}
        apply_tok = {}
        tt_pe = {}
        xload = {}
        wtokA = {}

        @block.gpsimd
        def _(g):
            for k in range(2):
                src = [self.Wq, self.Wk][k][:, 0:512]
                for h in range(2):
                    wtokA[k] = self.inc(g.dma_start(
                        self.wslab[k][:, h * 8:(h + 1) * 8, :],
                        src.rearrange("(c p) n -> p c n", p=128)[:, h * 8:(h + 1) * 8, :]),
                        "sWS%d" % k, 16)

        self.Bst = {"slab_pe": {}, "ev_bank": {i: None for i in range(4)}, "pbt": [None]}
        V0 = self.cnt["sDVE"]
        A0 = self.cnt["sACT"]
        X0 = self.cnt["sX0"]
        P0A = self.cnt["sPE"]

        @block.scalar
        def _(s, V0=V0, X0=X0, P0A=P0A):
            for tt in range(NT):
                if tt == 0:
                    self.W(s, self.t_rbbc)
                s.wait_ge(self.sem["sX0"], X0 + 16 * (tt + 1))
                if tt >= 2:
                    s.wait_ge(self.sem["sPE"], P0A + 4 * (tt - 2) + 2)
                self.inc(s.activation(self.lnA[tt % 2][:, :], self.xtA[0][:, :],
                                      AF.Square, accum_out=st[:, 1:2]), "sACT")
                s.wait_ge(self.sem["sDVE"], V0 + 9 * tt + 4)
                self._ln_rstd(s)
        assert self.cnt["sACT"] == A0 + 3 * NT

        for tt in range(NT):
            @block.sync
            def _(sy, tt=tt):
                if tt >= 1:
                    self.W(sy, apply_tok[tt - 1])
                xload[tt] = self.inc(
                    sy.dma_start(self.xtA[0][:, :], self.x[tt * 128:(tt + 1) * 128, :]),
                    "sX0", 16)

            @block.vector
            def _(v, tt=tt):
                self.W(v, xload[tt])
                self.vf(v, v.reduce_sum(st[:, 0:1], self.xtA[0][:, :], axis=AX.X))
                self.vf(v, v.tensor_scalar(st[:, 2:3], st[:, 0:1], -SQD, None,
                                           op0=ALU.mult))
                self.vf(v, v.tensor_mul(st[:, 3:4], st[:, 2:3], st[:, 2:3]))
                self.W(v, ("sACT", A0 + 3 * tt + 1))
                self.vf(v, v.scalar_tensor_tensor(st[:, 4:5], st[:, 1:2], SQD,
                                                  st[:, 3:4],
                                                  op0=ALU.mult, op1=ALU.subtract))
            tv = self.tokd("sDVE")
            assert tv[1] == V0 + 9 * tt + 4
            t_rs = ("sACT", A0 + 3 * (tt + 1))

            @block.vector
            def _(v, tt=tt, t_rs=t_rs):
                self.W(v, t_rs)
                apply_tok[tt] = self.inc(
                    v.tensor_scalar(self.lnA[tt % 2][:, :], self.xtA[0][:, :],
                                    self.st[:, 2:3], self.st[:, 6:7],
                                    op0=ALU.add, op1=ALU.mult), "sDVE")

            tt_pe[tt] = self._transp16(self.lnA[tt % 2], self.ln1T, tt, apply_tok[tt], stA)
            self.lnT_tok[tt] = self.tokd("sDVE")

            for k in range(2):
                bk = tt % 2 if k == 0 else 2 + tt % 2

                @block.tensor
                def _(t, tt=tt, k=k, bk=bk):
                    if tt == 0:
                        self.W(t, wtokA[k])
                    self.W(t, self.lnT_tok[tt], self.Bst["ev_bank"][bk])
                    last = None
                    for fc in range(ND):
                        last = t.matmul(self.pb[bk][:, :],
                                        self.ln1T[:, fc, tt * 128:(tt + 1) * 128],
                                        self.wslab[k][:, fc, :],
                                        start=(fc == 0), stop=(fc == ND - 1))
                    self.Bst["pbt"][0] = self.inc(last, "sPE")
                    if tt == NT - 1:
                        self.Bst["slab_pe"][k] = self.Bst["pbt"][0]

                @block.vector
                def _(v, tt=tt, k=k, bk=bk):
                    self.W(v, self.Bst["pbt"][0])
                    dst = (self.stage if k == 0 else self.stageK)[:, tt, :]
                    self.Bst["ev_bank"][bk] = self.inc(
                        v.tensor_copy(dst, self.pb[bk][:, :]), "sDVE")

        @block.sync
        def _(sy):
            self.W(sy, self.tokd("sDVE"))
            self.inc(sy.dma_start(self.scrQ[0].rearrange("t p n -> p t n"),
                                  self.stage[:, :, :]), "sSPQ", 16)
            self.inc(sy.dma_start(self.scrK[0].rearrange("t p n -> p t n"),
                                  self.stageK[:, :, :]), "sSPQ", 16)
        self.Bst["stage_spill"] = self.tokd("sSPQ")
        self.t_ln1T = self.tokd("sDVE")
        self.t_A_pe = self.tokd("sPE")
        self._dbg_mark("A")

    def _phaseB(self):
        block = self.block
        pb = self.pb
        WQKV = [self.Wq, self.Wk, self.Wv]
        slab_pe = self.Bst["slab_pe"]
        ev_bank = self.Bst["ev_bank"]
        stage_spill = self.Bst["stage_spill"]
        pbt = self.Bst["pbt"]

        for fo in range(4):
            for wi in range(3):
                if fo == 0 and wi < 2:
                    continue
                k = fo * 3 + wi
                wtok = [None]

                @block.gpsimd
                def _(g, fo=fo, wi=wi, k=k):
                    if k >= 2:
                        self.W(g, slab_pe[k - 2])
                    src = WQKV[wi][:, fo * 512:(fo + 1) * 512]
                    snm = "sWS%d" % (k % 2)
                    for h in range(2):
                        wtok[0] = self.inc(g.dma_start(
                            self.wslab[k % 2][:, h * 8:(h + 1) * 8, :],
                            src.rearrange("(c p) n -> p c n", p=128)[:, h * 8:(h + 1) * 8, :]),
                            snm, 16)

                if wi < 2:
                    for tt in range(NT):
                        @block.tensor
                        def _(t, k=k, tt=tt):
                            if tt == 0:
                                self.W(t, wtok[0])
                            if k < 2:
                                self.W(t, self.lnT_tok[tt])
                            bk = tt % 2
                            self.W(t, ev_bank[bk])
                            last = None
                            for fc in range(ND):
                                last = t.matmul(pb[bk][:, :],
                                                self.ln1T[:, fc, tt * 128:(tt + 1) * 128],
                                                self.wslab[k % 2][:, fc, :],
                                                start=(fc == 0), stop=(fc == ND - 1))
                            pbt[0] = self.inc(last, "sPE")
                            if tt == NT - 1:
                                slab_pe[k] = pbt[0]

                        @block.vector
                        def _(v, tt=tt):
                            self.W(v, pbt[0])
                            if tt == 0:
                                self.W(v, stage_spill)
                            ev_bank[tt % 2] = self.inc(
                                v.tensor_copy(self.stage[:, tt, :], pb[tt % 2][:, :]), "sDVE")

                    @block.sync
                    def _(sy, fo=fo, wi=wi):
                        self.W(sy, self.tokd("sDVE"))
                        dst = (self.scrQ if wi == 0 else self.scrK)[fo]
                        self.inc(sy.dma_start(dst.rearrange("t p n -> p t n"),
                                              self.stage[:, :, :]), "sSPQ", 16)
                    stage_spill = self.tokd("sSPQ")
                else:
                    for i in range(16):
                        fc4 = i // 4

                        @block.tensor
                        def _(t, k=k, fo=fo, fc4=fc4, i=i):
                            tb4 = i % 4
                            if i == 0:
                                self.W(t, wtok[0])
                            if k < 3:
                                self.W(t, self.lnT_tok[4 * tb4 + 3])
                            bk = 2 + i % 2
                            self.W(t, ev_bank[bk])
                            last = None
                            for fc in range(ND):
                                last = t.matmul(pb[bk][:, :],
                                                self.wslab[k % 2][:, fc, fc4 * 128:(fc4 + 1) * 128],
                                                self.ln1T[:, fc, tb4 * 512:(tb4 + 1) * 512],
                                                start=(fc == 0), stop=(fc == ND - 1))
                            pbt[0] = self.inc(last, "sPE")
                            if i == 15:
                                slab_pe[k] = pbt[0]

                        @block.vector
                        def _(v, fo=fo, fc4=fc4, i=i):
                            tb4 = i % 4
                            self.W(v, pbt[0])
                            if i == 0 and fo in (0, 3):
                                self.W(v, stage_spill)
                            dst = self.VT[:, fo * 4 + fc4, tb4 * 512:(tb4 + 1) * 512]
                            ev_bank[2 + i % 2] = self.inc(
                                v.tensor_copy(dst, pb[2 + i % 2][:, :]), "sDVE")
        self.t_B_pe = self.tokd("sPE")
        self.t_B_spill = self.tokd("sSPQ")
        self.t_B_vt = self.tokd("sDVE")
        self._dbg_mark("B")

    def _phaseC(self):
        block = self.block
        pb = self.pb
        st = self.st
        t_Kload = [None]

        @block.sync
        def _(sy):
            self.W(sy, self.t_B_pe, self.t_B_spill)
            for fo in range(4):
                for hh in range(2):
                    t_Kload[0] = self.inc(sy.dma_start(
                        self.Kar[:, hh * 8:(hh + 1) * 8, fo * 512:(fo + 1) * 512],
                        self.scrK[fo][hh * 8:(hh + 1) * 8].rearrange("t p n -> p t n")),
                        "sKL", 16)

        qload, qcons = {}, {}
        s_evac = {}
        t_exp = {}
        t_rz = {}
        av_ev = {0: None, 1: None}
        att_sp = None
        stC = {"ttok": {0: None, 1: None}, "trev": {0: None, 1: None}}
        Stok = {}
        avtok = [None]

        def emit_S(ec):
            fo = ec // 4
            if ec % 4 == 0:
                @block.sync
                def _(sy, fo=fo):
                    if fo >= 2:
                        self.W(sy, qcons[fo - 2])
                    for hh in range(2):
                        qload[fo] = self.inc(sy.dma_start(
                            self.qslab[fo % 2][:, hh * 8:(hh + 1) * 8, :],
                            self.scrQ[fo][hh * 8:(hh + 1) * 8].rearrange("t p n -> p t n")),
                            "sQS%d" % (fo % 2), 16)

            @block.tensor
            def _(t, ec=ec, fo=fo):
                self.W(t, qload[fo], t_Kload[0], s_evac.get(ec - 1))
                if ec == 0:
                    self.W(t, self.t_B_vt)
                esub = ec % 4
                last = None
                for fb in range(4):
                    for tt in range(NT):
                        last = t.matmul(pb[fb][:, :],
                                        self.qslab[fo % 2][:, tt, esub * 128:(esub + 1) * 128],
                                        self.Kar[:, tt, fb * 512:(fb + 1) * 512],
                                        start=(tt == 0), stop=(tt == NT - 1))
                Stok[ec] = self.inc(last, "sPE")
                if esub == 3:
                    qcons[fo] = Stok[ec]

        def emit_TAV_mm(ec):
            self._transp16(self.Ptile, self.ATcol, 0, t_exp[ec], stC)
            t_at = self.tokd("sDVE")
            A0 = self.cnt["sACT"]
            avt = []
            for tb4 in range(4):
                @block.tensor
                def _(t, ec=ec, tb4=tb4, t_at=t_at):
                    if tb4 == 0:
                        self.W(t, t_at)
                    bk = 4 + tb4 % 2
                    self.W(t, av_ev[tb4 % 2])
                    last = None
                    for fc in range(ND):
                        last = t.matmul(pb[bk][:, :], self.ATcol[:, fc, :],
                                        self.VT[:, fc, tb4 * 512:(tb4 + 1) * 512],
                                        start=(fc == 0), stop=(fc == ND - 1))
                    avt.append(self.inc(last, "sPE"))
                av_ev[tb4 % 2] = ("sACT", A0 + tb4 + 2)
            return avt, A0

        def emit_scalar(ec_tav, avt, ec_sm):
            nonlocal att_sp
            V0 = self.cnt["sDVE"]

            @block.scalar
            def _(sc, ec_tav=ec_tav, avt=avt, ec_sm=ec_sm, V0=V0):
                if ec_sm is not None:
                    self.W(sc, Stok[ec_sm])
                    if ec_sm >= 1:
                        self.W(sc, t_exp[ec_sm - 1])
                    last = None
                    for fb in range(4):
                        last = sc.activation(self.Srow[:, fb * 512:(fb + 1) * 512],
                                             pb[fb][:, :], AF.Copy, scale=0.125)
                    self.inc(last, "sACT")
                    s_evac[ec_sm] = self.tokd("sACT")
                if avt is not None:
                    for tb4 in range(4):
                        self.W(sc, avt[tb4], t_rz[ec_tav])
                        if tb4 == 0:
                            self.W(sc, att_sp)
                        self.inc(sc.activation(
                            self.attst[:, tb4 * 512:(tb4 + 1) * 512],
                            pb[4 + tb4 % 2][:, :], AF.Copy,
                            scale=self.rzc[:, ec_tav % 2:ec_tav % 2 + 1]), "sACT")
                if ec_sm is not None:
                    sc.wait_ge(self.sem["sDVE"], V0 + 1)
                    if ec_sm >= 1:
                        self.W(sc, stC["ttok"][1])
                    self.inc(sc.activation(self.Ptile[:, :], self.Srow[:, :], AF.Exp,
                                           bias=st[:, 16 + ec_sm % 2:17 + ec_sm % 2],
                                           accum_out=st[:, 18 + ec_sm % 2:19 + ec_sm % 2]),
                             "sACT")
                    t_exp[ec_sm] = self.tokd("sACT")

            if ec_sm is not None:
                @block.vector
                def _(v, ec=ec_sm, V0=V0):
                    self.W(v, s_evac[ec])
                    self.vf(v, v.reduce_max(st[:, 16 + ec % 2:17 + ec % 2], self.Srow[:, :],
                                            axis=AX.X, negate=True))
                    assert self.cnt["sDVE"] == V0 + 1
                    self.W(v, t_exp[ec])
                    self.vf(v, v.reciprocal(self.rzc[:, ec % 2:ec % 2 + 1],
                                            st[:, 18 + ec % 2:19 + ec % 2]))
                    t_rz[ec] = self.tokd("sDVE")

            if avt is not None:
                @block.sync
                def _(sy, ec=ec_tav):
                    self.W(sy, self.tokd("sACT") if ec_sm is None
                           else av_ev[1])
                    self.inc(sy.dma_start(
                        self.scrATT[:, :, ec, :].rearrange("t p n -> p t n"),
                        self.attst.rearrange("p (t n) -> p t n", t=NT)), "sASP", 16)
                att_sp = self.tokd("sASP")

        emit_S(0)
        emit_scalar(None, None, 0)
        for ec in range(1, 16):
            emit_S(ec)
            avt, _ = emit_TAV_mm(ec - 1)
            emit_scalar(ec - 1, avt, ec)
        avt, A0f = emit_TAV_mm(15)
        for b in (0, 1):
            av_ev[b] = ("sACT", av_ev[b][1] - 1)
        emit_scalar(15, avt, None)
        self.t_C_sp = self.tokd("sASP")
        self.t_C_pe = self.tokd("sPE")
        self.t_C_sev = s_evac[15]
        self.t_C_avev = max(av_ev[0], av_ev[1], key=lambda x: x[1])
        self._dbg_mark("C")

    def _phaseD(self):
        block = self.block
        pb = self.pb
        st = self.st
        t_wo = [None]

        @block.gpsimd
        def _(g):
            self.W(g, self.t_C_pe)
            for fo in range(4):
                for hh in range(2):
                    t_wo[0] = self.inc(g.dma_start(
                        self.Wosb[:, hh * 8:(hh + 1) * 8, fo * 512:(fo + 1) * 512],
                        self.Wo[:, fo * 512:(fo + 1) * 512]
                        .rearrange("(c p) n -> p c n", p=128)[:, hh * 8:(hh + 1) * 8, :]),
                        "sWO", 16)

        att_ld, x_ld, x2_sp = {}, {}, {}
        ytok, x2tok, apply2, ttokD = {}, {}, {}, {}
        stD = {"ttok": {0: None, 1: None}, "trev": {0: None, 1: None}}
        V0D = self.cnt["sDVE"]
        A0D = self.cnt["sACT"]

        def emit_loads(tt):
            @block.sync
            def _(sy, tt=tt):
                if tt == 0:
                    self.W(sy, self.t_C_sp)
                if tt >= 2:
                    self.W(sy, ytok[tt - 2])
                att_ld[tt] = self.inc(sy.dma_start(
                    self.attsl[tt % 2][:, :, :], self.scrATT[tt]),
                    "sAL%d" % (tt % 2), 16)

            @block.gpsimd
            def _(g, tt=tt):
                if tt >= 1:
                    self.W(g, ("sDVE", V0D + 15 * (tt - 1) + 8))
                x_ld[tt] = self.inc(g.dma_start(
                    self.xtD[:, :], self.x[tt * 128:(tt + 1) * 128, :]), "sXD", 16)

        def emit_Y(tt):
            @block.tensor
            def _(t, tt=tt):
                self.W(t, att_ld[tt])
                if tt == 0:
                    self.W(t, t_wo[0], self.t_C_sev)
                else:
                    self.W(t, ("sACT", A0D + 7 * (tt - 1) + 1))
                last = None
                for db in range(4):
                    for ec in range(ND):
                        last = t.matmul(pb[db][:, :], self.attsl[tt % 2][:, ec, :],
                                        self.Wosb[:, ec, db * 512:(db + 1) * 512],
                                        start=(ec == 0), stop=(ec == ND - 1))
                ytok[tt] = self.inc(last, "sPE")

        def emit_chain(tt):
            V0 = self.cnt["sDVE"]
            A0 = self.cnt["sACT"]
            assert V0 == V0D + 15 * tt and A0 == A0D + 7 * tt

            @block.scalar
            def _(s, tt=tt, V0=V0):
                self.W(s, ytok[tt])
                if tt >= 1:
                    self.W(s, x2tok[tt - 1])
                if tt >= 2:
                    self.W(s, x2_sp[tt - 2], apply2[tt - 2])
                last = None
                for db in range(4):
                    last = s.activation(self.ytD[:, db * 512:(db + 1) * 512],
                                        pb[db][:, :], AF.Copy,
                                        accum_out=st[:, 8 + db:9 + db])
                self.af(s, last)
                self.inc(s.activation(self.x2tD[tt % 2][:, :], self.ytD[:, :], AF.Square,
                                      accum_out=st[:, 1:2]), "sACT")
                s.wait_ge(self.sem["sDVE"], V0 + 6)
                self.af(s, s.activation(st[:, 5:6], st[:, 4:5], AF.Ln, bias=EPS))
                self.af(s, s.activation(st[:, 6:7], st[:, 5:6], AF.Exp, scale=-0.5))
                s.wait_ge(self.sem["sDVE"], V0 + 8)
                self.inc(s.activation(self.ytD[:, :], self.x2tD[tt % 2][:, :], AF.Square,
                                      accum_out=st[:, 17:18]), "sACT")
                s.wait_ge(self.sem["sDVE"], V0 + 12)
                self.af(s, s.activation(st[:, 21:22], st[:, 20:21], AF.Ln, bias=EPS))
                self.af(s, s.activation(st[:, 22:23], st[:, 21:22], AF.Exp, scale=-0.5))
            t_evsq = ("sACT", A0 + 2)
            t_r = ("sACT", A0 + 4)
            t_sq2 = ("sACT", A0 + 5)
            t_rs2 = ("sACT", A0 + 7)
            assert self.cnt["sACT"] == A0 + 7

            @block.vector
            def _(v, tt=tt, t_evsq=t_evsq):
                self.W(v, t_evsq)
                self.vf(v, v.tensor_add(st[:, 12:13], st[:, 8:9], st[:, 9:10]))
                self.vf(v, v.tensor_add(st[:, 13:14], st[:, 10:11], st[:, 11:12]))
                self.vf(v, v.tensor_add(st[:, 0:1], st[:, 12:13], st[:, 13:14]))
                self.vf(v, v.tensor_scalar(st[:, 2:3], st[:, 0:1], -SQD, None, op0=ALU.mult))
                self.vf(v, v.tensor_mul(st[:, 3:4], st[:, 2:3], st[:, 2:3]))
                self.vf(v, v.scalar_tensor_tensor(st[:, 4:5], st[:, 1:2], SQD, st[:, 3:4],
                                                  op0=ALU.mult, op1=ALU.subtract))

            @block.vector
            def _(v, tt=tt, t_r=t_r):
                self.W(v, t_r, x_ld[tt])
                self.vf(v, v.tensor_scalar(self.x2tD[tt % 2][:, :], self.ytD[:, :],
                                           st[:, 2:3], st[:, 6:7],
                                           op0=ALU.add, op1=ALU.mult))
                self.vf(v, v.tensor_add(self.x2tD[tt % 2][:, :], self.x2tD[tt % 2][:, :],
                                        self.xtD[:, :]))
                x2tok[tt] = self.tokd("sDVE")
            assert x2tok[tt][1] == V0 + 8

            @block.sync
            def _(sy, tt=tt):
                self.W(sy, x2tok[tt])
                x2_sp[tt] = self.inc(
                    sy.dma_start(self.scrX2[tt], self.x2tD[tt % 2][:, :]),
                    "sX2S%d" % (tt % 2), 16)

            @block.vector
            def _(v, tt=tt, t_sq2=t_sq2):
                self.vf(v, v.reduce_sum(st[:, 16:17], self.x2tD[tt % 2][:, :], axis=AX.X))
                self.vf(v, v.tensor_scalar(st[:, 18:19], st[:, 16:17], -SQD, None,
                                           op0=ALU.mult))
                self.vf(v, v.tensor_mul(st[:, 19:20], st[:, 18:19], st[:, 18:19]))
                self.W(v, t_sq2)
                self.vf(v, v.scalar_tensor_tensor(st[:, 20:21], st[:, 17:18], SQD,
                                                  st[:, 19:20],
                                                  op0=ALU.mult, op1=ALU.subtract))
            assert self.cnt["sDVE"] == V0 + 12

            @block.vector
            def _(v, tt=tt, t_rs2=t_rs2):
                self.W(v, t_rs2)
                if tt >= 2:
                    self.W(v, ttokD[tt - 2])
                apply2[tt] = self.inc(
                    v.tensor_scalar(self.lnD[tt % 2][:, :], self.x2tD[tt % 2][:, :],
                                    st[:, 18:19], st[:, 22:23],
                                    op0=ALU.add, op1=ALU.mult), "sDVE")

            ttokD[tt] = self._transp16(self.lnD[tt % 2], self.ln2T, tt, apply2[tt], stD)

        for tt in range(NT):
            emit_loads(tt)
            emit_Y(tt)
            if tt >= 1:
                emit_chain(tt - 1)
        emit_chain(NT - 1)
        self.t_ln2T = self.tokd("sDVE")
        self.t_D_x2sp = [x2_sp[NT - 2], x2_sp[NT - 1]]
        self.t_D_yev = None
        self._dbg_mark("D")

    def _phaseE(self):
        block = self.block
        pb = self.pb
        t_bias1 = [None]

        @block.sync
        def _(sy):
            self.inc(sy.dma_start(self.brows[0:8, :],
                                  self.down_b.rearrange("(c n) -> c n", c=8)), "sBR", 16)
            t_bias1[0] = self.inc(sy.dma_start(
                self.brows[8:16, :],
                self.shared_b1.rearrange("(c n) -> c n", c=8)), "sBR", 16)

        bt = [None]

        @block.tensor
        def _(t):
            self.W(t, t_bias1[0], self.t_C_avev)
            bt[0] = self.inc(t.transpose(pb[5][:, 0:128], self.brows[:, :],
                                         self.identf[:, :]), "sPE")

        @block.vector
        def _(v):
            self.W(v, bt[0])
            v.tensor_copy(self.dwnbT[:, 0:8], pb[5][:, 0:8])
            self.inc(v.tensor_copy(self.sb1T[:, 0:8], pb[5][:, 8:16]), "sDVE")
        t_btv = self.tokd("sDVE")

        self.b1off = {}
        boff = 0
        b1t = {}
        for e in range(8):
            self.b1off[e] = boff

            @block.sync
            def _(sy, e=e):
                if e >= 1:
                    self.W(sy, b1t[e - 1])
                else:
                    self.W(sy, bt[0])
                self.inc(sy.dma_start(self.brows[0:NJ[e], :],
                                      self.eb1[e].rearrange("(c n) -> c n", c=NJ[e])),
                         "sBR", 16)

            @block.tensor
            def _(t, e=e):
                self.W(t, self.tokd("sBR"), self.tokd("sDVE"))
                b1t[e] = self.inc(t.transpose(pb[5][:, 0:128], self.brows[:, :],
                                              self.identf[:, :]), "sPE")

            @block.vector
            def _(v, e=e, boff=boff):
                self.W(v, b1t[e])
                self.inc(v.tensor_copy(self.b1T[:, boff:boff + NJ[e]],
                                       pb[5][:, 0:NJ[e]]), "sDVE")
            boff += NJ[e]
        t_b1 = self.tokd("sDVE")

        @block.scalar
        def _(s):
            self.W(s, t_b1)
            self.inc(s.activation(self.rT[:, 0:160], self.b1T[:, 0:160], AF.Gelu), "sACT")
        self.t_rT = self.tokd("sACT")

        gcons = {}
        gt = [None]
        gev = {0: self.t_D_yev, 1: self.t_D_yev}
        dwl = {}
        S0 = self.cnt["sG2S"]
        g2src = self.g2st.rearrange("p (t n) -> p t n", t=NT)
        for lh in range(2):
            @block.gpsimd
            def _(g, lh=lh):
                if lh == 0:
                    self.W(g, self.t_ln2T, self.t_D_x2sp[0], self.t_D_x2sp[1])
                else:
                    self.W(g, gcons[0])
                for hh in range(2):
                    dwl[lh] = self.inc(g.dma_start(
                        self.dwsl[:, hh * 8:(hh + 1) * 8, :],
                        self.down_w[:, lh * 512:(lh + 1) * 512]
                        .rearrange("(c p) n -> p c n", p=128)[:, hh * 8:(hh + 1) * 8, :]),
                        "sDW", 16)

            A0 = self.cnt["sACT"]
            order = []
            for i in range(16):
                order.append(("ev", i))
                if i % 2 == 1:
                    order.append(("g2", lh * 8 + i // 2))
            tok_of = {it: ("sACT", A0 + pos + 1) for pos, it in enumerate(order)}

            gtl = []
            for i in range(16):
                lc4, tb4 = i // 4, i % 4

                @block.tensor
                def _(t, lh=lh, lc4=lc4, tb4=tb4, i=i):
                    if i == 0:
                        self.W(t, dwl[lh])
                    bk = i % 2
                    self.W(t, gev[bk])
                    last = None
                    for fc in range(ND):
                        last = t.matmul(pb[bk][:, :],
                                        self.dwsl[:, fc, lc4 * 128:(lc4 + 1) * 128],
                                        self.ln2T[:, fc, tb4 * 512:(tb4 + 1) * 512],
                                        start=(fc == 0), stop=(fc == ND - 1))
                    gtl.append(self.inc(last, "sPE"))
                    if i == 15:
                        gcons[lh] = gtl[-1]
                gev[i % 2] = tok_of[("ev", i)]

            @block.scalar
            def _(sc, lh=lh, gtl=gtl, order=order, S0=S0):
                for it in order:
                    if it[0] == "ev":
                        i = it[1]
                        lc4, tb4 = i // 4, i % 4
                        Lc = lh * 4 + lc4
                        self.W(sc, gtl[i])
                        if lh == 0 and i == 0:
                            self.W(sc, t_btv, self.t_D_x2sp[0], self.t_D_x2sp[1],
                                   self.t_ln2T)
                        self.inc(sc.activation(
                            self.GT[:, Lc, tb4 * 512:(tb4 + 1) * 512], pb[i % 2][:, :],
                            AF.Gelu, bias=self.dwnbT[:, Lc:Lc + 1]), "sACT")
                    else:
                        fc = it[1]
                        if fc >= 1:
                            sc.wait_ge(self.sem["sG2S"], S0 + 16 * fc)
                        self.inc(sc.activation(self.g2st[:, :], self.ln2T[:, fc, :],
                                               AF.Gelu), "sACT")
            assert self.cnt["sACT"] == A0 + 24

            @block.sync
            def _(sy, order=order, tok_of=tok_of):
                for it in order:
                    if it[0] != "g2":
                        continue
                    fc = it[1]
                    self.W(sy, tok_of[it])
                    self.inc(sy.dma_start(
                        self.scrG2[:, :, fc, :].rearrange("t p n -> p t n"),
                        g2src), "sG2S", 16)
        self.t_g2 = self.tokd("sG2S")
        assert self.t_g2[1] == S0 + 16 * ND
        self.t_GT = self.tokd("sACT")


        lgev = {}
        lgt = [None]
        for tt in range(NT):
            @block.tensor
            def _(t, tt=tt):
                if tt == 0:
                    self.W(t, self.t_wc, self.t_C_avev)
                bk = 4 + tt % 2
                if tt >= 2:
                    self.W(t, lgev[tt - 2])
                last = None
                for fc in range(ND):
                    last = t.matmul(pb[bk][:, 0:8],
                                    self.ln2T[:, fc, tt * 128:(tt + 1) * 128],
                                    self.rwsb[:, fc, :], start=(fc == 0), stop=(fc == ND - 1))
                lgt[0] = self.inc(last, "sPE")

            @block.vector
            def _(v, tt=tt):
                self.W(v, lgt[0])
                if tt == 0:
                    self.W(v, self.t_rbbc)
                lgev[tt] = self.inc(
                    v.tensor_add(self.RW[:, tt, 0, :], pb[4 + tt % 2][:, 0:8],
                                 self.rbbc[:, :]), "sDVE")
        t_logit = self.tokd("sDVE")

        RW, rst = self.RW, self.rst

        def bc(col):
            base = rst[:, :, col:col + 1]
            return bass.AP(base.tensor, base.offset,
                           [list(base.ap[0]), list(base.ap[1]), [0, 8]])

        names = ["LG", "PR", "GE", "C1", "C2", "C3", "LE", "M1", "M2", "P2", "AL", "EX", "EX2"]
        S = {nm: RW[:, :, i, :] for i, nm in enumerate(names)}
        t_mid = [None]

        @block.vector
        def _(v):
            f = lambda inst: self.vf(v, inst)
            self.W(v, t_logit)
            f(v.reduce_max(rst[:, :, 0:1], S["LG"], axis=AX.X, negate=True))
            f(v.tensor_tensor(S["EX"], S["LG"], bc(0), op=ALU.add))
            t_mid[0] = self.tokd("sDVE")

        @block.scalar
        def _(s):
            self.W(s, t_mid[0])
            self.inc(s.activation(S["EX2"], S["EX"], AF.Exp), "sACT")
        t_e = self.tokd("sACT")

        @block.vector
        def _(v):
            f = lambda inst: self.vf(v, inst)
            self.W(v, t_e)
            f(v.reduce_sum(rst[:, :, 1:2], S["EX2"], axis=AX.X))
            f(v.reciprocal(rst[:, :, 2:3], rst[:, :, 1:2]))
            f(v.tensor_tensor(S["PR"], S["EX2"], bc(2), op=ALU.mult))

            def firstmax(dst, src, vcol):
                f(v.reduce_max(rst[:, :, vcol:vcol + 1], src, axis=AX.X))
                f(v.tensor_tensor(S["GE"], src, bc(vcol), op=ALU.is_ge))
                f(v.tensor_copy(S["C1"][:, :, 0:1], S["GE"][:, :, 0:1]))
                f(v.tensor_add(S["C1"][:, :, 1:8], S["GE"][:, :, 1:8], S["GE"][:, :, 0:7]))
                f(v.tensor_copy(S["C2"][:, :, 0:2], S["C1"][:, :, 0:2]))
                f(v.tensor_add(S["C2"][:, :, 2:8], S["C1"][:, :, 2:8], S["C1"][:, :, 0:6]))
                f(v.tensor_copy(S["C3"][:, :, 0:4], S["C2"][:, :, 0:4]))
                f(v.tensor_add(S["C3"][:, :, 4:8], S["C2"][:, :, 4:8], S["C2"][:, :, 0:4]))
                f(v.tensor_single_scalar(S["LE"], S["C3"], 1.0, op=ALU.is_le))
                f(v.tensor_mul(dst, S["GE"], S["LE"]))

            firstmax(S["M1"], S["PR"], 3)
            f(v.scalar_tensor_tensor(S["P2"], S["M1"], NEG_BIG, S["PR"],
                                     op0=ALU.mult, op1=ALU.add))
            firstmax(S["M2"], S["P2"], 4)
            f(v.tensor_sub(rst[:, :, 5:6], rst[:, :, 4:5], rst[:, :, 3:4]))
            t_mid[0] = self.tokd("sDVE")

        @block.scalar
        def _(s):
            self.W(s, t_mid[0])
            self.inc(s.activation(rst[:, :, 6:7], rst[:, :, 5:6], AF.Exp), "sACT")
        t_e2 = self.tokd("sACT")

        @block.vector
        def _(v):
            f = lambda inst: self.vf(v, inst)
            self.W(v, t_e2)
            f(v.tensor_single_scalar(rst[:, :, 7:8], rst[:, :, 6:7], 1.0, op=ALU.add))
            f(v.reciprocal(rst[:, :, 8:9], rst[:, :, 7:8]))
            f(v.tensor_mul(rst[:, :, 9:10], rst[:, :, 6:7], rst[:, :, 8:9]))
            f(v.tensor_add(rst[:, :, 10:11], rst[:, :, 8:9], rst[:, :, 9:10]))
            f(v.tensor_tensor(S["GE"], S["M1"], bc(8), op=ALU.mult))
            f(v.tensor_tensor(S["M2"], S["M2"], bc(9), op=ALU.mult))
            f(v.tensor_add(S["AL"], S["GE"], S["M2"]))
            f(v.tensor_copy(S["EX"][:, :, 0:1], rst[:, :, 10:11]))
        t_alpha = self.tokd("sDVE")

        agev = {}
        agt = [None]
        for tt in range(NT):
            @block.tensor
            def _(t, tt=tt):
                if tt == 0:
                    self.W(t, t_alpha, gcons[1])
                bk = 4 + tt % 2
                if tt < 2:
                    self.W(t, lgev[14 + tt])
                else:
                    self.W(t, agev[tt - 2])
                base = self.RW[:, tt, 10, 0:1]
                ag9 = bass.AP(base.tensor, base.offset, [list(base.ap[0]), [1, 9]])
                agt[0] = self.inc(
                    t.transpose(pb[bk][0:9, 0:128], ag9, self.identf[:, :]), "sPE")

            @block.vector
            def _(v, tt=tt):
                self.W(v, agt[0])
                if tt == 0:
                    self.W(v, self.t_GT)
                agev[tt] = self.inc(
                    v.tensor_copy(self.agT9[0:9, tt * 128:(tt + 1) * 128],
                                  pb[4 + tt % 2][0:9, 0:128]), "sDVE")
        self.t_agT = self.tokd("sDVE")

        @block.gpsimd
        def _(g):
            self.W(g, self.t_agT)
            self.t_agsp = self.inc(g.dma_start(self.scrAG, self.agT9[:, :]), "sAGS", 16)
        self.t_E_pe = self.tokd("sPE")
        self.t_E_gcons = gcons[1]
        self._dbg_mark("E")

    def _phaseF(self):
        block = self.block
        pb = self.pb
        zb_ev = {0: self.t_agT, 1: self.t_agT}
        p_act = {0: self.t_GT, 1: self.t_GT}
        wdve = {0: None, 1: None}
        w1gate = {0: None, 1: None}
        w2free = {0: self.t_E_pe, 1: self.t_E_pe}
        gp = [0]
        w1c = [0]
        w1slot = {0: None, 1: None, 2: None}
        w1of = {}
        zbt = [None]
        wcons = {}
        pend = [None]
        self._shev = {0: None, 1: None}

        t_bc = [None]

        @block.sync
        def _(sy):
            self.W(sy, self.t_g2)
            t_bc[0] = self.inc(sy.dma_start(
                self.B2bc[:, :], self.eb2[0][None, :].to_broadcast([128, L])), "sBC", 16)
        for e in range(1, 8):
            @block.sync
            def _(sy, e=e):
                self.W(sy, t_bc[0], self.tokd("sDVE"))
                if e == 1:
                    self.W(sy, self.t_agsp)
                t_bc[0] = self.inc(sy.dma_start(
                    self.xtraF[:, 0:L], self.eb2[e][None, :].to_broadcast([128, L])),
                    "sBC", 16)

            @block.vector
            def _(v, e=e):
                self.W(v, t_bc[0])
                self.inc(v.tensor_add(self.B2bc[:, :], self.B2bc[:, :],
                                      self.xtraF[:, 0:L]), "sDVE")
        t_b2bc = self.tokd("sDVE")

        gs_ld = {}
        abc_ld = {}
        t_sb2 = [None]
        for tb in range(NTB):
            @block.sync
            def _(sy, tb=tb):
                self.W(sy, self.t_agsp, self.tokd("sDVE"))
                gs_ld[tb] = self.inc(sy.dma_start(
                    self.gsbc[:, :],
                    self.scrAG[8:9, tb * TB:(tb + 1) * TB].to_broadcast([128, TB])),
                    "sABC", 16)

            for e in range(9):
                njc = NJ[e] if e < 8 else 8
                groups = [(g0, min(g0 + JG, njc)) for g0 in range(0, njc, JG)]
                if e < 8:
                    @block.sync
                    def _(sy, tb=tb, e=e):
                        self.W(sy, self.tokd("sDVE"))
                        abc_ld[(tb, e)] = self.inc(sy.dma_start(
                            self.abc[:, :],
                            self.scrAG[e:e + 1, tb * TB:(tb + 1) * TB]
                            .to_broadcast([128, TB])), "sABC", 16)

                    @block.vector
                    def _(v, tb=tb, e=e):
                        self.W(v, abc_ld[(tb, e)], gs_ld[tb])
                        self.inc(v.scalar_tensor_tensor(self.bbc[:, :], self.abc[:, :], -1.0,
                                                        self.gsbc[:, :], op0=ALU.mult,
                                                        op1=ALU.add), "sDVE")
                    t_ab = self.tokd("sDVE")
                else:
                    @block.sync
                    def _(sy, tb=tb):
                        self.W(sy, self.tokd("sDVE"))
                        t_sb2[0] = self.inc(sy.dma_start(
                            self.xtraF[:, 0:L],
                            self.shared_b2[None, :].to_broadcast([128, L])), "sBC", 16)
                    t_ab = None

                for (g0, g1) in groups:
                    nj = g1 - g0
                    assert nj == JG
                    w1s = self.ew1[e] if e < 8 else self.shared_w1
                    w2s = self.ew2[e] if e < 8 else self.shared_w2
                    t_w = {}
                    T_base = self.cnt["sPE"]
                    nb = NTI if pend[0] is not None else 0

                    def pred_pe(j, g0=g0, T_base=T_base, nb=nb):
                        d = j - g0 + 1
                        return ("sPE", T_base + d + (nb if d > 2 else 0))

                    def w1load(j, tb=tb, e=e, g0=g0, w1s=w1s, t_w=t_w):
                        slot = w1c[0] % 3
                        w1c[0] += 1
                        w1of[(tb, e, j)] = slot

                        @block.gpsimd
                        def _(g, j=j, slot=slot):
                            if w1slot[slot] is not None:
                                self.W(g, w1slot[slot])
                            elif slot == 2:
                                self.W(g, self.t_agsp, self.t_E_gcons)
                            if tb == 0 and e == 0 and j == 0:
                                self.W(g, self.t_E_gcons)
                            t_w[j] = self.inc(g.dma_start(
                                self.w1sl[slot][:, :, :], w1s[j]),
                                "sW1%d" % slot, 16)

                    def pmm(j, pred_act, tb=tb, e=e, g0=g0, pred_pe=pred_pe,
                            t_w=t_w):
                        slot = w1of[(tb, e, j)]

                        @block.tensor
                        def _(t, j=j, slot=slot):
                            self.W(t, t_w[j])
                            if j - 2 >= g0:
                                self.W(t, pred_act(j - 2))
                            else:
                                self.W(t, p_act[j % 2])
                            bk = (j % 2) * 2
                            last = None
                            for h in range(2):
                                for Lc in range(NL):
                                    last = t.matmul(
                                        pb[bk + h][:, :], self.w1sl[slot][:, Lc, :],
                                        self.GT[:, Lc,
                                                tb * TB + h * 512:tb * TB + (h + 1) * 512],
                                        start=(Lc == 0), stop=(Lc == NL - 1))
                            wcons[(tb, e, j)] = self.inc(last, "sPE")
                            assert wcons[(tb, e, j)] == pred_pe(j), \
                                (wcons[(tb, e, j)], pred_pe(j))
                            w1gate[j % 2] = wcons[(tb, e, j)]
                        w1slot[slot] = wcons[(tb, e, j)]

                    w1load(g0)
                    w1load(g0 + 1)
                    pmm(g0, None)
                    pmm(g0 + 1, None)
                    if pend[0] is not None:
                        pend[0]()
                        pend[0] = None

                    half = gp[0] % 2
                    woff = half * JG
                    A_base = self.cnt["sACT"]
                    V_base = self.cnt["sDVE"]

                    def pred_act(j, g0=g0, A_base=A_base):
                        return ("sACT", A_base + (j - g0 + 1))

                    def pred_dve(j, g0=g0, V_base=V_base):
                        return ("sDVE", V_base + 2 * (j - g0 + 1))

                    def w2load(j, jj, tb=tb, e=e, half=half, woff=woff, w2s=w2s):
                        @block.gpsimd
                        def _(g, j=j, jj=jj):
                            if jj == 0:
                                self.W(g, w2free[half])
                            if tb == 0 and e == 0 and j == 0:
                                self.W(g, self.t_g2)
                            self.inc(g.dma_start(
                                self.w2grp[:, woff + jj, :],
                                w2s[j * 128:(j + 1) * 128, :]),
                                "sW2%d" % half, 16)

                    for j in range(g0 + 2, g1):
                        w1load(j)
                        w2load(j - 2, j - 2 - g0)
                        pmm(j, pred_act)
                    w2load(g1 - 2, g1 - 2 - g0)
                    w2load(g1 - 1, g1 - 1 - g0)

                    @block.scalar
                    def _(s, tb=tb, e=e, g0=g0, g1=g1, woff=woff,
                          pred_act=pred_act, pred_dve=pred_dve):
                        for j in range(g0, g1):
                            jj = j - g0
                            self.W(s, wcons[(tb, e, j)])
                            if e < 8:
                                if j - 2 >= g0:
                                    self.W(s, pred_dve(j - 2))
                                elif wdve[j % 2]:
                                    self.W(s, wdve[j % 2])
                            if tb == 0 and e == 0 and j == 0:
                                self.W(s, self.t_rT, self.t_agsp)
                            bk = (j % 2) * 2
                            bias = (self.b1T[:, self.b1off[e] + j:self.b1off[e] + j + 1]
                                    if e < 8 else self.sb1T[:, j:j + 1])
                            dst = (self.stl[j % 2][:, :] if e < 8
                                   else self.wtil[:, woff + jj, :])
                            tk = self.inc(s.activation(dst, self.pq[bk // 2][:, :],
                                                       AF.Gelu, bias=bias), "sACT")
                            assert tk == pred_act(j), (tk, pred_act(j))
                            p_act[j % 2] = tk

                    if e < 8:
                        for j in range(g0, g1):
                            jj = j - g0

                            @block.vector
                            def _(v, tb=tb, e=e, j=j, jj=jj, g0=g0, t_ab=t_ab,
                                  woff=woff, pred_act=pred_act, pred_dve=pred_dve):
                                self.W(v, pred_act(j))
                                if j == g0:
                                    self.W(v, t_ab, wdve[0], wdve[1])
                                else:
                                    self.W(v, pred_dve(j - 1))
                                if tb == 0 and e == 0 and j == 0:
                                    self.W(v, self.t_rT)
                                self.vf(v, v.tensor_mul(self.xtraB[:, :], self.abc[:, :],
                                                        self.stl[j % 2][:, :]))
                                tk = self.inc(
                                    v.scalar_tensor_tensor(
                                        self.wtil[:, woff + jj, :], self.bbc[:, :],
                                        self.rT[:, self.b1off[e] + j:self.b1off[e] + j + 1],
                                        self.xtraB[:, :], op0=ALU.mult, op1=ALU.add),
                                    "sDVE")
                                assert tk == pred_dve(j), (tk, pred_dve(j))
                                wdve[j % 2] = tk
                    t_wgrp = self.tokd("sDVE") if e < 8 else self.tokd("sACT")
                    t_w2grp = self.tokd("sW2%d" % half)

                    def emit_pass2(tb=tb, e=e, g0=g0, nj=nj, half=half, woff=woff,
                                   t_wgrp=t_wgrp, t_w2grp=t_w2grp):
                        for ti in range(NTI):
                            tt = tb * NTI + ti

                            @block.tensor
                            def _(t, ti=ti):
                                if ti == 0:
                                    self.W(t, t_wgrp, t_w2grp)
                                bk = 4 + (ti % 2) * 2
                                self.W(t, zb_ev[ti % 2])
                                last = None
                                for h in range(2):
                                    for jj in range(nj):
                                        last = t.matmul(
                                            pb[bk + h][:, :],
                                            self.wtil[:, woff + jj,
                                                      ti * 128:(ti + 1) * 128],
                                            self.w2grp[:, woff + jj,
                                                       h * 512:(h + 1) * 512],
                                            start=(jj == 0), stop=(jj == nj - 1))
                                zbt[0] = self.inc(last, "sPE")

                            if e < 8:
                                @block.vector
                                def _(v, ti=ti, tt=tt):
                                    self.W(v, zbt[0])
                                    if tb == 0 and e == 0 and g0 == 0 and ti == 0:
                                        self.W(v, t_b2bc)
                                    bk = 4 + (ti % 2) * 2
                                    z2 = self.z[:, tt, :].rearrange("p (a c) -> p a c",
                                                                    a=2)
                                    last = None
                                    for h in range(2):
                                        if e == 0 and g0 == 0:
                                            last = v.scalar_tensor_tensor(
                                                z2[:, h, :],
                                                self.B2bc[:, h * 512:(h + 1) * 512],
                                                self.rst[:, tt, 10:11], pb[bk + h][:, :],
                                                op0=ALU.mult, op1=ALU.add)
                                        else:
                                            last = v.tensor_add(z2[:, h, :], z2[:, h, :],
                                                                pb[bk + h][:, :])
                                    zb_ev[ti % 2] = self.inc(last, "sDVE")
                            else:
                                @block.scalar
                                def _(s, ti=ti):
                                    self.W(s, zbt[0], self._shev[ti % 2])
                                    bk = 4 + (ti % 2) * 2
                                    last = None
                                    for h in range(2):
                                        last = s.activation(
                                            self.stl[ti % 2][:, h * 512:(h + 1) * 512],
                                            pb[bk + h][:, :], AF.Copy, scale=0.1)
                                    self.inc(last, "sACT")
                                t_sh = self.tokd("sACT")

                                @block.vector
                                def _(v, ti=ti, tt=tt, t_sh=t_sh):
                                    self.W(v, t_sh, t_sb2[0])
                                    z2 = self.z[:, tt, :].rearrange("p (a c) -> p a c",
                                                                    a=2)
                                    x2 = self.xtraF[:, 0:L].rearrange("p (a c) -> p a c",
                                                                      a=2)
                                    last = None
                                    for h in range(2):
                                        last = v.tensor_add(
                                            z2[:, h, :], z2[:, h, :],
                                            self.stl[ti % 2][:, h * 512:(h + 1) * 512])
                                    self.vf(v, last)
                                    last = None
                                    for h in range(2):
                                        last = v.scalar_tensor_tensor(
                                            z2[:, h, :], x2[:, h, :], 0.1, z2[:, h, :],
                                            op0=ALU.mult, op1=ALU.add)
                                    zb_ev[ti % 2] = self.inc(last, "sDVE")
                                    self._shev[ti % 2] = zb_ev[ti % 2]
                        w2free[half] = zbt[0]
                        if e == 8:
                            wdve[0] = wdve[1] = self.tokd("sDVE")
                        gp[0] += 1
                    pend[0] = emit_pass2
        if pend[0] is not None:
            pend[0]()
            pend[0] = None
        self.t_F_z = self.tokd("sDVE")
        self.t_F_pe = self.tokd("sPE")
        self._dbg_mark("F")

    def _phaseG(self):
        block = self.block
        pb, pbbf = self.pb, self.pbbf
        st = self.st
        trev = {0: self.t_F_z, 1: self.t_F_z}
        zc_tok = {}
        ztt = {}
        P0 = self.cnt["sPE"]
        A0 = self.cnt["sACT"]

        @block.scalar
        def _(s, P0=P0):
            self.W(s, self.t_F_z)
            for tt in range(NT):
                if tt >= 1:
                    s.wait_ge(self.sem["sPE"], P0 + tt)
                self.inc(s.activation(self.zst[:, :], self.z[:, tt, :], AF.Copy), "sACT")
        for tt in range(NT):
            zc_tok[tt] = ("sACT", A0 + tt + 1)
        assert self.cnt["sACT"] == A0 + NT

        for tt in range(NT):
            @block.tensor
            def _(t, tt=tt):
                self.W(t, zc_tok[tt], trev[tt % 2])
                bank = pbbf[6 + tt % 2]
                last = None
                for Lc in range(NL):
                    last = t.transpose(bank[:, Lc * 128:(Lc + 1) * 128],
                                       self.zst[:, Lc * 128:(Lc + 1) * 128], self.identb[:, :])
                ztt[tt] = self.inc(last, "sPE")
                assert ztt[tt][1] == P0 + tt + 1

            @block.vector
            def _(v, tt=tt):
                self.W(v, ztt[tt])
                bank = pbbf[6 + tt % 2]
                trev[tt % 2] = self.inc(
                    v.tensor_copy(self.z16T[:, :, tt * 128:(tt + 1) * 128],
                                  bank[:, 0:NL * 128].rearrange("p (c n) -> p c n", c=NL)),
                    "sDVE")
        t_z16 = self.tokd("sDVE")
        t_zcast = zc_tok[NT - 1]

        t_gw = [None]

        @block.gpsimd
        def _(g):
            self.W(g, self.t_F_pe)
            for fo in range(4):
                for hh in range(2):
                    self.inc(g.dma_start(
                        self.corew[:, hh * 8:(hh + 1) * 8, fo * 512:(fo + 1) * 512],
                        self.core_w[:, fo * 512:(fo + 1) * 512]
                        .rearrange("(c p) n -> p c n", p=128)[:, hh * 8:(hh + 1) * 8, :]),
                        "sGW", 16)
            self.W(g, zc_tok[NL - 1])
            for hh in range(2):
                t_gw[0] = self.inc(g.dma_start(
                    self.upw[:, hh * 4:(hh + 1) * 4, :],
                    self.up_w.rearrange("(c p) n -> p c n", p=128)[:, hh * 4:(hh + 1) * 4, :]),
                    "sGW", 16)

        t_budld = [None]

        @block.sync
        def _(sy):
            self.W(sy, self.t_F_z, t_zcast)
            self.inc(sy.dma_start(self.B_ud[:, :],
                                  self.up_b[None, :].to_broadcast([128, D])), "sBUD", 16)
            t_budld[0] = self.inc(sy.dma_start(
                self.y2t[:, :], self.core_b[None, :].to_broadcast([128, D])), "sBUD", 16)

        @block.vector
        def _(v):
            self.W(v, t_budld[0])
            self.inc(v.tensor_add(self.B_ud[:, :], self.B_ud[:, :], self.y2t[:, :]), "sDVE")
        t_bud = self.tokd("sDVE")

        g2ld, x2ld, out_tok, out_st = {}, {}, {}, {}
        gy = {}
        y_ev = None
        y_evs = {}
        V1 = self.cnt["sDVE"]
        A1 = self.cnt["sACT"]

        @block.scalar
        def _(s, V1=V1):
            for tt in range(NT):
                s.wait_ge(self.sem["sDVE"], V1 + 9 * tt + 1)
                self.inc(s.activation(self.sqbG[:, :], self.y2t[:, :], AF.Square,
                                      accum_out=st[:, 1:2]), "sACT")
                s.wait_ge(self.sem["sDVE"], V1 + 9 * tt + 7)
                self._ln_rstd(s)
        assert self.cnt["sACT"] == A1 + 3 * NT

        for tt in range(NT):
            @block.sync
            def _(sy, tt=tt):
                if tt == 0:
                    self.W(sy, self.t_g2, self.t_F_z, t_zcast)
                if tt >= 2:
                    self.W(sy, gy[tt - 2])
                g2ld[tt] = self.inc(sy.dma_start(
                    self.g2sl[tt % 2][:, :, :], self.scrG2[tt]),
                    "sGL%d" % (tt % 2), 16)
                if tt < 2:
                    x2ld[tt] = self.inc(sy.dma_start(
                        self.x2tG[tt % 2][:, :], self.scrX2[tt]),
                        "sXG%d" % (tt % 2), 16)

            if tt >= 2:
                @block.gpsimd
                def _(g, tt=tt):
                    self.W(g, gy[tt - 2], out_st[tt - 2])
                    x2ld[tt] = self.inc(g.dma_start(
                        self.x2tG[tt % 2][:, :], self.scrX2[tt]),
                        "sXB%d" % (tt % 2), 16)

            @block.tensor
            def _(t, tt=tt):
                bq = (tt % 2) * 4
                self.W(t, y_evs.get(tt - 2))
                if tt <= 1:
                    self.W(t, t_gw[0], t_z16, t_zcast)
                for db in range(4):
                    for Lc in range(NL):
                        t.matmul(pb[bq + db][:, :],
                                 self.z16T[:, Lc, tt * 128:(tt + 1) * 128],
                                 self.upw[:, Lc, db * 512:(db + 1) * 512],
                                 start=(Lc == 0), stop=False)
                self.W(t, g2ld[tt])
                last = None
                for db in range(4):
                    for fc in range(ND):
                        last = t.matmul(pb[bq + db][:, :],
                                        self.g2sl[tt % 2][:, fc, :],
                                        self.corew[:, fc, db * 512:(db + 1) * 512],
                                        start=False, stop=(fc == ND - 1))
                gy[tt] = self.inc(last, "sPE")

            @block.vector
            def _(v, tt=tt):
                bq = (tt % 2) * 4
                self.W(v, gy[tt])
                if tt == 0:
                    self.W(v, t_bud)
                last = None
                for db in range(4):
                    last = v.scalar_tensor_tensor(
                        self.y2t[:, db * 512:(db + 1) * 512], pb[bq + db][:, :], 1.0,
                        self.B_ud[:, db * 512:(db + 1) * 512],
                        op0=ALU.mult, op1=ALU.add, accum_out=st[:, 8 + db:9 + db])
                self.vf(v, last)
            y_ev = self.tokd("sDVE")
            y_evs[tt] = y_ev
            assert y_ev[1] == V1 + 9 * tt + 1
            t_sq = ("sACT", A1 + 3 * tt + 1)

            @block.vector
            def _(v, tt=tt, t_sq=t_sq):
                st_ = st
                self.vf(v, v.tensor_add(st_[:, 12:13], st_[:, 8:9], st_[:, 9:10]))
                self.vf(v, v.tensor_add(st_[:, 13:14], st_[:, 10:11], st_[:, 11:12]))
                self.vf(v, v.tensor_add(st_[:, 0:1], st_[:, 12:13], st_[:, 13:14]))
                self.vf(v, v.tensor_scalar(st_[:, 2:3], st_[:, 0:1], -SQD, None, op0=ALU.mult))
                self.vf(v, v.tensor_mul(st_[:, 3:4], st_[:, 2:3], st_[:, 2:3]))
                self.W(v, t_sq)
                self.vf(v, v.scalar_tensor_tensor(st_[:, 4:5], st_[:, 1:2], SQD, st_[:, 3:4],
                                                  op0=ALU.mult, op1=ALU.subtract))
            tv = self.tokd("sDVE")
            assert tv[1] == V1 + 9 * tt + 7
            t_r = ("sACT", A1 + 3 * tt + 3)

            @block.vector
            def _(v, tt=tt, t_r=t_r):
                self.W(v, t_r, x2ld[tt])
                self.vf(v, v.tensor_scalar(self.outt[:, :], self.y2t[:, :],
                                           st[:, 2:3], st[:, 6:7],
                                           op0=ALU.add, op1=ALU.mult))
                out_tok[tt] = self.inc(
                    v.tensor_add(self.x2tG[tt % 2][:, :], self.x2tG[tt % 2][:, :],
                                 self.outt[:, :]), "sDVE")

            @block.sync
            def _(sy, tt=tt):
                self.W(sy, out_tok[tt])
                out_st[tt] = self.inc(
                    sy.dma_start(self.out[tt * 128:(tt + 1) * 128, :],
                                 self.x2tG[tt % 2][:, :]), "sOU%d" % (tt % 2), 16)
        self._dbg_mark("G")

    def _dbg_mark(self, phase):
        for (name, ph, view_fn) in self.dbg:
            if ph == phase:
                self.dbg_done.append((name, view_fn, dict(self.cnt)))

    def _dbg_flush(self):
        block, nc = self.block, self.nc

        @block.sync
        def _(sy):
            for (name, view_fn, waits) in self.dbg_done:
                for snm, vv in waits.items():
                    if vv > 0 and snm != "sOUT":
                        sy.wait_ge(self.sem[snm], vv)
                ap = view_fn(self)
                dst = nc.dram_tensor("dbg_" + name, list(ap.shape), ap.dtype,
                                     kind="ExternalOutput").ap()
                self.inc(sy.dma_start(dst, ap), "sOUT", 16)

    def _final_wait(self):
        block = self.block

        @block.sync
        def _(sy):
            for nm in ["sOUT", "sOU0", "sOU1"]:
                if self.cnt[nm] > 0:
                    sy.wait_ge(self.sem[nm], self.cnt[nm])


def _digest_w1(w):
    Lr, H = w.shape
    nj = H // 128
    return np.ascontiguousarray(
        np.asarray(w, np.float32).reshape(Lr // 128, 128, nj, 128)
        .transpose(2, 1, 0, 3))


def _core_inputs(inputs):
    m = {
        "Wq": inputs["L0_attn_Wq"], "Wk": inputs["L0_attn_Wk"],
        "Wv": inputs["L0_attn_Wv"], "Wo": inputs["L0_attn_Wo"],
        "down_w": inputs["L0_ffn_down_w"], "down_b": inputs["L0_ffn_down_b"],
        "up_w": inputs["L0_ffn_up_w"], "up_b": inputs["L0_ffn_up_b"],
        "router_w": inputs["L0_ffn_router_w"], "router_b": inputs["L0_ffn_router_b"],
        "shared_w1": _digest_w1(inputs["L0_ffn_shared_w1"]),
        "shared_b1": inputs["L0_ffn_shared_b1"],
        "shared_w2": inputs["L0_ffn_shared_w2"], "shared_b2": inputs["L0_ffn_shared_b2"],
        "core_w": inputs["L0_ffn_core_w"], "core_b": inputs["L0_ffn_core_b"],
    }
    for e in range(8):
        m[f"e{e}_w1"] = _digest_w1(inputs[f"L0_ffn_e{e}_w1"])
        m[f"e{e}_b1"] = inputs[f"L0_ffn_e{e}_b1"]
        m[f"e{e}_w2"] = inputs[f"L0_ffn_e{e}_w2"]
        m[f"e{e}_b2"] = inputs[f"L0_ffn_e{e}_b2"]
    m["ident"] = np.eye(128, dtype=np.float32)
    cost = np.array([1024 * h + h * 1024 for h in HID], np.float32)
    m["negcost"] = (-COST_LAMBDA * cost)[None, :]
    return {k: np.ascontiguousarray(np.asarray(v, dtype=np.float32)) for k, v in m.items()}


def kernel(**inputs):
    x = np.ascontiguousarray(np.asarray(inputs["L0_attn_x"], dtype=np.float32))
    mk = MK()
    base = _core_inputs(inputs)
    in_maps = []
    for c in range(8):
        mm = dict(base)
        mm["x"] = np.ascontiguousarray(x[c])
        in_maps.append(mm)
    res = run_bass_kernel_spmd(mk.nc, in_maps, core_ids=list(range(8)))
    return np.stack([r["out"] for r in res.results], axis=0)
